# version 12
# baseline (speedup 1.0000x reference)
import math
from contextlib import ExitStack

import numpy as np
import ml_dtypes

import concourse.bass as bass
import concourse.mybir as mybir
from concourse.bass_utils import run_bass_kernel_spmd

F32 = mybir.dt.float32
BF16 = mybir.dt.bfloat16
FP8 = mybir.dt.float8e4
AF = mybir.ActivationFunctionType
ALU = mybir.AluOpType

NPBF16 = ml_dtypes.bfloat16
NPFP8 = ml_dtypes.float8_e4m3

B = 4
S = 4096
D = 2048
H = 16
HD = 128
T = 2048
CC = 16
NSB = 4
SBW = 512
KCW = 128
NKC = 16
DFF = 8192
HC = 64

ATT_SCALE = 1.0 / math.sqrt(HD)
SHIFT = 18.0
LN_EPS = 1e-5

_PHASES = 4
_SUB = None



def _chunkT(w):
    C, M = w.shape
    return np.ascontiguousarray(
        np.asarray(w).reshape(C // 128, 128, M).transpose(1, 0, 2).astype(np.float32))


def _mask_tiles(r):
    p = np.arange(128)[:, None]
    f = np.arange(512)[None, :]
    tiles = np.zeros((16, 128, 512), dtype=NPFP8)
    for pi in (0, 1):
        e = -1 if (r == 0 and pi == 1) else 0
        ep = 0 if (r == 0 and pi == 1) else 1
        for d in range(4):
            tiles[pi * 4 + d] = (p - f + 128 * d <= e).astype(NPFP8)
            tiles[8 + pi * 4 + d] = (p - f + 128 * d >= ep).astype(NPFP8)
    return tiles


def _stage_inputs(inputs):
    g = {}
    g['l0_qwT'] = _chunkT(np.asarray(inputs['L0_attn_q_w']).T)
    g['l0_kwT'] = _chunkT(np.asarray(inputs['L0_attn_k_w']).T)
    g['l0_vwT'] = _chunkT(np.asarray(inputs['L0_attn_v_w']).T)
    g['l0_owT'] = _chunkT(np.asarray(inputs['L0_attn_o_w']).T)
    g['l0_wgT'] = _chunkT(np.asarray(inputs['L0_ffn_W_gate']).T)
    wqkv = np.asarray(inputs['L0_ffn_W_qkv'])
    g['l0_qT'] = _chunkT(wqkv[0:D].T)
    g['l0_kvT'] = _chunkT(wqkv[D:3 * D].T)
    g['l1_wq'] = _chunkT(np.asarray(inputs['L1_attn_Wq']))
    g['l1_wk'] = _chunkT(np.asarray(inputs['L1_attn_Wk']))
    g['l1_wv'] = _chunkT(np.asarray(inputs['L1_attn_Wv']))
    g['l1_wo'] = _chunkT(np.asarray(inputs['L1_attn_Wo']))
    g['l1_wgT'] = _chunkT(np.asarray(inputs['L1_ffn_w_gate']).T)
    g['l1_wuT'] = _chunkT(np.asarray(inputs['L1_ffn_w_up']).T)
    g['l1_wdT'] = _chunkT(np.asarray(inputs['L1_ffn_w_down']).T)

    x0 = np.asarray(inputs['L0_attn_x'], dtype=np.float32)
    ident = np.eye(128, dtype=NPBF16)
    ones_col = np.ones((128, 1), dtype=np.float32)
    ones_sq = np.ones((128, 128), dtype=np.float32)
    ones_row = np.ones((1, 128), dtype=np.float32)
    ones_colb = np.ones((128, 1), dtype=NPBF16)
    ones_sqb = np.ones((128, 128), dtype=NPBF16)

    in_maps = []
    for c in range(8):
        b, r = c // 2, c % 2
        m = dict(g)
        xown = np.ascontiguousarray(x0[b, r::2].T)
        m['x0own'] = np.ascontiguousarray(
            xown.reshape(CC, 128, T).transpose(1, 0, 2))
        m['masks'] = np.ascontiguousarray(_mask_tiles(r).transpose(1, 0, 2))
        edge = np.zeros((128, 4), dtype=np.float32)
        edge[:, 0] = 1.0 if r == 0 else 0.0
        edge[:, 1] = 0.0 if r == 0 else 1.0 / S
        m['edge'] = edge
        patch = np.zeros((128, 1), dtype=np.float32)
        if r == 1:
            patch[:, 0] = 1.0
        m['patch'] = patch
        m['ident'] = ident
        m['ones_col'] = ones_col
        m['ones_sq'] = ones_sq
        m['ones_row'] = ones_row
        m['ones_colb'] = ones_colb
        m['ones_sqb'] = ones_sqb
        in_maps.append(m)
    return in_maps


def _assemble(results):
    out = np.empty((B, S, D), dtype=np.float32)
    for c in range(8):
        b, r = c // 2, c % 2
        o = np.asarray(results[c]['out_fm'])
        out[b, r::2, :] = o.transpose(1, 0, 2).reshape(D, T).T
    return out



_UN = [0]


def _u(name):
    _UN[0] += 1
    return "%s_%d" % (name, _UN[0])


class P:
    pass


def _diag_d(sb, j):
    d = j - 4 * sb
    return d if 0 <= d <= 3 else None


def _causal_chunks(sb):
    return [(pi, j) for pi in (0, 1) for j in range(4 * sb + 4)]


def _anti_chunks(sb):
    return [(pi, j) for pi in (0, 1) for j in range(4 * sb, NKC)]


def _full_chunks(sb):
    return [(pi, j) for pi in (0, 1) for j in range(NKC)]


def _bank(p, i, width=SBW):
    return p.ps[:, i * 512:i * 512 + width]


def build_program(phases=4):
    nc = bass.Bass('TRN2', target_bir_lowering=False)
    p = P()
    p.nc = nc

    dI = lambda name, shape, dt=F32: nc.dram_tensor(name, shape, dt, kind='ExternalInput').ap()
    dX = lambda name, shape, dt=BF16: nc.dram_tensor(name, shape, dt).ap()

    p.x0own = dI('x0own', [128, CC, T])
    p.w = {}
    for k, shape in {
        'l0_qwT': (128, CC, D), 'l0_kwT': (128, CC, HD), 'l0_vwT': (128, CC, HD),
        'l0_owT': (128, CC, D), 'l0_wgT': (128, CC, H), 'l0_qT': (128, CC, D),
        'l0_kvT': (128, CC, 2 * D), 'l1_wq': (128, CC, D), 'l1_wk': (128, CC, D),
        'l1_wv': (128, CC, D), 'l1_wo': (128, CC, D), 'l1_wgT': (128, CC, DFF),
        'l1_wuT': (128, CC, DFF), 'l1_wdT': (128, HC, D),
    }.items():
        p.w[k] = dI(k, list(shape))
    p.masks_in = dI('masks', [128, 16, SBW], FP8)
    p.edge_in = dI('edge', [128, 4])
    p.patch_in = dI('patch', [128, 1])
    p.ident_in = dI('ident', [128, 128], BF16)
    p.onesc_in = dI('ones_col', [128, 1])
    p.onessq_in = dI('ones_sq', [128, 128])
    p.onesr_in = dI('ones_row', [1, 128])
    p.onescb_in = dI('ones_colb', [128, 1], BF16)
    p.onesqb_in = dI('ones_sqb', [128, 128], BF16)
    p.out_fm = nc.dram_tensor('out_fm', [128, CC, T], F32, kind='ExternalOutput').ap()

    p.q_scr = dX('q_scr', [H, 128, T])
    p.k_loc = dX('k_loc', [2, H, 128, T])
    p.v_loc = dX('v_loc', [2, 128, NKC, D])
    p.gated_scr = dX('gated_scr', [128, HC, T])
    p.agki = dX('agki', [H, 128, T])
    p.agko = dX('agko', [2, H, 128, T])
    p.agvi = dX('agvi', [128, NKC, D])
    p.agvo = dX('agvo', [2, 128, NKC, D])
    p.ag_kv_in = dX('ag_kv_in', [2, 128, NKC, HD])
    p.ag_kv_out = dX('ag_kv_out', [2, 2, 128, NKC, HD])
    p.ag_x_in = dX('ag_x_in', [128, CC, T])
    p.ag_x_out = dX('ag_x_out', [2, 128, CC, T])
    p.RG = [[0, 1], [2, 3], [4, 5], [6, 7]]

    with ExitStack() as es:
        p.es = es
        sb_t = lambda name, shape, dt: es.enter_context(nc.sbuf_tensor(name, shape, dt))
        p.xsb = sb_t('xsb', [128, CC, T], F32)
        p.consts = sb_t('consts', [128, 8], F32)
        p.onesc = sb_t('onesc', [128, 1], F32)
        p.onessq = sb_t('onessq', [128, 128], F32)
        p.onesr = sb_t('onesr', [1, 128], F32)
        p.onescb = sb_t('onescb', [128, 1], BF16)
        p.onesqb = sb_t('onesqb', [128, 128], BF16)
        p.ident = sb_t('ident_sb', [128, 128], BF16)
        p.edge = sb_t('edge_sb', [128, 4], F32)
        p.patch = sb_t('patch_sb', [128, 1], F32)
        p.masks = sb_t('masks_sb', [128, 8, SBW], FP8)
        p.lnm = sb_t('lnm', [1, SBW], F32)
        p.lnt1 = sb_t('lnt1', [1, SBW], F32)
        p.lnt2 = sb_t('lnt2', [1, SBW], F32)
        p.lns = sb_t('lns', [1, SBW], F32)
        p.sbst = sb_t('sbst', [128, 2, SBW], F32)
        p.sqbuf = sb_t('sqbuf', [128, 2, SBW], F32)
        p.evac = sb_t('evac', [128, 3, SBW], BF16)
        p.sumv = sb_t('sumv', [128, H], F32)
        p.sumvb = sb_t('sumvb', [128, H], F32)

        class Ctr:
            def __init__(self, name):
                self.h = es.enter_context(nc.semaphore(name))
                self.n = 0
        p.cS = Ctr('cS')
        p.cG = Ctr('cG')
        p.cM = Ctr('cM')
        p.cW0 = Ctr('cW0')
        p.cW1 = Ctr('cW1')
        p.cE0 = Ctr('cE0')
        p.cE1 = Ctr('cE1')
        p.cE2 = Ctr('cE2')
        p.cKV = Ctr('cKV')
        p.cKV1 = Ctr('cKV1')
        p.cQ0 = Ctr('cQ0')
        p.cQ1 = Ctr('cQ1')
        p.cX = Ctr('cX')
        p.cD = Ctr('cD')
        p.cD1 = Ctr('cD1')
        p.cA = Ctr('cA')
        p.dma_ctrs = [p.cS, p.cG, p.cM, p.cW0, p.cW1, p.cE0, p.cE1, p.cE2,
                      p.cKV, p.cKV1, p.cQ0, p.cQ1, p.cX, p.cD, p.cD1, p.cA]
        p.sT = Ctr('sT')
        p.sS = Ctr('sS')
        p.sV = Ctr('sV')
        p.sV2 = Ctr('sV2')
        p.sS2 = Ctr('sS2')
        p.sCC = Ctr('sCC')

        p.blk = es.enter_context(nc.Block())
        p.ps = es.enter_context(nc.psum_tensor('arena', [128, 4096], F32))
        p.x_tok = 0
        p.po_free = 0
        p.bankS = {}
        p.evS = {}

        _prologue(p)
        if phases >= 1:
            _l0_attn(p)
        if phases >= 2:
            _l0_ffn(p)
        if phases >= 3:
            _l1_attn(p)
        if phases >= 4:
            _l1_ffn(p)
        _epilogue(p)
    return nc


def _bank_bf(p, i, cols=128):
    return p.ps[:, i * 512:i * 512 + cols // 2].bitcast(BF16)


def _prologue(p):
    nc, blk = p.nc, p.blk

    @blk.sync
    def _(s):
        for dst, src in ((p.edge, p.edge_in), (p.patch, p.patch_in),
                         (p.ident, p.ident_in), (p.onesc, p.onesc_in),
                         (p.onessq, p.onessq_in), (p.onesr, p.onesr_in),
                         (p.onescb, p.onescb_in), (p.onesqb, p.onesqb_in)):
            s.dma_start(out=dst[:, :], in_=src[:, :]).then_inc(p.cS.h, 16)
            p.cS.n += 16
        s.dma_start(out=p.xsb[:, :, :], in_=p.x0own[:, :, :]).then_inc(p.cS.h, 16)
        p.cS.n += 16

    vals = [-SHIFT, ATT_SCALE, LN_EPS, -0.5, 2.0]

    @blk.vector
    def _(v):
        for i, val in enumerate(vals):
            _li1 = v.memset(p.consts[:, i:i + 1], float(val))
        _li1.then_inc(p.sV.h, 1)
    p.sV.n += 1
    for i, val in enumerate(vals):
        nc.const_aps.register(F32, float(val), p.consts[:, i:i + 1])

    tok = p.cS.n
    tokv = p.sV.n
    for eng in ('tensor', 'scalar', 'vector', 'gpsimd', 'sync'):
        def _w(e, tok=tok, tokv=tokv):
            e.wait_ge(p.cS.h, tok)
            e.wait_ge(p.sV.h, tokv)
        getattr(blk, eng)(_w)


def _epilogue(p):
    blk = p.blk
    tokV, tokT, tokS = p.sV.n, p.sT.n, p.sS.n

    @blk.sync
    def _(s):
        s.wait_ge(p.sV.h, tokV)
        s.wait_ge(p.sT.h, tokT)
        s.wait_ge(p.sS.h, tokS)
        s.dma_start(out=p.out_fm[:, :, :], in_=p.xsb[:, :, :]).then_inc(p.cS.h, 16)
        p.cS.n += 16
        s.wait_ge(p.cS.h, p.cS.n)


def _load_masks(p, base, wait_v):
    @p.blk.sync
    def _(s):
        s.wait_ge(p.sV.h, wait_v)
        s.dma_start(out=p.masks[:, :, :], in_=p.masks_in[:, base:base + 8, :]).then_inc(p.cM.h, 16)
    p.cM.n += 16
    return p.cM.n



def _phase_barrier(p):
    blk = p.blk
    toks = [(c.h, c.n) for c in p.dma_ctrs if c.n]
    for eng in ('tensor', 'scalar', 'vector', 'gpsimd', 'sync'):
        def _w(e, toks=toks):
            for h, n in toks:
                e.wait_ge(h, n)
        getattr(blk, eng)(_w)



def _proj_to_scratch(p, wkey, dst, nheads, col0=0):
    nc, blk = p.nc, p.blk
    _phase_barrier(p)
    with nc.sbuf_tensor(_u('pw2'), [128, 2, CC, 256], BF16) as pw2, \
         nc.sbuf_tensor(_u('pxt'), [128, 2, CC, SBW], BF16) as pxt:
        win = p.w[wkey]
        wuse = {}
        xuse = {}
        ngr = (nheads + 1) // 2
        for hg in range(ngr):
            nh = min(2, nheads - hg * 2)
            ws = hg % 2
            cw = p.cW0 if ws == 0 else p.cW1
            prev = wuse.get(hg - 2, 0)

            @blk.gpsimd
            def _(g, hg=hg, nh=nh, ws=ws, prev=prev, cw=cw):
                g.wait_ge(p.sT.h, prev)
                g.dma_start(out=pw2[:, ws, :, 0:nh * 128],
                            in_=win[:, :, col0 + hg * 256:col0 + hg * 256 + nh * 128]).then_inc(cw.h, 16)
            cw.n += 16
            tokW = cw.n

            for tsb in range(NSB):
                it = hg * NSB + tsb
                xs = it % 2
                xprev = xuse.get(it - 2, 0)
                xt = p.x_tok

                @blk.vector
                def _(v, tsb=tsb, xs=xs, xprev=xprev, xt=xt):
                    v.wait_ge(p.sV.h, xt)
                    v.wait_ge(p.sT.h, xprev)
                    v.tensor_copy(pxt[:, xs, :, :],
                                  p.xsb[:, :, tsb * SBW:(tsb + 1) * SBW]).then_inc(p.sV2.h, 1)
                p.sV2.n += 1
                tokX = p.sV2.n

                for hh in range(nh):
                    h = hg * 2 + hh
                    bk = 4 + (hh % 2)
                    ev = hh % 2
                    bs = p.bankS.get(bk, 0)

                    @blk.tensor
                    def _(t, hh=hh, ws=ws, xs=xs, bk=bk, tokW=tokW, tokX=tokX,
                          bs=bs, cw=cw):
                        t.wait_ge(cw.h, tokW)
                        t.wait_ge(p.sV2.h, tokX)
                        t.wait_ge(p.sS.h, bs)
                        for cc in range(CC):
                            _lip = t.matmul(_bank(p, bk), pw2[:, ws, cc, hh * 128:(hh + 1) * 128],
                                     pxt[:, xs, cc, :], start=(cc == 0), stop=(cc == CC - 1))
                        _lip.then_inc(p.sT.h, 1)
                    p.sT.n += 1
                    tokMM = p.sT.n
                    evfree = p.evS.get(ev, 0)
                    ce = p.cE0 if ev == 0 else p.cE1

                    @blk.scalar
                    def _(s, ev=ev, bk=bk, tokMM=tokMM, evfree=evfree, ce=ce):
                        s.wait_ge(p.sT.h, tokMM)
                        s.wait_ge(ce.h, evfree)
                        s.copy(out=p.evac[:, ev, :], in_=_bank(p, bk)).then_inc(p.sS.h, 1)
                    p.sS.n += 1
                    p.bankS[bk] = p.sS.n
                    tokEv = p.sS.n

                    @blk.sync
                    def _(s, h=h, tsb=tsb, ev=ev, tokEv=tokEv, ce=ce):
                        s.wait_ge(p.sS.h, tokEv)
                        s.dma_start(out=dst[h, :, tsb * SBW:(tsb + 1) * SBW],
                                    in_=p.evac[:, ev, :]).then_inc(ce.h, 16)
                    ce.n += 16
                    p.evS[ev] = ce.n
                xuse[it] = p.sT.n
            wuse[hg] = p.sT.n
    return None



def _tok_score(T0, i, k_):
    if i == 0:
        return T0 + 1
    return T0 + (i + 1) + (i - 1) * (k_ - 1)


def _tok_den(T0, cv, n, k_):
    if cv >= n - 1:
        return T0 + n * k_
    return T0 + (cv + 2) + (cv + 1) * (k_ - 1)


def _attention(p, chunks_of, mask_base, kload, vload, do_oproj, ow_key,
               l1_edge=False, kv_shared=False, kv_wait=None):
    nc, blk = p.nc, p.blk
    _phase_barrier(p)
    with nc.sbuf_tensor(_u('akb'), [128, 2, NKC, HD], BF16) as akb, \
         nc.sbuf_tensor(_u('avb'), [128, 2, NKC, HD], BF16) as avb, \
         nc.sbuf_tensor(_u('aqb'), [128, 2, SBW], BF16) as aqb, \
         nc.sbuf_tensor(_u('apt'), [128, 4, SBW], BF16) as apt, \
         nc.sbuf_tensor(_u('aden'), [128, SBW], F32) as aden, \
         nc.sbuf_tensor(_u('aosb'), [128, H, SBW], BF16) as aosb, \
         nc.sbuf_tensor(_u('aow'), [128, 2, 1024], BF16) as aow:

        ufree = {}
        owfree = {}
        bpl = {}
        p0done = {}
        kvspan = {}
        for sb in range(NSB):
            chunks = chunks_of(sb)
            n = len(chunks)
            sv = l1_edge and sb == 0
            sbsl = slice(sb * SBW, (sb + 1) * SBW)

            for h in range(H):
                u = sb * H + h
                qub = h % 2
                ft = ufree.get(u - 2, 0)
                ftk = ufree.get(u - 1, 0)
                ftk0 = p0done.get(u - 1, ftk)
                loads = []
                if kv_shared:
                    if h == 0:
                        for pi in (0, 1):
                            js = sorted(j for (pj, j) in chunks if pj == pi)
                            j1 = js[-1] + 1
                            d0 = kvspan.get(pi, 0)
                            if j1 > d0:
                                loads.append((pi, d0, j1, 0))
                                kvspan[pi] = j1
                else:
                    for pi in (0, 1):
                        js = sorted(j for (pj, j) in chunks if pj == pi)
                        loads.append((pi, js[0], js[-1] + 1,
                                      ftk0 if pi == 0 else ftk))

                ltk0 = [0]

                @blk.sync
                def _(s, h=h, qub=qub, sb=sb, ft=ft, loads=loads, sbsl=sbsl):
                    if ft:
                        s.wait_ge(p.sT.h, ft)
                    cq = p.cQ0 if qub == 0 else p.cQ1
                    s.dma_start(out=aqb[:, qub, :], in_=p.q_scr[h, :, sbsl]).then_inc(cq.h, 16)
                    cq.n += 16
                    if loads and kv_wait is not None:
                        for _kwh, _kwv in kv_wait:
                            s.wait_ge(_kwh, _kwv)
                    hh = 0 if kv_shared else h
                    for pi, j0, j1, wtk in loads:
                        ck = p.cKV if pi == 0 else p.cKV1
                        if wtk:
                            s.wait_ge(p.sT.h, wtk)
                        kload(s, akb[:, pi, j0:j1, :], pi, j0, j1, hh)
                        ck.n += 16
                        vload(s, avb[:, pi, j0:j1, :], pi, j0, j1, hh)
                        ck.n += 16
                        if pi == 0:
                            ltk0[0] = ck.n
                if any(pi == 0 for pi, _a, _b, _c in loads):
                    p.last_ltk0 = p.cKV.n
                ltk0[0] = p.cKV.n if not kv_shared else p.last_ltk0
                ltk = p.cKV1.n
                ltq = (p.cQ0 if qub == 0 else p.cQ1).n

                T0, S0 = p.sT.n, p.sS.n
                sv2b = p.sV2.n
                xt_guard = 0
                if h == 0:
                    xt_guard = p.po_free if (do_oproj and sb > 0) else p.x_tok
                pof = p.po_free if (h == 0 and do_oproj) else 0
                k_ = 4 if sv else 3
                bp = 0 if (sv or u % 2 == 0) else 1
                bav, bdn = (6, 7) if bp == 0 else (4, 5)
                diag = [(_diag_d(sb, j) if mask_base is not None else None)
                        for (_pi, j) in chunks]
                ndth = []
                _c = 0
                for _dd in diag:
                    if _dd is not None:
                        _c += 1
                    ndth.append(_c)
                bpw = bpl.get(bp)

                cq = p.cQ0 if qub == 0 else p.cQ1

                @blk.tensor
                def _(t, sb=sb, h=h, qub=qub, ltk=ltk, ltk0v=ltk0[0], ltq=ltq,
                      cq=cq, chunks=chunks, S0=S0, sv=sv, n=n, diag=diag,
                      ndth=ndth, sv2b=sv2b, bav=bav, bdn=bdn, bpw=bpw,
                      xt_guard=xt_guard, pof=pof):
                    t.wait_ge(p.cKV.h, ltk0v)
                    t.wait_ge(cq.h, ltq)
                    if xt_guard:
                        t.wait_ge(p.sV.h, xt_guard)
                    if pof:
                        t.wait_ge(p.sV.h, pof)

                    def consume(cv, last):
                        pv, jv = chunks[cv]
                        if cv == 0 and bpw is not None:
                            t.wait_ge(bpw[0], bpw[1])
                        if diag[cv] is not None:
                            t.wait_ge(p.sV2.h, sv2b + ndth[cv])
                        else:
                            t.wait_ge(p.sS.h, S0 + cv + 1)
                        t.matmul(_bank(p, bav), avb[:, pv, jv, :],
                                 apt[:, cv % 4, :],
                                 start=(cv == 0), stop=last).then_inc(p.sT.h, 1)
                        if sv:
                            t.matmul(p.ps[:, 2048 + h:2049 + h],
                                     avb[:, pv, jv, :], p.onescb[:, :],
                                     start=(cv == 0), stop=last).then_inc(p.sT.h, 1)
                        t.matmul(p.ps[:, bdn * 512:bdn * 512 + SBW],
                                 p.onesqb[:, :], apt[:, cv % 4, :],
                                 start=(cv == 0), stop=last).then_inc(p.sT.h, 1)
                    first_p1 = next((ii for ii, (pii, _) in enumerate(chunks) if pii == 1), None)
                    for i, (pi, j) in enumerate(chunks):
                        if i == first_p1:
                            t.wait_ge(p.cKV1.h, ltk)
                        if i >= 4:
                            t.wait_ge(p.sS.h, S0 + i - 3)
                        t.matmul(_bank(p, i % 4),
                                 akb[:, pi, j, :], aqb[:, qub, :],
                                 start=True, stop=True).then_inc(p.sT.h, 1)
                        if i >= 1:
                            consume(i - 1, False)
                    consume(n - 1, True)
                p.sT.n += n * k_
                tok_den = p.sT.n
                ufree[u] = tok_den
                npi0 = sum(1 for (_pi, _j) in chunks if _pi == 0)
                p0done[u] = _tok_den(T0, npi0 - 1, n, k_) if npi0 < n else tok_den

                @blk.scalar
                def _(s, T0=T0, n=n, k_=k_):
                    for i in range(n):
                        s.wait_ge(p.sT.h, _tok_score(T0, i, k_))
                        if i >= 4:
                            s.wait_ge(p.sT.h, _tok_den(T0, i - 4, n, k_))
                        s.activation(out=apt[:, i % 4, :], in_=_bank(p, i % 4),
                                     func=AF.Exp, bias=-SHIFT, scale=ATT_SCALE).then_inc(p.sS.h, 1)
                p.sS.n += n

                @blk.vector
                def _(v, chunks=chunks, S0=S0, sb=sb, h=h, diag=diag,
                      tok_den=tok_den, n=n, sv2base=sv2b, bav=bav, bdn=bdn,
                      nvt=p.sV.n + 1, nvb=p.sV.n):
                    sv2 = sv2base
                    for i, (pi, j) in enumerate(chunks):
                        if diag[i] is not None:
                            v.wait_ge(p.sS.h, S0 + i + 1)
                            v.tensor_mul(apt[:, i % 4, :], apt[:, i % 4, :],
                                         p.masks[:, pi * 4 + diag[i], :]).then_inc(p.sV2.h, 1)
                            sv2 += 1
                    v.wait_ge(p.sT.h, tok_den)
                    if l1_edge and sb == 3:
                        v.tensor_tensor(p.ps[:, bdn * 512 + SBW - 1:bdn * 512 + SBW],
                                        p.ps[:, bdn * 512 + SBW - 1:bdn * 512 + SBW],
                                        p.patch[:, :], ALU.add).then_inc(p.sV2.h, 1)
                        sv2 += 1
                        v.wait_ge(p.sV2.h, sv2)
                    v.wait_ge(p.sV.h, nvb)
                    v.reciprocal(aden[:, :], p.ps[:, bdn * 512:bdn * 512 + SBW]).then_inc(p.sV2.h, 1)
                    sv2 += 1
                    v.wait_ge(p.sV2.h, sv2)
                    v.tensor_tensor(aosb[:, h, :], _bank(p, bav), aden[:, :],
                                    ALU.mult).then_inc(p.sV.h, 1)
                    if l1_edge and sb == 3:
                        v.wait_ge(p.sV.h, nvt)
                        v.scalar_tensor_tensor(
                            aosb[:, h, SBW - 1:SBW], aosb[:, h, SBW - 1:SBW],
                            p.edge[:, 0:1], p.sumvb[:, h:h + 1],
                            ALU.mult, ALU.add).then_inc(p.sV.h, 1)
                ndiag = sum(1 for _dd in diag if _dd is not None)
                p.sV2.n += ndiag + (1 if (l1_edge and sb == 3) else 0) + 1
                p.sV.n += 1 + (1 if (l1_edge and sb == 3) else 0)
                bpl[bp] = (p.sV.h, p.sV.n)
            ev_all = p.sV.n

            if sv:
                tokT = p.sT.n

                @blk.vector
                def _(v, tokT=tokT, sv2base=p.sV2.n):
                    v.wait_ge(p.sT.h, tokT)
                    v.tensor_copy(p.sumv[:, :], p.ps[:, 2048:2048 + H]).then_inc(p.sV2.h, 1)
                    v.wait_ge(p.sV2.h, sv2base + 1)
                    v.scalar_tensor_tensor(p.sumvb[:, :], p.sumv[:, :],
                                           p.edge[:, 1:2], p.sumv[:, :],
                                           ALU.mult, ALU.bypass).then_inc(p.sV.h, 1)
                p.sV2.n += 1
                p.sV.n += 1
                bpl[1] = (p.sV.h, p.sV.n)
                ev_all = p.sV.n

            if do_oproj:
                win = p.w[ow_key]
                rtoks = []
                for half in (0, 1):
                    for cc in range(CC):
                        ccb = cc % 2
                        ft = owfree.get(ccb, 0)

                        cw = p.cW0 if ccb == 0 else p.cW1

                        @blk.gpsimd
                        def _(g, cc=cc, half=half, ccb=ccb, ft=ft, cw=cw):
                            if ft:
                                g.wait_ge(p.sT.h, ft)
                            g.dma_start(out=aow[:, ccb, :],
                                        in_=win[:, cc, half * 1024:(half + 1) * 1024]).then_inc(cw.h, 16)
                        cw.n += 16
                        tokW = cw.n
                        rt0 = rtoks[0] if rtoks else 0

                        @blk.tensor
                        def _(t, cc=cc, half=half, ccb=ccb, tokW=tokW,
                              ev_all=ev_all, rt0=rt0, cw=cw):
                            t.wait_ge(cw.h, tokW)
                            if cc == 0:
                                if half == 0:
                                    t.wait_ge(p.sV.h, ev_all)
                                    t.wait_ge(p.sV.h, p.po_free)
                                else:
                                    t.wait_ge(p.sV.h, rt0)
                            for d2c in range(8):
                                _li3 = t.matmul(p.ps[:, d2c * 512:d2c * 512 + SBW],
                                         aow[:, ccb, d2c * 128:(d2c + 1) * 128],
                                         aosb[:, cc, :],
                                         start=(cc == 0), stop=(cc == CC - 1))
                            _li3.then_inc(p.sT.h, 1)
                        p.sT.n += 1
                        owfree[ccb] = p.sT.n
                    tok_half = p.sT.n

                    @blk.vector
                    def _(v, half=half, sbsl=sbsl, tok_half=tok_half):
                        v.wait_ge(p.sT.h, tok_half)
                        for d2c in range(8):
                            cc2 = half * 8 + d2c
                            _li4 = v.tensor_tensor(p.xsb[:, cc2, sbsl], p.xsb[:, cc2, sbsl],
                                            p.ps[:, d2c * 512:d2c * 512 + SBW], ALU.add)
                        _li4.then_inc(p.sV.h, 1)
                    p.sV.n += 1
                    rtoks.append(p.sV.n)
                p.po_free = rtoks[-1]
                _ln_only(p, sbsl, rtoks[-1])
                bpl[0] = (p.sV2.h, p.ln_b67)
                bpl[1] = (p.sV2.h, p.ln_b45)
            else:
                _res_ln(p, sb, from_po=False, y_sb=aosb, ytok=ev_all)



def _res_ln(p, sb, from_po, y_sb, ytok):
    assert not from_po
    nc, blk = p.nc, p.blk
    sbsl = slice(sb * SBW, (sb + 1) * SBW)

    @blk.vector
    def _(v, sb=sb, ytok=ytok):
        if from_po:
            v.wait_ge(p.sT.h, ytok)
            for cc in range(CC):
                _li5 = v.tensor_tensor(p.xsb[:, cc, sbsl], p.xsb[:, cc, sbsl],
                                p.ps[:, cc * 128:(cc + 1) * 128], ALU.add)
        else:
            v.wait_ge(p.sV.h, ytok)
            for cc in range(CC):
                _li5 = v.tensor_tensor(p.xsb[:, cc, sbsl], p.xsb[:, cc, sbsl],
                                y_sb[:, cc, :], ALU.add)
        _li5.then_inc(p.sV.h, 1)
    p.sV.n += 1
    tok_r = p.sV.n
    if from_po:
        p.po_free = tok_r
    _ln_only(p, sbsl, tok_r)


def _ln_only(p, sbsl, tok_r):
    nc, blk = p.nc, p.blk
    S0 = p.sS.n
    T0sq = p.sT.n + 1

    @blk.scalar
    def _(s, tok_r=tok_r, T0sq=T0sq):
        s.wait_ge(p.sV.h, tok_r)
        for cc in range(CC):
            if cc >= 2:
                s.wait_ge(p.sT.h, T0sq + cc - 1)
            s.activation(out=p.sqbuf[:, cc % 2, :], in_=p.xsb[:, cc, sbsl],
                         func=AF.Square).then_inc(p.sS.h, 1)
    p.sS.n += CC

    @blk.tensor
    def _(t, tok_r=tok_r, S0=S0):
        t.wait_ge(p.sV.h, tok_r)
        for cc in range(CC):
            _li6 = t.matmul(p.ps[0:1, 2048:2048 + SBW], p.onesc[:, :], p.xsb[:, cc, sbsl],
                     start=(cc == 0), stop=(cc == CC - 1))
        _li6.then_inc(p.sT.h, 1)
        for cc in range(CC):
            t.wait_ge(p.sS.h, S0 + cc + 1)
            t.matmul(p.ps[0:1, 2560:2560 + SBW], p.onesc[:, :], p.sqbuf[:, cc % 2, :],
                     start=(cc == 0), stop=(cc == CC - 1)).then_inc(p.sT.h, 1)
    p.sT.n += 1 + CC
    tok_stats = p.sT.n

    p.ln_b45 = p.sV2.n + 2

    @blk.vector
    def _(v, tok_stats=tok_stats, sv2b=p.sV2.n):
        v.wait_ge(p.sT.h, tok_stats)
        v.tensor_scalar_mul(p.lnm[0:1, :], p.ps[0:1, 2048:2048 + SBW], 1.0 / D).then_inc(p.sV2.h, 1)
        v.tensor_scalar_mul(p.lnt1[0:1, :], p.ps[0:1, 2560:2560 + SBW], 1.0 / D).then_inc(p.sV2.h, 1)
        v.wait_ge(p.sV2.h, sv2b + 2)
        v.tensor_mul(p.lnt2[0:1, :], p.lnm[0:1, :], p.lnm[0:1, :]).then_inc(p.sV2.h, 1)
        v.wait_ge(p.sV2.h, sv2b + 3)
        v.tensor_sub(p.lnt1[0:1, :], p.lnt1[0:1, :], p.lnt2[0:1, :]).then_inc(p.sV2.h, 1)
        v.wait_ge(p.sV2.h, sv2b + 4)
        v.tensor_scalar_add(p.lnt1[0:1, :], p.lnt1[0:1, :], LN_EPS).then_inc(p.sV.h, 1)
    p.sV2.n += 4
    p.sV.n += 1
    tok_var = p.sV.n

    @blk.scalar
    def _(s, tok_var=tok_var, ss2b=p.sS2.n):
        s.wait_ge(p.sV.h, tok_var)
        s.activation(out=p.lnt2[0:1, :], in_=p.lnt1[0:1, :], func=AF.Ln).then_inc(p.sS2.h, 1)
        s.wait_ge(p.sS2.h, ss2b + 1)
        s.activation(out=p.lns[0:1, :], in_=p.lnt2[0:1, :], func=AF.Exp,
                     scale=-0.5).then_inc(p.sS.h, 1)
    p.sS2.n += 1
    p.sS.n += 1
    tok_s0 = p.sS.n

    @blk.vector
    def _(v, tok_s0=tok_s0, sv2b=p.sV2.n):
        v.wait_ge(p.sS.h, tok_s0)
        v.tensor_mul(p.lnt2[0:1, :], p.lns[0:1, :], p.lns[0:1, :]).then_inc(p.sV2.h, 1)
        v.wait_ge(p.sV2.h, sv2b + 1)
        v.tensor_mul(p.lnt2[0:1, :], p.lnt2[0:1, :], p.lnt1[0:1, :]).then_inc(p.sV2.h, 1)
        v.wait_ge(p.sV2.h, sv2b + 2)
        v.tensor_scalar(p.lnt2[0:1, :], p.lnt2[0:1, :], -0.5, 1.5, ALU.mult, ALU.add).then_inc(p.sV2.h, 1)
        v.wait_ge(p.sV2.h, sv2b + 3)
        v.tensor_mul(p.lns[0:1, :], p.lns[0:1, :], p.lnt2[0:1, :]).then_inc(p.sV2.h, 1)
        v.wait_ge(p.sV2.h, sv2b + 4)
        v.tensor_mul(p.lnm[0:1, :], p.lnm[0:1, :], p.lns[0:1, :]).then_inc(p.sV.h, 1)
    p.sV2.n += 4
    p.sV.n += 1
    tok_s = p.sV.n

    @blk.tensor
    def _(t, tok_s=tok_s):
        t.wait_ge(p.sV.h, tok_s)
        t.matmul(p.ps[:, 3072:3072 + SBW], p.onesr[:, :], p.lns[0:1, :], start=True, stop=True)
        t.matmul(p.ps[:, 3584:3584 + SBW], p.onesr[:, :], p.lnm[0:1, :], start=True, stop=True).then_inc(p.sT.h, 1)
    p.sT.n += 1
    tok_bc = p.sT.n

    p.ln_b67 = p.sV2.n + 1

    @blk.vector
    def _(v, tok_bc=tok_bc, sv2b=p.sV2.n):
        v.wait_ge(p.sT.h, tok_bc)
        v.tensor_copy(p.sbst[:, 0, :], p.ps[:, 3072:3072 + SBW])
        v.tensor_copy(p.sbst[:, 1, :], p.ps[:, 3584:3584 + SBW]).then_inc(p.sV2.h, 1)
        v.wait_ge(p.sV2.h, sv2b + 1)
        for cc in range(CC):
            _li10 = v.tensor_mul(p.xsb[:, cc, sbsl], p.xsb[:, cc, sbsl], p.sbst[:, 0, :])
        _li10.then_inc(p.sV2.h, 1)
        v.wait_ge(p.sV2.h, sv2b + 2)
        for cc in range(CC):
            _li11 = v.tensor_sub(p.xsb[:, cc, sbsl], p.xsb[:, cc, sbsl], p.sbst[:, 1, :])
        _li11.then_inc(p.sV.h, 1)
    p.sV2.n += 2
    p.sV.n += 1
    p.x_tok = p.sV.n



def _l0_attn(p):
    nc, blk = p.nc, p.blk
    _phase_barrier(p)

    with nc.sbuf_tensor(_u('kvw'), [128, CC, 2 * HD], BF16) as kvw, \
         nc.sbuf_tensor(_u('kvx'), [128, 2, CC, SBW], BF16) as kvx:

        @blk.gpsimd
        def _(g):
            g.dma_start(out=kvw[:, :, 0:HD], in_=p.w['l0_kwT'][:, :, :]).then_inc(p.cG.h, 16)
            g.dma_start(out=kvw[:, :, HD:2 * HD], in_=p.w['l0_vwT'][:, :, :]).then_inc(p.cG.h, 16)
        p.cG.n += 32
        tokW = p.cG.n

        kst = {}
        xslot = {}
        x_pre = p.sT.n
        for tsb in range(NSB):
            sl = slice(tsb * SBW, (tsb + 1) * SBW)
            tb = tsb % 2
            tokT_prev = xslot.get(tsb - 2, x_pre)

            @blk.vector
            def _(v, sl=sl, tb=tb, tokT_prev=tokT_prev):
                v.wait_ge(p.sT.h, tokT_prev)
                v.tensor_copy(kvx[:, tb, :, :], p.xsb[:, :, sl]).then_inc(p.sV2.h, 1)
            p.sV2.n += 1
            tokX = p.sV2.n
            b4, b5 = p.bankS.get(4, 0), p.bankS.get(5, 0)

            @blk.tensor
            def _(t, tb=tb, tokX=tokX, tokW=tokW, b4=b4, b5=b5):
                t.wait_ge(p.cG.h, tokW)
                t.wait_ge(p.sV2.h, tokX)
                t.wait_ge(p.sS.h, max(b4, b5))
                for cc in range(CC):
                    _li11 = t.matmul(_bank(p, 4), kvw[:, cc, 0:HD], kvx[:, tb, cc, :],
                             start=(cc == 0), stop=(cc == CC - 1))
                _li11.then_inc(p.sT.h, 1)
                for cc in range(CC):
                    _li12 = t.matmul(_bank(p, 5), kvw[:, cc, HD:2 * HD], kvx[:, tb, cc, :],
                             start=(cc == 0), stop=(cc == CC - 1))
                _li12.then_inc(p.sT.h, 1)
            p.sT.n += 2
            tokKV = p.sT.n
            xslot[tsb] = tokKV

            @blk.scalar
            def _(s, tokKV=tokKV, pk=kst.get('k', 0)):
                s.wait_ge(p.sT.h, tokKV)
                s.wait_ge(p.cE0.h, pk)
                s.copy(out=p.evac[:, 0, :], in_=_bank(p, 4)).then_inc(p.sS.h, 1)
                s.copy(out=p.evac[:, 1, :], in_=_bank(p, 5)).then_inc(p.sS.h, 1)
            p.sS.n += 2
            p.bankS[4] = p.sS.n - 1
            p.bankS[5] = p.sS.n
            tokEv = p.sS.n

            @blk.sync
            def _(s, tsb=tsb, tokEv=tokEv):
                s.wait_ge(p.sS.h, tokEv)
                s.dma_start(out=p.ag_kv_in[0, :, tsb * 4:(tsb + 1) * 4, :],
                            in_=p.evac[:, 0, :].rearrange('p (c d) -> p c d', d=HD)).then_inc(p.cE0.h, 16)
            p.cE0.n += 16
            kst['k'] = p.cE0.n

            for tt in (range(4) if _SUB not in ('l0a_prodK',) else []):
                b7 = p.bankS.get(7, 0)

                @blk.tensor
                def _(t, tt=tt, tokEv=tokEv, b7=b7):
                    t.wait_ge(p.sS.h, tokEv)
                    t.wait_ge(p.sS.h, b7)
                    t.transpose(_bank_bf(p, 7),
                                p.evac[:, 1, tt * 128:(tt + 1) * 128], p.ident[:, :]).then_inc(p.sT.h, 1)
                p.sT.n += 1
                tokTr = p.sT.n

                @blk.scalar
                def _(s, tt=tt, tokTr=tokTr, pv=kst.get('v', 0)):
                    s.wait_ge(p.sT.h, tokTr)
                    s.wait_ge(p.cE2.h, pv)
                    s.copy(out=p.evac[:, 2, tt * 128:(tt + 1) * 128],
                           in_=_bank_bf(p, 7)).then_inc(p.sS.h, 1)
                p.sS.n += 1
                p.bankS[7] = p.sS.n
            tokVt = p.sS.n

            if _SUB not in ('l0a_prodK',):
                @blk.sync
                def _(s, tsb=tsb, tokVt=tokVt):
                    s.wait_ge(p.sS.h, tokVt)
                    s.dma_start(out=p.ag_kv_in[1, :, tsb * 4:(tsb + 1) * 4, :],
                                in_=p.evac[:, 2, :].rearrange('p (c d) -> p c d', d=HD)).then_inc(p.cE2.h, 16)
                p.cE2.n += 16
                kst['v'] = p.cE2.n
        tok_k, tok_v = p.cE0.n, p.cE2.n

    if _SUB in ('l0a_prod', 'l0a_prodK'):
        @blk.gpsimd
        def _(g, tok_k=tok_k, tok_v=tok_v):
            g.wait_ge(p.cE0.h, tok_k)
            g.wait_ge(p.cE2.h, tok_v)
        return

    @blk.gpsimd
    def _(g, tok_k=tok_k, tok_v=tok_v):
        g.wait_ge(p.cE0.h, tok_k)
        g.wait_ge(p.cE2.h, tok_v)
        g.collective_compute('AllGather', ALU.bypass, replica_groups=p.RG,
                             ins=[p.ag_kv_in[:, :, :, :].opt()],
                             outs=[p.ag_kv_out[:, :, :, :, :].opt()]).then_inc(p.sCC.h, 1)
    p.sCC.n += 1
    kv_ready = p.sCC.n

    if _SUB == 'l0a_kv':
        return
    _proj_to_scratch(p, 'l0_qwT', p.q_scr, H)
    if _SUB == 'l0a_kvq':
        return

    def kload(s, dst, pi, j0, j1, h):
        ck = p.cKV if pi == 0 else p.cKV1
        s.dma_start(out=dst, in_=p.ag_kv_out[pi, 0, :, j0:j1, :]).then_inc(ck.h, 16)

    def vload(s, dst, pi, j0, j1, h):
        ck = p.cKV if pi == 0 else p.cKV1
        s.dma_start(out=dst, in_=p.ag_kv_out[pi, 1, :, j0:j1, :]).then_inc(ck.h, 16)

    _attention(p, _full_chunks, None, kload, vload, do_oproj=True,
               ow_key='l0_owT', kv_shared=True, kv_wait=[(p.sCC.h, kv_ready)])



def _l0_ffn(p):
    nc, blk = p.nc, p.blk
    _phase_barrier(p)
    mask_tok = _load_masks(p, 0, p.x_tok)

    with nc.sbuf_tensor(_u('gw'), [128, 2, CC, H], BF16) as gw, \
         nc.sbuf_tensor(_u('gwf'), [128, CC, H], F32) as gwf, \
         nc.sbuf_tensor(_u('fxh'), [128, 2, CC, 128], BF16) as fxh, \
         nc.sbuf_tensor(_u('fxl'), [128, 2, CC, 128], BF16) as fxl, \
         nc.sbuf_tensor(_u('gind'), [128, NKC, H], F32) as gind, \
         nc.sbuf_tensor(_u('gtop'), [128, NKC, H], F32) as gtop, \
         nc.sbuf_tensor(_u('glg'), [128, 2, H], F32) as glg, \
         nc.sbuf_tensor(_u('glge'), [128, 2, H], F32) as glge, \
         nc.sbuf_tensor(_u('gmx'), [128, 2, 2], F32) as gmx, \
         nc.sbuf_tensor(_u('gden'), [128, 2, 1], F32) as gden, \
         nc.sbuf_tensor(_u('fkw'), [128, 2, CC, 256], BF16) as fkw, \
         nc.sbuf_tensor(_u('fsh'), [128, 2, NKC, HD], BF16) as fsh:

        @blk.gpsimd
        def _(g):
            g.dma_start(out=gwf[:, :, :], in_=p.w['l0_wgT'][:, :, :]).then_inc(p.cG.h, 16)
            g.dma_start(out=gw[:, 0, :, :], in_=p.w['l0_wgT'][:, :, :]).then_inc(p.cG.h, 16)
        p.cG.n += 32
        tokGW = p.cG.n

        @blk.vector
        def _(v, tokGW=tokGW):
            v.wait_ge(p.cG.h, tokGW)
            v.tensor_sub(gw[:, 1, :, :], gwf[:, :, :], gw[:, 0, :, :]).then_inc(p.sV.h, 1)
        p.sV.n += 1
        tokGWL = p.sV.n

        kvdone = {}
        glgcopy = {}
        expdone = {}
        gtdone = {}
        fkw_use = {}
        comb_done = {}
        g_pre = p.sT.n
        for tc in range(NKC):
            tsl = slice(tc * 128, (tc + 1) * 128)
            tb = tc % 2
            tokT_prev = kvdone.get(tc - 2, g_pre)
            xt = p.x_tok

            @blk.vector
            def _(v, tsl=tsl, tb=tb, tokT_prev=tokT_prev, xt=xt, sv2b=p.sV2.n):
                v.wait_ge(p.sV.h, xt)
                v.wait_ge(p.sT.h, tokT_prev)
                v.tensor_copy(fxh[:, tb, :, :], p.xsb[:, :, tsl]).then_inc(p.sV2.h, 1)
                v.wait_ge(p.sV2.h, sv2b + 1)
                v.scalar_tensor_tensor(fxl[:, tb, :, :], fxh[:, tb, :, :], -1.0,
                                       p.xsb[:, :, tsl], ALU.mult, ALU.add).then_inc(p.sV.h, 1)
            p.sV2.n += 1
            p.sV.n += 1
            tokXL = p.sV.n
            gbk = 6 + tb
            bkg = glgcopy.get(tc - 2, 0)

            @blk.tensor
            def _(t, tb=tb, gbk=gbk, tokXL=tokXL, tokGWL=tokGWL, bkg=bkg):
                t.wait_ge(p.sV.h, max(tokXL, tokGWL, bkg))
                for cc in range(CC):
                    t.matmul(_bank(p, gbk, H), fxh[:, tb, cc, :], gw[:, 0, cc, :],
                             start=(cc == 0), stop=False)
                for cc in range(CC):
                    t.matmul(_bank(p, gbk, H), fxh[:, tb, cc, :], gw[:, 1, cc, :],
                             start=False, stop=False)
                for cc in range(CC):
                    _li16 = t.matmul(_bank(p, gbk, H), fxl[:, tb, cc, :], gw[:, 0, cc, :],
                             start=False, stop=(cc == CC - 1))
                _li16.then_inc(p.sT.h, 1)
            p.sT.n += 1
            tokLG = p.sT.n

            @blk.vector
            def _(v, tc=tc, tb=tb, gbk=gbk, tokLG=tokLG, svb=p.sV.n,
                  sv2b=p.sV2.n, geprev=expdone.get(tc - 2, 0)):
                v.wait_ge(p.sT.h, tokLG)
                v.wait_ge(p.sS.h, geprev)
                v.tensor_copy(glg[:, tb, :], _bank(p, gbk, H)).then_inc(p.sV.h, 1)
                v.wait_ge(p.sV.h, svb + 1)
                v.tensor_reduce(gmx[:, tb, 0:1], glg[:, tb, :], mybir.AxisListType.X,
                                ALU.max).then_inc(p.sV2.h, 1)
                v.wait_ge(p.sV2.h, sv2b + 1)
                v.tensor_scalar_mul(gmx[:, tb, 1:2], gmx[:, tb, 0:1], -1.0).then_inc(p.sV2.h, 1)
                v.tensor_scalar(gind[:, tc, :], glg[:, tb, :], gmx[:, tb, 0:1], None,
                                ALU.is_ge).then_inc(p.sV.h, 1)
            p.sV2.n += 2
            p.sV.n += 2
            glgcopy[tc] = p.sV.n - 1
            tokGMAX = p.sV.n

            @blk.scalar
            def _(s, tb=tb, tokGMAX=tokGMAX, gtprev=gtdone.get(tc - 2, 0)):
                s.wait_ge(p.sV.h, max(tokGMAX, gtprev))
                s.activation(out=glge[:, tb, :], in_=glg[:, tb, :], func=AF.Exp,
                             bias=gmx[:, tb, 1:2], accum_out=gden[:, tb, :]).then_inc(p.sS.h, 1)
            p.sS.n += 1
            tokGE = p.sS.n
            expdone[tc] = tokGE

            @blk.vector
            def _(v, tc=tc, tb=tb, tokGE=tokGE, sv2b=p.sV2.n):
                v.wait_ge(p.sS.h, tokGE)
                v.reciprocal(gden[:, tb, :], gden[:, tb, :]).then_inc(p.sV2.h, 1)
                v.wait_ge(p.sV2.h, sv2b + 1)
                v.tensor_scalar(gtop[:, tc, :], gind[:, tc, :], gden[:, tb, 0:1], None,
                                ALU.mult).then_inc(p.sV.h, 1)
            p.sV2.n += 1
            p.sV.n += 1
            gtdone[tc] = p.sV.n
            tok_gt = p.sV.n

            for e in range(8):
                shi = 0 if e < 4 else 1
                hbase = (e % 4) * 4
                tokKWs = []
                for half in (0, 1):
                    lh = (tc * 8 + e) * 2 + half
                    ws = lh % 2
                    cw = p.cW0 if ws == 0 else p.cW1
                    wprev = fkw_use.get(lh - 2, g_pre)

                    @blk.gpsimd
                    def _(g, e=e, half=half, ws=ws, wprev=wprev, cw=cw):
                        g.wait_ge(p.sT.h, wprev)
                        g.dma_start(out=fkw[:, ws, :, :],
                                    in_=p.w['l0_kvT'][:, :, e * 512 + half * 256:e * 512 + (half + 1) * 256]).then_inc(cw.h, 16)
                    cw.n += 16
                    tokKWs.append((cw, cw.n, ws))
                bk = 4 + (e % 2)
                bkv = comb_done.get((e % 2, tc * 8 + e - 2), 0)

                @blk.tensor
                def _(t, tb=tb, bk=bk, tokKWs=tokKWs, tokXL=tokXL, bkv=bkv):
                    t.wait_ge(p.sV.h, max(tokXL, bkv))
                    for half, (cw, twk, ws) in enumerate(tokKWs):
                        t.wait_ge(cw.h, twk)
                        for cc in range(CC):
                            _li20 = t.matmul(_bank(p, bk)[:, half * 256:(half + 1) * 256],
                                     fxh[:, tb, cc, :], fkw[:, ws, cc, :],
                                     start=(half == 0 and cc == 0),
                                     stop=(half == 1 and cc == CC - 1))
                        if half == 0:
                            _li20.then_inc(p.sT.h, 1)
                    _li20.then_inc(p.sT.h, 1)
                p.sT.n += 2
                tokKVm = p.sT.n
                fkw_use[(tc * 8 + e) * 2] = tokKVm - 1
                fkw_use[(tc * 8 + e) * 2 + 1] = tokKVm

                @blk.vector
                def _(v, e=e, tc=tc, bk=bk, shi=shi, hbase=hbase, tokKVm=tokKVm,
                      tokGMAX=tokGMAX, tok_gt=tok_gt, sv2b=p.sV2.n):
                    v.wait_ge(p.sT.h, tokKVm)
                    v.wait_ge(p.sV.h, tokGMAX if e < 4 else tok_gt)
                    gsel = gind if e < 4 else gtop
                    first = (e % 4 == 0)
                    src4 = _bank(p, bk)
                    sv2 = sv2b
                    for hh in range(4):
                        h = hbase + hh
                        so = hh * 128
                        if not (first and hh == 0):
                            v.wait_ge(p.sV2.h, sv2)
                        if first and hh == 0:
                            _li21 = v.tensor_scalar(
                                fsh[:, shi, tc, :], src4[:, so:so + 128],
                                gsel[:, tc, h:h + 1], None, ALU.mult)
                        else:
                            _li21 = v.scalar_tensor_tensor(
                                fsh[:, shi, tc, :], src4[:, so:so + 128],
                                gsel[:, tc, h:h + 1], fsh[:, shi, tc, :],
                                ALU.mult, ALU.add)
                        _li21.then_inc(p.sV2.h, 1)
                        sv2 += 1
                    v.wait_ge(p.sV2.h, sv2)
                    v.engine_nop().then_inc(p.sV.h, 1)
                p.sV2.n += 4
                p.sV.n += 1
                comb_done[(e % 2, tc * 8 + e)] = p.sV.n
            kvdone[tc] = p.sT.n
        tok_sel = p.sV.n

        with nc.sbuf_tensor(_u('fshT'), [128, NKC, HD], BF16) as fshT:
            for tc in range(NKC):
                b7 = p.bankS.get(7, 0)

                @blk.tensor
                def _(t, tc=tc, tok_sel=tok_sel, b7=b7):
                    t.wait_ge(p.sV.h, tok_sel)
                    t.wait_ge(p.sS.h, b7)
                    t.transpose(_bank_bf(p, 7), fsh[:, 0, tc, :], p.ident[:, :]).then_inc(p.sT.h, 1)
                p.sT.n += 1
                tokTr = p.sT.n

                @blk.scalar
                def _(s, tc=tc, tokTr=tokTr):
                    s.wait_ge(p.sT.h, tokTr)
                    s.copy(out=fshT[:, tc, :], in_=_bank_bf(p, 7)).then_inc(p.sS.h, 1)
                p.sS.n += 1
                p.bankS[7] = p.sS.n
            tokTall = p.sS.n

            @blk.sync
            def _(s, tokTall=tokTall, tok_sel=tok_sel):
                s.wait_ge(p.sS.h, tokTall)
                s.wait_ge(p.sV.h, tok_sel)
                s.dma_start(out=p.ag_kv_in[0, :, :, :], in_=fshT[:, :, :]).then_inc(p.cA.h, 16)
                s.dma_start(out=p.ag_kv_in[1, :, :, :], in_=fsh[:, 1, :, :]).then_inc(p.cA.h, 16)
            p.cA.n += 32
            tok_stores = p.cA.n

    @blk.gpsimd
    def _(g, tok_stores=tok_stores):
        g.wait_ge(p.cA.h, tok_stores)
        g.collective_compute('AllGather', ALU.bypass, replica_groups=p.RG,
                             ins=[p.ag_kv_in[:, :, :, :].opt()],
                             outs=[p.ag_kv_out[:, :, :, :, :].opt()]).then_inc(p.sCC.h, 1)
    p.sCC.n += 1
    kv_ready = p.sCC.n

    _proj_to_scratch(p, 'l0_qT', p.q_scr, H)

    def kload(s, dst, pi, j0, j1, h):
        ck = p.cKV if pi == 0 else p.cKV1
        s.dma_start(out=dst, in_=p.ag_kv_out[pi, 0, :, j0:j1, :]).then_inc(ck.h, 16)

    def vload(s, dst, pi, j0, j1, h):
        ck = p.cKV if pi == 0 else p.cKV1
        s.dma_start(out=dst, in_=p.ag_kv_out[pi, 1, :, j0:j1, :]).then_inc(ck.h, 16)

    _attention(p, _causal_chunks, 0, kload, vload, do_oproj=False,
               ow_key=None, kv_shared=True, kv_wait=[(p.sCC.h, kv_ready)])



def _l1_attn(p):
    nc, blk = p.nc, p.blk
    _phase_barrier(p)
    mask_tok = _load_masks(p, 8, p.x_tok)

    xt = p.x_tok
    with nc.sbuf_tensor(_u('lkw'), [128, 2, CC, 256], BF16) as lkw, \
         nc.sbuf_tensor(_u('lxt'), [128, 2, CC, SBW], BF16) as lxt:
        wuse = {}
        xuse = {}
        k_pre = p.sT.n
        for tsb in range(NSB):
            xs = tsb % 2
            xprev = xuse.get(tsb - 2, k_pre)
            tsl = slice(tsb * SBW, (tsb + 1) * SBW)

            @blk.vector
            def _(v, tsl=tsl, xs=xs, xprev=xprev, xt=xt):
                v.wait_ge(p.sV.h, xt)
                v.wait_ge(p.sT.h, xprev)
                v.tensor_copy(lxt[:, xs, :, :], p.xsb[:, :, tsl]).then_inc(p.sV2.h, 1)
            p.sV2.n += 1
            tokX = p.sV2.n

            for hp in range(8):
                q = tsb * 8 + hp
                ws = q % 2
                cw = p.cW0 if ws == 0 else p.cW1
                wprev = wuse.get(q - 2, k_pre)

                @blk.gpsimd
                def _(g, hp=hp, ws=ws, wprev=wprev, cw=cw):
                    g.wait_ge(p.sT.h, wprev)
                    g.dma_start(out=lkw[:, ws, :, :],
                                in_=p.w['l1_wk'][:, :, hp * 256:(hp + 1) * 256]).then_inc(cw.h, 16)
                cw.n += 16
                tokKW = cw.n

                for hh in range(2):
                    h = hp * 2 + hh
                    bk = 4 + hh
                    ev = hh
                    bs = p.bankS.get(bk, 0)

                    @blk.tensor
                    def _(t, hh=hh, ws=ws, xs=xs, bk=bk, tokX=tokX,
                          tokKW=tokKW, bs=bs, cw=cw):
                        t.wait_ge(cw.h, tokKW)
                        t.wait_ge(p.sV2.h, tokX)
                        t.wait_ge(p.sS.h, bs)
                        for cc in range(CC):
                            _li23 = t.matmul(_bank(p, bk), lkw[:, ws, cc, hh * 128:(hh + 1) * 128],
                                     lxt[:, xs, cc, :], start=(cc == 0), stop=(cc == CC - 1))
                        _li23.then_inc(p.sT.h, 1)
                    p.sT.n += 1
                    tokMM = p.sT.n
                    evfree = p.evS.get(ev, 0)
                    ce = p.cE0 if ev == 0 else p.cE1

                    @blk.scalar
                    def _(s, ev=ev, bk=bk, tokMM=tokMM, evfree=evfree, ce=ce):
                        s.wait_ge(p.sT.h, tokMM)
                        s.wait_ge(ce.h, evfree)
                        s.copy(out=p.evac[:, ev, :], in_=_bank(p, bk)).then_inc(p.sS.h, 1)
                    p.sS.n += 1
                    p.bankS[bk] = p.sS.n
                    tokEv = p.sS.n

                    @blk.sync
                    def _(s, h=h, tsl=tsl, ev=ev, tokEv=tokEv, ce=ce):
                        s.wait_ge(p.sS.h, tokEv)
                        s.dma_start(out=p.agki[h, :, tsl],
                                    in_=p.evac[:, ev, :]).then_inc(ce.h, 16)
                    ce.n += 16
                    p.evS[ev] = ce.n
                wuse[q] = p.sT.n
            xuse[tsb] = p.sT.n

    _phase_barrier(p)
    with nc.sbuf_tensor(_u('lvw'), [128, 2, CC, 512], BF16) as lvw, \
         nc.sbuf_tensor(_u('lxc'), [128, 2, CC, 128], BF16) as lxc:
        vwuse = {}
        vxuse = {}
        v_pre = p.sT.n
        for vh in range(2):
            for half in (0, 1):
                idx = 2 * vh + half
                cw = p.cW0 if half == 0 else p.cW1
                wprev = vwuse.get(idx - 2, v_pre)

                @blk.gpsimd
                def _(g, vh=vh, half=half, wprev=wprev, cw=cw):
                    g.wait_ge(p.sT.h, wprev)
                    c0 = vh * 1024 + half * 512
                    g.dma_start(out=lvw[:, half, :, :],
                                in_=p.w['l1_wv'][:, :, c0:c0 + 512]).then_inc(cw.h, 16)
                cw.n += 16
            tokVW0 = p.cW0.n
            tokVW1 = p.cW1.n

            for tc in range(NKC):
                tb = tc % 2
                it4 = vh * NKC + tc
                tsl = slice(tc * 128, (tc + 1) * 128)
                xprev = vxuse.get(it4 - 2, v_pre)

                @blk.vector
                def _(v, tsl=tsl, tb=tb, xprev=xprev, xt=xt):
                    v.wait_ge(p.sV.h, xt)
                    v.wait_ge(p.sT.h, xprev)
                    v.tensor_copy(lxc[:, tb, :, :], p.xsb[:, :, tsl]).then_inc(p.sV2.h, 1)
                p.sV2.n += 1
                tokXC = p.sV2.n
                b4, b5 = p.bankS.get(4, 0), p.bankS.get(5, 0)

                @blk.tensor
                def _(t, tb=tb, tokXC=tokXC, tokVW0=tokVW0, tokVW1=tokVW1,
                      b4=b4, b5=b5):
                    t.wait_ge(p.cW0.h, tokVW0)
                    t.wait_ge(p.cW1.h, tokVW1)
                    t.wait_ge(p.sV2.h, tokXC)
                    t.wait_ge(p.sS.h, max(b4, b5))
                    for cc in range(CC):
                        t.matmul(_bank(p, 4), lxc[:, tb, cc, :], lvw[:, 0, cc, :],
                                 start=(cc == 0), stop=(cc == CC - 1))
                    for cc in range(CC):
                        _li24 = t.matmul(_bank(p, 5), lxc[:, tb, cc, :], lvw[:, 1, cc, :],
                                 start=(cc == 0), stop=(cc == CC - 1))
                    _li24.then_inc(p.sT.h, 1)
                p.sT.n += 1
                tokVm = p.sT.n
                vxuse[it4] = tokVm
                if tc == NKC - 1:
                    vwuse[2 * vh] = tokVm
                    vwuse[2 * vh + 1] = tokVm

                @blk.scalar
                def _(s, tokVm=tokVm, pv=p.evS.get('l1v', 0)):
                    s.wait_ge(p.sT.h, tokVm)
                    s.wait_ge(p.cE2.h, pv)
                    s.copy(out=p.evac[:, 0, :], in_=_bank(p, 4))
                    s.copy(out=p.evac[:, 1, :], in_=_bank(p, 5)).then_inc(p.sS.h, 1)
                p.sS.n += 1
                p.bankS[4] = p.sS.n
                p.bankS[5] = p.sS.n
                tokEv = p.sS.n

                @blk.sync
                def _(s, tc=tc, vh=vh, tokEv=tokEv):
                    s.wait_ge(p.sS.h, tokEv)
                    s.dma_start(out=p.agvi[:, tc, vh * 1024:vh * 1024 + 512],
                                in_=p.evac[:, 0, :]).then_inc(p.cE2.h, 16)
                    s.dma_start(out=p.agvi[:, tc, vh * 1024 + 512:(vh + 1) * 1024],
                                in_=p.evac[:, 1, :]).then_inc(p.cE2.h, 16)
                p.cE2.n += 32
                p.evS['l1v'] = p.cE2.n

    @blk.gpsimd
    def _(g, e0=p.cE0.n, e1=p.cE1.n, e2=p.cE2.n):
        g.wait_ge(p.cE0.h, e0)
        g.wait_ge(p.cE1.h, e1)
        g.wait_ge(p.cE2.h, e2)
        g.collective_compute('AllGather', ALU.bypass, replica_groups=p.RG,
                             ins=[p.agki[:, :, :].opt()],
                             outs=[p.agko[:, :, :, :].opt()]).then_inc(p.sCC.h, 1)
        g.collective_compute('AllGather', ALU.bypass, replica_groups=p.RG,
                             ins=[p.agvi[:, :, :].opt()],
                             outs=[p.agvo[:, :, :, :].opt()]).then_inc(p.sCC.h, 1)
    p.sCC.n += 2
    kv_ag = p.sCC.n

    _proj_to_scratch(p, 'l1_wq', p.q_scr, H)
    kv_stores = [(p.sCC.h, kv_ag)]

    def kload(s, dst, pi, j0, j1, h):
        ck = p.cKV if pi == 0 else p.cKV1
        s.dma_start(out=dst,
                    in_=p.agko[pi, h, :, j0 * KCW:j1 * KCW].rearrange(
                        'p (c d) -> p c d', d=HD)).then_inc(ck.h, 16)

    def vload(s, dst, pi, j0, j1, h):
        ck = p.cKV if pi == 0 else p.cKV1
        s.dma_start(out=dst,
                    in_=p.agvo[pi, :, j0:j1, h * HD:(h + 1) * HD]).then_inc(ck.h, 16)

    _attention(p, _anti_chunks, 0, kload, vload, do_oproj=True,
               ow_key='l1_wo', l1_edge=True, kv_shared=False,
               kv_wait=kv_stores)



def _l1_ffn(p):
    nc, blk = p.nc, p.blk
    _phase_barrier(p)
    p.bankS[4] = 0
    p.bankS[5] = 0

    with nc.sbuf_tensor(_u('fgw'), [128, 2, CC, 256], BF16) as fgw, \
         nc.sbuf_tensor(_u('fuw'), [128, 2, CC, 256], BF16) as fuw, \
         nc.sbuf_tensor(_u('fxb'), [128, CC, SBW], BF16) as fxb, \
         nc.sbuf_tensor(_u('fsg'), [128, SBW], BF16) as fsg:

        xt = p.x_tok
        wslot = {}
        w_pre = p.sT.n
        for hq in range(NKC):
            for half in (0, 1):
                idx = 2 * hq + half
                cw = p.cW0 if half == 0 else p.cW1
                wprev = wslot.get(idx - 2, w_pre)

                @blk.gpsimd
                def _(g, hq=hq, half=half, wprev=wprev, cw=cw):
                    g.wait_ge(p.sT.h, wprev)
                    c0 = hq * 512 + half * 256
                    g.dma_start(out=fgw[:, half, :, :],
                                in_=p.w['l1_wgT'][:, :, c0:c0 + 256]).then_inc(cw.h, 16)
                    g.dma_start(out=fuw[:, half, :, :],
                                in_=p.w['l1_wuT'][:, :, c0:c0 + 256]).then_inc(cw.h, 16)
                cw.n += 32
            tokW0 = p.cW0.n
            tokW1 = p.cW1.n

            for tsb in range(NSB):
                tsl = slice(tsb * SBW, (tsb + 1) * SBW)
                tokT_prev2 = p.sT.n

                @blk.vector
                def _(v, tsl=tsl, tokT_prev2=tokT_prev2, xt=xt):
                    v.wait_ge(p.sV.h, xt)
                    v.wait_ge(p.sT.h, tokT_prev2)
                    v.tensor_copy(fxb[:, :, :], p.xsb[:, :, tsl]).then_inc(p.sV2.h, 1)
                p.sV2.n += 1
                tokXB = p.sV2.n
                for hh in range(4):
                    hc = hq * 4 + hh
                    hs = hh // 2
                    ho = (hh % 2) * 128
                    tw = tokW0 if hs == 0 else tokW1
                    cwx = p.cW0 if hs == 0 else p.cW1
                    b4 = p.bankS.get(4, 0)
                    b5 = p.bankS.get(5, 0)

                    @blk.tensor
                    def _(t, hs=hs, ho=ho, tw=tw, cwx=cwx, tokXB=tokXB, b4=b4, b5=b5):
                        t.wait_ge(cwx.h, tw)
                        t.wait_ge(p.sV2.h, tokXB)
                        t.wait_ge(p.sS.h, b4)
                        for cc in range(CC):
                            _li25 = t.matmul(_bank(p, 4), fgw[:, hs, cc, ho:ho + 128],
                                     fxb[:, cc, :], start=(cc == 0), stop=(cc == CC - 1))
                        _li25.then_inc(p.sT.h, 1)
                        t.wait_ge(p.sV.h, b5)
                        for cc in range(CC):
                            _li26 = t.matmul(_bank(p, 5), fuw[:, hs, cc, ho:ho + 128],
                                     fxb[:, cc, :], start=(cc == 0), stop=(cc == CC - 1))
                        _li26.then_inc(p.sT.h, 1)
                    p.sT.n += 2
                    if tsb == NSB - 1 and hh == 1:
                        wslot[2 * hq] = p.sT.n
                    if tsb == NSB - 1 and hh == 3:
                        wslot[2 * hq + 1] = p.sT.n
                    tokG = p.sT.n - 1
                    tokU = p.sT.n

                    @blk.scalar
                    def _(s, tokG=tokG, pgt=p.sV.n):
                        s.wait_ge(p.sT.h, tokG)
                        s.wait_ge(p.sV.h, pgt)
                        s.activation(out=fsg[:, :], in_=_bank(p, 4),
                                     func=AF.Silu).then_inc(p.sS.h, 1)
                    p.sS.n += 1
                    p.bankS[4] = p.sS.n
                    tokSG = p.sS.n
                    evfree = p.evS.get(0, 0)

                    @blk.vector
                    def _(v, tokSG=tokSG, tokU=tokU, evfree=evfree):
                        v.wait_ge(p.sS.h, tokSG)
                        v.wait_ge(p.sT.h, tokU)
                        v.wait_ge(p.cE0.h, evfree)
                        v.tensor_tensor(p.evac[:, 0, :], _bank(p, 5), fsg[:, :],
                                        ALU.mult).then_inc(p.sV.h, 1)
                    p.sV.n += 1
                    p.bankS[5] = p.sV.n
                    tokGT = p.sV.n

                    @blk.sync
                    def _(s, hc=hc, tsl=tsl, tokGT=tokGT):
                        s.wait_ge(p.sV.h, tokGT)
                        s.dma_start(out=p.gated_scr[:, hc, tsl],
                                    in_=p.evac[:, 0, :]).then_inc(p.cE0.h, 16)
                    p.cE0.n += 16
                    p.evS[0] = p.cE0.n
        g_stores = p.cE0.n

    _phase_barrier(p)
    with nc.sbuf_tensor(_u('dwd'), [128, 2, 8, 1024], BF16) as dwd, \
         nc.sbuf_tensor(_u('dgt'), [128, 2, 8, SBW], BF16) as dgt:

        dslot = {}
        it = 0
        for tsb in range(NSB):
            tsl = slice(tsb * SBW, (tsb + 1) * SBW)
            rtoks = []
            for d2h in range(2):
                for hb in range(8):
                    sl2 = it % 2
                    prev_use = dslot.get(it - 2, 0)

                    cwx = p.cW0 if sl2 == 0 else p.cW1
                    cdx = p.cD if sl2 == 0 else p.cD1

                    @blk.gpsimd
                    def _(g, d2h=d2h, hb=hb, sl2=sl2, prev_use=prev_use, cwx=cwx):
                        g.wait_ge(p.sT.h, prev_use)
                        g.dma_start(out=dwd[:, sl2, :, :],
                                    in_=p.w['l1_wdT'][:, hb * 8:(hb + 1) * 8,
                                                      d2h * 1024:(d2h + 1) * 1024]).then_inc(cwx.h, 16)
                    cwx.n += 16
                    tokDW = cwx.n

                    @blk.sync
                    def _(s, hb=hb, tsl=tsl, sl2=sl2, prev_use=prev_use, g_stores=g_stores, cdx=cdx):
                        s.wait_ge(p.cE0.h, g_stores)
                        s.wait_ge(p.sT.h, prev_use)
                        s.dma_start(out=dgt[:, sl2, :, :],
                                    in_=p.gated_scr[:, hb * 8:(hb + 1) * 8, tsl]).then_inc(cdx.h, 16)
                    cdx.n += 16
                    tokGL = cdx.n
                    pof = p.po_free if hb == 0 else 0

                    @blk.tensor
                    def _(t, d2h=d2h, hb=hb, sl2=sl2, tokDW=tokDW, tokGL=tokGL, pof=pof,
                          cwx=cwx, cdx=cdx):
                        t.wait_ge(cwx.h, tokDW)
                        t.wait_ge(cdx.h, tokGL)
                        if pof:
                            t.wait_ge(p.sV.h, pof)
                        for hc in range(8):
                            for d2c in range(8):
                                _li27 = t.matmul(p.ps[:, d2c * 512:d2c * 512 + SBW],
                                         dwd[:, sl2, hc, d2c * 128:(d2c + 1) * 128],
                                         dgt[:, sl2, hc, :],
                                         start=(hb == 0 and hc == 0),
                                         stop=(hb == 7 and hc == 7))
                        _li27.then_inc(p.sT.h, 1)
                    p.sT.n += 1
                    dslot[it] = p.sT.n
                    it += 1
                tok_half = p.sT.n

                @blk.vector
                def _(v, d2h=d2h, tsl=tsl, tok_half=tok_half):
                    v.wait_ge(p.sT.h, tok_half)
                    for d2c in range(8):
                        cc = d2h * 8 + d2c
                        _li28 = v.tensor_tensor(p.xsb[:, cc, tsl], p.xsb[:, cc, tsl],
                                        p.ps[:, d2c * 512:d2c * 512 + SBW], ALU.add)
                    _li28.then_inc(p.sV.h, 1)
                p.sV.n += 1
                p.po_free = p.sV.n
                rtoks.append(p.sV.n)
            _ln_only(p, tsl, rtoks[-1])
            p.po_free = p.x_tok


def kernel(**inputs):
    in_maps = _stage_inputs(inputs)
    nc = build_program(_PHASES)
    res = run_bass_kernel_spmd(nc, in_maps, core_ids=list(range(8)))
    return _assemble(res.results)


# version 13
# speedup vs baseline: 1.0228x; 1.0228x over previous
import math
from contextlib import ExitStack

import numpy as np
import ml_dtypes

import concourse.bass as bass
import concourse.mybir as mybir
from concourse.bass_utils import run_bass_kernel_spmd

F32 = mybir.dt.float32
BF16 = mybir.dt.bfloat16
FP8 = mybir.dt.float8e4
AF = mybir.ActivationFunctionType
ALU = mybir.AluOpType

NPBF16 = ml_dtypes.bfloat16
NPFP8 = ml_dtypes.float8_e4m3

B = 4
S = 4096
D = 2048
H = 16
HD = 128
T = 2048
CC = 16
NSB = 4
SBW = 512
KCW = 128
NKC = 16
DFF = 8192
HC = 64

ATT_SCALE = 1.0 / math.sqrt(HD)
SHIFT = 18.0
LN_EPS = 1e-5

_PHASES = 4
_SUB = None



def _chunkT(w):
    C, M = w.shape
    return np.ascontiguousarray(
        np.asarray(w).reshape(C // 128, 128, M).transpose(1, 0, 2).astype(np.float32))


def _mask_tiles(r):
    p = np.arange(128)[:, None]
    f = np.arange(512)[None, :]
    tiles = np.zeros((16, 128, 512), dtype=NPFP8)
    for pi in (0, 1):
        e = -1 if (r == 0 and pi == 1) else 0
        ep = 0 if (r == 0 and pi == 1) else 1
        for d in range(4):
            tiles[pi * 4 + d] = (p - f + 128 * d <= e).astype(NPFP8)
            tiles[8 + pi * 4 + d] = (p - f + 128 * d >= ep).astype(NPFP8)
    return tiles


def _stage_inputs(inputs):
    g = {}
    g['l0_qwT'] = _chunkT(np.asarray(inputs['L0_attn_q_w']).T)
    g['l0_kwT'] = _chunkT(np.asarray(inputs['L0_attn_k_w']).T)
    g['l0_vwT'] = _chunkT(np.asarray(inputs['L0_attn_v_w']).T)
    g['l0_owT'] = _chunkT(np.asarray(inputs['L0_attn_o_w']).T)
    g['l0_wgT'] = _chunkT(np.asarray(inputs['L0_ffn_W_gate']).T)
    wqkv = np.asarray(inputs['L0_ffn_W_qkv'])
    g['l0_qT'] = _chunkT(wqkv[0:D].T)
    g['l0_kvT'] = _chunkT(wqkv[D:3 * D].T)
    g['l1_wq'] = _chunkT(np.asarray(inputs['L1_attn_Wq']))
    g['l1_wk'] = _chunkT(np.asarray(inputs['L1_attn_Wk']))
    g['l1_wv'] = _chunkT(np.asarray(inputs['L1_attn_Wv']))
    g['l1_wo'] = _chunkT(np.asarray(inputs['L1_attn_Wo']))
    g['l1_wgT'] = _chunkT(np.asarray(inputs['L1_ffn_w_gate']).T)
    g['l1_wuT'] = _chunkT(np.asarray(inputs['L1_ffn_w_up']).T)
    g['l1_wdT'] = _chunkT(np.asarray(inputs['L1_ffn_w_down']).T)

    x0 = np.asarray(inputs['L0_attn_x'], dtype=np.float32)
    ident = np.eye(128, dtype=NPBF16)
    ones_col = np.ones((128, 1), dtype=np.float32)
    ones_sq = np.ones((128, 128), dtype=np.float32)
    ones_row = np.ones((1, 128), dtype=np.float32)
    ones_colb = np.ones((128, 1), dtype=NPBF16)
    ones_sqb = np.ones((128, 128), dtype=NPBF16)

    in_maps = []
    for c in range(8):
        b, r = c // 2, c % 2
        m = dict(g)
        xown = np.ascontiguousarray(x0[b, r::2].T)
        m['x0own'] = np.ascontiguousarray(
            xown.reshape(CC, 128, T).transpose(1, 0, 2))
        m['masks'] = np.ascontiguousarray(_mask_tiles(r).transpose(1, 0, 2))
        edge = np.zeros((128, 4), dtype=np.float32)
        edge[:, 0] = 1.0 if r == 0 else 0.0
        edge[:, 1] = 0.0 if r == 0 else 1.0 / S
        m['edge'] = edge
        patch = np.zeros((128, 1), dtype=np.float32)
        if r == 1:
            patch[:, 0] = 1.0
        m['patch'] = patch
        m['ident'] = ident
        m['ones_col'] = ones_col
        m['ones_sq'] = ones_sq
        m['ones_row'] = ones_row
        m['ones_colb'] = ones_colb
        m['ones_sqb'] = ones_sqb
        in_maps.append(m)
    return in_maps


def _assemble(results):
    out = np.empty((B, S, D), dtype=np.float32)
    for c in range(8):
        b, r = c // 2, c % 2
        o = np.asarray(results[c]['out_fm'])
        out[b, r::2, :] = o.transpose(1, 0, 2).reshape(D, T).T
    return out



_UN = [0]


def _u(name):
    _UN[0] += 1
    return "%s_%d" % (name, _UN[0])


class P:
    pass


def _diag_d(sb, j):
    d = j - 4 * sb
    return d if 0 <= d <= 3 else None


def _causal_chunks(sb):
    return [(pi, j) for pi in (0, 1) for j in range(4 * sb + 4)]


def _anti_chunks(sb):
    return [(pi, j) for pi in (0, 1) for j in range(4 * sb, NKC)]


def _full_chunks(sb):
    return [(pi, j) for pi in (0, 1) for j in range(NKC)]


def _bank(p, i, width=SBW):
    return p.ps[:, i * 512:i * 512 + width]


def build_program(phases=4):
    nc = bass.Bass('TRN2', target_bir_lowering=False)
    p = P()
    p.nc = nc

    dI = lambda name, shape, dt=F32: nc.dram_tensor(name, shape, dt, kind='ExternalInput').ap()
    dX = lambda name, shape, dt=BF16: nc.dram_tensor(name, shape, dt).ap()

    p.x0own = dI('x0own', [128, CC, T])
    p.w = {}
    for k, shape in {
        'l0_qwT': (128, CC, D), 'l0_kwT': (128, CC, HD), 'l0_vwT': (128, CC, HD),
        'l0_owT': (128, CC, D), 'l0_wgT': (128, CC, H), 'l0_qT': (128, CC, D),
        'l0_kvT': (128, CC, 2 * D), 'l1_wq': (128, CC, D), 'l1_wk': (128, CC, D),
        'l1_wv': (128, CC, D), 'l1_wo': (128, CC, D), 'l1_wgT': (128, CC, DFF),
        'l1_wuT': (128, CC, DFF), 'l1_wdT': (128, HC, D),
    }.items():
        p.w[k] = dI(k, list(shape))
    p.masks_in = dI('masks', [128, 16, SBW], FP8)
    p.edge_in = dI('edge', [128, 4])
    p.patch_in = dI('patch', [128, 1])
    p.ident_in = dI('ident', [128, 128], BF16)
    p.onesc_in = dI('ones_col', [128, 1])
    p.onessq_in = dI('ones_sq', [128, 128])
    p.onesr_in = dI('ones_row', [1, 128])
    p.onescb_in = dI('ones_colb', [128, 1], BF16)
    p.onesqb_in = dI('ones_sqb', [128, 128], BF16)
    p.out_fm = nc.dram_tensor('out_fm', [128, CC, T], F32, kind='ExternalOutput').ap()

    p.q_scr = dX('q_scr', [H, 128, T])
    p.k_loc = dX('k_loc', [2, H, 128, T])
    p.v_loc = dX('v_loc', [2, 128, NKC, D])
    p.gated_scr = dX('gated_scr', [128, HC, T])
    p.agki = dX('agki', [H, 128, T])
    p.agko = dX('agko', [2, H, 128, T])
    p.agvi = dX('agvi', [128, NKC, D])
    p.agvo = dX('agvo', [2, 128, NKC, D])
    p.ag_kv_in = dX('ag_kv_in', [2, 128, NKC, HD])
    p.ag_kv_out = dX('ag_kv_out', [2, 2, 128, NKC, HD])
    p.ag_x_in = dX('ag_x_in', [128, CC, T])
    p.ag_x_out = dX('ag_x_out', [2, 128, CC, T])
    p.RG = [[0, 1], [2, 3], [4, 5], [6, 7]]

    with ExitStack() as es:
        p.es = es
        sb_t = lambda name, shape, dt: es.enter_context(nc.sbuf_tensor(name, shape, dt))
        p.xsb = sb_t('xsb', [128, CC, T], F32)
        p.consts = sb_t('consts', [128, 8], F32)
        p.onesc = sb_t('onesc', [128, 1], F32)
        p.onessq = sb_t('onessq', [128, 128], F32)
        p.onesr = sb_t('onesr', [1, 128], F32)
        p.onescb = sb_t('onescb', [128, 1], BF16)
        p.onesqb = sb_t('onesqb', [128, 128], BF16)
        p.ident = sb_t('ident_sb', [128, 128], BF16)
        p.edge = sb_t('edge_sb', [128, 4], F32)
        p.patch = sb_t('patch_sb', [128, 1], F32)
        p.masks = sb_t('masks_sb', [128, 8, SBW], FP8)
        p.lnm = sb_t('lnm', [1, SBW], F32)
        p.lnt1 = sb_t('lnt1', [1, SBW], F32)
        p.lnt2 = sb_t('lnt2', [1, SBW], F32)
        p.lns = sb_t('lns', [1, SBW], F32)
        p.sbst = sb_t('sbst', [128, 2, SBW], F32)
        p.sqbuf = sb_t('sqbuf', [128, 1, SBW], F32)
        p.evac = sb_t('evac', [128, 3, SBW], BF16)
        p.sumv = sb_t('sumv', [128, H], F32)
        p.sumvb = sb_t('sumvb', [128, H], F32)

        class Ctr:
            def __init__(self, name):
                self.h = es.enter_context(nc.semaphore(name))
                self.n = 0
        p.cS = Ctr('cS')
        p.cG = Ctr('cG')
        p.cM = Ctr('cM')
        p.cW0 = Ctr('cW0')
        p.cW1 = Ctr('cW1')
        p.cE0 = Ctr('cE0')
        p.cE1 = Ctr('cE1')
        p.cE2 = Ctr('cE2')
        p.cKV = Ctr('cKV')
        p.cKV1 = Ctr('cKV1')
        p.cQ0 = Ctr('cQ0')
        p.cQ1 = Ctr('cQ1')
        p.cX = Ctr('cX')
        p.cD = Ctr('cD')
        p.cD1 = Ctr('cD1')
        p.cA = Ctr('cA')
        p.dma_ctrs = [p.cS, p.cG, p.cM, p.cW0, p.cW1, p.cE0, p.cE1, p.cE2,
                      p.cKV, p.cKV1, p.cQ0, p.cQ1, p.cX, p.cD, p.cD1, p.cA]
        p.sT = Ctr('sT')
        p.sS = Ctr('sS')
        p.sV = Ctr('sV')
        p.sV2 = Ctr('sV2')
        p.sS2 = Ctr('sS2')
        p.sCC = Ctr('sCC')

        p.blk = es.enter_context(nc.Block())
        p.ps = es.enter_context(nc.psum_tensor('arena', [128, 4096], F32))
        p.x_tok = 0
        p.po_free = 0
        p.bankS = {}
        p.evS = {}

        _prologue(p)
        if phases >= 1:
            _l0_attn(p)
        if phases >= 2:
            _l0_ffn(p)
        if phases >= 3:
            _l1_attn(p)
        if phases >= 4:
            _l1_ffn(p)
        _epilogue(p)
    return nc


def _bank_bf(p, i, cols=128):
    return p.ps[:, i * 512:i * 512 + cols // 2].bitcast(BF16)


def _prologue(p):
    nc, blk = p.nc, p.blk

    @blk.sync
    def _(s):
        for dst, src in ((p.edge, p.edge_in), (p.patch, p.patch_in),
                         (p.ident, p.ident_in), (p.onesc, p.onesc_in),
                         (p.onessq, p.onessq_in), (p.onesr, p.onesr_in),
                         (p.onescb, p.onescb_in), (p.onesqb, p.onesqb_in)):
            s.dma_start(out=dst[:, :], in_=src[:, :]).then_inc(p.cS.h, 16)
            p.cS.n += 16
        s.dma_start(out=p.xsb[:, :, :], in_=p.x0own[:, :, :]).then_inc(p.cS.h, 16)
        p.cS.n += 16

    vals = [-SHIFT, ATT_SCALE, LN_EPS, -0.5, 2.0]

    @blk.vector
    def _(v):
        for i, val in enumerate(vals):
            _li1 = v.memset(p.consts[:, i:i + 1], float(val))
        _li1.then_inc(p.sV.h, 1)
    p.sV.n += 1
    for i, val in enumerate(vals):
        nc.const_aps.register(F32, float(val), p.consts[:, i:i + 1])

    tok = p.cS.n
    tokv = p.sV.n
    for eng in ('tensor', 'scalar', 'vector', 'gpsimd', 'sync'):
        def _w(e, tok=tok, tokv=tokv):
            e.wait_ge(p.cS.h, tok)
            e.wait_ge(p.sV.h, tokv)
        getattr(blk, eng)(_w)


def _epilogue(p):
    blk = p.blk
    tokV, tokT, tokS = p.sV.n, p.sT.n, p.sS.n

    @blk.sync
    def _(s):
        s.wait_ge(p.sV.h, tokV)
        s.wait_ge(p.sT.h, tokT)
        s.wait_ge(p.sS.h, tokS)
        s.dma_start(out=p.out_fm[:, :, :], in_=p.xsb[:, :, :]).then_inc(p.cS.h, 16)
        p.cS.n += 16
        s.wait_ge(p.cS.h, p.cS.n)


def _load_masks(p, base, wait_v):
    @p.blk.sync
    def _(s):
        s.wait_ge(p.sV.h, wait_v)
        s.dma_start(out=p.masks[:, :, :], in_=p.masks_in[:, base:base + 8, :]).then_inc(p.cM.h, 16)
    p.cM.n += 16
    return p.cM.n



def _phase_barrier(p):
    blk = p.blk
    toks = [(c.h, c.n) for c in p.dma_ctrs if c.n]
    for eng in ('tensor', 'scalar', 'vector', 'gpsimd', 'sync'):
        def _w(e, toks=toks):
            for h, n in toks:
                e.wait_ge(h, n)
        getattr(blk, eng)(_w)



def _proj_to_scratch(p, wkey, dst, nheads, col0=0):
    nc, blk = p.nc, p.blk
    _phase_barrier(p)
    with nc.sbuf_tensor(_u('pw2'), [128, 2, CC, 256], BF16) as pw2, \
         nc.sbuf_tensor(_u('pxt'), [128, 2, CC, SBW], BF16) as pxt:
        win = p.w[wkey]
        wuse = {}
        xuse = {}
        ngr = (nheads + 1) // 2
        for hg in range(ngr):
            nh = min(2, nheads - hg * 2)
            ws = hg % 2
            cw = p.cW0 if ws == 0 else p.cW1
            prev = wuse.get(hg - 2, 0)

            @blk.gpsimd
            def _(g, hg=hg, nh=nh, ws=ws, prev=prev, cw=cw):
                g.wait_ge(p.sT.h, prev)
                g.dma_start(out=pw2[:, ws, :, 0:nh * 128],
                            in_=win[:, :, col0 + hg * 256:col0 + hg * 256 + nh * 128]).then_inc(cw.h, 16)
            cw.n += 16
            tokW = cw.n

            for tsb in range(NSB):
                it = hg * NSB + tsb
                xs = it % 2
                xprev = xuse.get(it - 2, 0)
                xt = p.x_tok

                @blk.vector
                def _(v, tsb=tsb, xs=xs, xprev=xprev, xt=xt):
                    v.wait_ge(p.sV.h, xt)
                    v.wait_ge(p.sT.h, xprev)
                    v.tensor_copy(pxt[:, xs, :, :],
                                  p.xsb[:, :, tsb * SBW:(tsb + 1) * SBW]).then_inc(p.sV2.h, 1)
                p.sV2.n += 1
                tokX = p.sV2.n

                for hh in range(nh):
                    h = hg * 2 + hh
                    bk = 4 + (hh % 2)
                    ev = hh % 2
                    bs = p.bankS.get(bk, 0)

                    @blk.tensor
                    def _(t, hh=hh, ws=ws, xs=xs, bk=bk, tokW=tokW, tokX=tokX,
                          bs=bs, cw=cw):
                        t.wait_ge(cw.h, tokW)
                        t.wait_ge(p.sV2.h, tokX)
                        t.wait_ge(p.sS.h, bs)
                        for cc in range(CC):
                            _lip = t.matmul(_bank(p, bk), pw2[:, ws, cc, hh * 128:(hh + 1) * 128],
                                     pxt[:, xs, cc, :], start=(cc == 0), stop=(cc == CC - 1))
                        _lip.then_inc(p.sT.h, 1)
                    p.sT.n += 1
                    tokMM = p.sT.n
                    evfree = p.evS.get(ev, 0)
                    ce = p.cE0 if ev == 0 else p.cE1

                    @blk.scalar
                    def _(s, ev=ev, bk=bk, tokMM=tokMM, evfree=evfree, ce=ce):
                        s.wait_ge(p.sT.h, tokMM)
                        s.wait_ge(ce.h, evfree)
                        s.copy(out=p.evac[:, ev, :], in_=_bank(p, bk)).then_inc(p.sS.h, 1)
                    p.sS.n += 1
                    p.bankS[bk] = p.sS.n
                    tokEv = p.sS.n

                    @blk.sync
                    def _(s, h=h, tsb=tsb, ev=ev, tokEv=tokEv, ce=ce):
                        s.wait_ge(p.sS.h, tokEv)
                        s.dma_start(out=dst[h, :, tsb * SBW:(tsb + 1) * SBW],
                                    in_=p.evac[:, ev, :]).then_inc(ce.h, 16)
                    ce.n += 16
                    p.evS[ev] = ce.n
                xuse[it] = p.sT.n
            wuse[hg] = p.sT.n
    return None



def _tok_score(T0, i, k_):
    if i == 0:
        return T0 + 1
    return T0 + (i + 1) + (i - 1) * (k_ - 1)


def _tok_den(T0, cv, n, k_):
    if cv >= n - 1:
        return T0 + n * k_
    return T0 + (cv + 2) + (cv + 1) * (k_ - 1)


def _attention(p, chunks_of, mask_base, kload, vload, do_oproj, ow_key,
               l1_edge=False, kv_shared=False, kv_wait=None):
    nc, blk = p.nc, p.blk
    _phase_barrier(p)
    with nc.sbuf_tensor(_u('akb'), [128, 2, NKC, HD], BF16) as akb, \
         nc.sbuf_tensor(_u('avb'), [128, 2, NKC, HD], BF16) as avb, \
         nc.sbuf_tensor(_u('aqb'), [128, 2, SBW], BF16) as aqb, \
         nc.sbuf_tensor(_u('apt'), [128, 4, SBW], BF16) as apt, \
         nc.sbuf_tensor(_u('aden'), [128, SBW], F32) as aden, \
         nc.sbuf_tensor(_u('aosb'), [128, H, SBW], BF16) as aosb, \
         nc.sbuf_tensor(_u('aow'), [128, 2, 1024], BF16) as aow:

        ufree = {}
        owfree = {}
        bpl = {}
        p0done = {}
        kvspan = {}
        for sb in range(NSB):
            chunks = chunks_of(sb)
            n = len(chunks)
            sv = l1_edge and sb == 0
            sbsl = slice(sb * SBW, (sb + 1) * SBW)

            for h in range(H):
                u = sb * H + h
                qub = h % 2
                ft = ufree.get(u - 2, 0)
                ftk = ufree.get(u - 1, 0)
                ftk0 = p0done.get(u - 1, ftk)
                loads = []
                if kv_shared:
                    if h == 0:
                        for pi in (0, 1):
                            js = sorted(j for (pj, j) in chunks if pj == pi)
                            j1 = js[-1] + 1
                            d0 = kvspan.get(pi, 0)
                            if j1 > d0:
                                loads.append((pi, d0, j1, 0))
                                kvspan[pi] = j1
                else:
                    for pi in (0, 1):
                        js = sorted(j for (pj, j) in chunks if pj == pi)
                        loads.append((pi, js[0], js[-1] + 1,
                                      ftk0 if pi == 0 else ftk))

                ltk0 = [0]

                @blk.sync
                def _(s, h=h, qub=qub, sb=sb, ft=ft, loads=loads, sbsl=sbsl):
                    if ft:
                        s.wait_ge(p.sT.h, ft)
                    cq = p.cQ0 if qub == 0 else p.cQ1
                    s.dma_start(out=aqb[:, qub, :], in_=p.q_scr[h, :, sbsl]).then_inc(cq.h, 16)
                    cq.n += 16
                    if loads and kv_wait is not None:
                        for _kwh, _kwv in kv_wait:
                            s.wait_ge(_kwh, _kwv)
                    hh = 0 if kv_shared else h
                    for pi, j0, j1, wtk in loads:
                        ck = p.cKV if pi == 0 else p.cKV1
                        if wtk:
                            s.wait_ge(p.sT.h, wtk)
                        kload(s, akb[:, pi, j0:j1, :], pi, j0, j1, hh)
                        ck.n += 16
                        vload(s, avb[:, pi, j0:j1, :], pi, j0, j1, hh)
                        ck.n += 16
                        if pi == 0:
                            ltk0[0] = ck.n
                if any(pi == 0 for pi, _a, _b, _c in loads):
                    p.last_ltk0 = p.cKV.n
                ltk0[0] = p.cKV.n if not kv_shared else p.last_ltk0
                ltk = p.cKV1.n
                ltq = (p.cQ0 if qub == 0 else p.cQ1).n

                T0, S0 = p.sT.n, p.sS.n
                sv2b = p.sV2.n
                xt_guard = 0
                if h == 0:
                    xt_guard = p.po_free if (do_oproj and sb > 0) else p.x_tok
                pof = p.po_free if (h == 0 and do_oproj) else 0
                k_ = 4 if sv else 3
                bp = 0 if (sv or u % 2 == 0) else 1
                bav, bdn = (6, 7) if bp == 0 else (4, 5)
                diag = [(_diag_d(sb, j) if mask_base is not None else None)
                        for (_pi, j) in chunks]
                ndth = []
                _c = 0
                for _dd in diag:
                    if _dd is not None:
                        _c += 1
                    ndth.append(_c)
                bpw = bpl.get(bp)

                cq = p.cQ0 if qub == 0 else p.cQ1

                @blk.tensor
                def _(t, sb=sb, h=h, qub=qub, ltk=ltk, ltk0v=ltk0[0], ltq=ltq,
                      cq=cq, chunks=chunks, S0=S0, sv=sv, n=n, diag=diag,
                      ndth=ndth, sv2b=sv2b, bav=bav, bdn=bdn, bpw=bpw,
                      xt_guard=xt_guard, pof=pof):
                    t.wait_ge(p.cKV.h, ltk0v)
                    t.wait_ge(cq.h, ltq)
                    if xt_guard:
                        t.wait_ge(p.sV.h, xt_guard)
                    if pof:
                        t.wait_ge(p.sV.h, pof)

                    def consume(cv, last):
                        pv, jv = chunks[cv]
                        if cv == 0 and bpw is not None:
                            t.wait_ge(bpw[0], bpw[1])
                        if diag[cv] is not None:
                            t.wait_ge(p.sV2.h, sv2b + ndth[cv])
                        else:
                            t.wait_ge(p.sS.h, S0 + cv + 1)
                        t.matmul(_bank(p, bav), avb[:, pv, jv, :],
                                 apt[:, cv % 4, :],
                                 start=(cv == 0), stop=last).then_inc(p.sT.h, 1)
                        if sv:
                            t.matmul(p.ps[:, 2048 + h:2049 + h],
                                     avb[:, pv, jv, :], p.onescb[:, :],
                                     start=(cv == 0), stop=last).then_inc(p.sT.h, 1)
                        t.matmul(p.ps[:, bdn * 512:bdn * 512 + SBW],
                                 p.onesqb[:, :], apt[:, cv % 4, :],
                                 start=(cv == 0), stop=last).then_inc(p.sT.h, 1)
                    first_p1 = next((ii for ii, (pii, _) in enumerate(chunks) if pii == 1), None)
                    for i, (pi, j) in enumerate(chunks):
                        if i == first_p1:
                            t.wait_ge(p.cKV1.h, ltk)
                        if i >= 4:
                            t.wait_ge(p.sS.h, S0 + i - 3)
                        t.matmul(_bank(p, i % 4),
                                 akb[:, pi, j, :], aqb[:, qub, :],
                                 start=True, stop=True).then_inc(p.sT.h, 1)
                        if i >= 1:
                            consume(i - 1, False)
                    consume(n - 1, True)
                p.sT.n += n * k_
                tok_den = p.sT.n
                ufree[u] = tok_den
                npi0 = sum(1 for (_pi, _j) in chunks if _pi == 0)
                p0done[u] = _tok_den(T0, npi0 - 1, n, k_) if npi0 < n else tok_den

                @blk.scalar
                def _(s, T0=T0, n=n, k_=k_):
                    for i in range(n):
                        s.wait_ge(p.sT.h, _tok_score(T0, i, k_))
                        if i >= 4:
                            s.wait_ge(p.sT.h, _tok_den(T0, i - 4, n, k_))
                        s.activation(out=apt[:, i % 4, :], in_=_bank(p, i % 4),
                                     func=AF.Exp, bias=-SHIFT, scale=ATT_SCALE).then_inc(p.sS.h, 1)
                p.sS.n += n

                @blk.vector
                def _(v, chunks=chunks, S0=S0, sb=sb, h=h, diag=diag,
                      tok_den=tok_den, n=n, sv2base=sv2b, bav=bav, bdn=bdn,
                      nvt=p.sV.n + 1, nvb=p.sV.n):
                    sv2 = sv2base
                    for i, (pi, j) in enumerate(chunks):
                        if diag[i] is not None:
                            v.wait_ge(p.sS.h, S0 + i + 1)
                            v.tensor_mul(apt[:, i % 4, :], apt[:, i % 4, :],
                                         p.masks[:, pi * 4 + diag[i], :]).then_inc(p.sV2.h, 1)
                            sv2 += 1
                    v.wait_ge(p.sT.h, tok_den)
                    if l1_edge and sb == 3:
                        v.tensor_tensor(p.ps[:, bdn * 512 + SBW - 1:bdn * 512 + SBW],
                                        p.ps[:, bdn * 512 + SBW - 1:bdn * 512 + SBW],
                                        p.patch[:, :], ALU.add).then_inc(p.sV2.h, 1)
                        sv2 += 1
                        v.wait_ge(p.sV2.h, sv2)
                    v.wait_ge(p.sV.h, nvb)
                    v.reciprocal(aden[:, :], p.ps[:, bdn * 512:bdn * 512 + SBW]).then_inc(p.sV2.h, 1)
                    sv2 += 1
                    v.wait_ge(p.sV2.h, sv2)
                    v.tensor_tensor(aosb[:, h, :], _bank(p, bav), aden[:, :],
                                    ALU.mult).then_inc(p.sV.h, 1)
                    if l1_edge and sb == 3:
                        v.wait_ge(p.sV.h, nvt)
                        v.scalar_tensor_tensor(
                            aosb[:, h, SBW - 1:SBW], aosb[:, h, SBW - 1:SBW],
                            p.edge[:, 0:1], p.sumvb[:, h:h + 1],
                            ALU.mult, ALU.add).then_inc(p.sV.h, 1)
                ndiag = sum(1 for _dd in diag if _dd is not None)
                p.sV2.n += ndiag + (1 if (l1_edge and sb == 3) else 0) + 1
                p.sV.n += 1 + (1 if (l1_edge and sb == 3) else 0)
                bpl[bp] = (p.sV.h, p.sV.n)
            ev_all = p.sV.n

            if sv:
                tokT = p.sT.n

                @blk.vector
                def _(v, tokT=tokT, sv2base=p.sV2.n):
                    v.wait_ge(p.sT.h, tokT)
                    v.tensor_copy(p.sumv[:, :], p.ps[:, 2048:2048 + H]).then_inc(p.sV2.h, 1)
                    v.wait_ge(p.sV2.h, sv2base + 1)
                    v.scalar_tensor_tensor(p.sumvb[:, :], p.sumv[:, :],
                                           p.edge[:, 1:2], p.sumv[:, :],
                                           ALU.mult, ALU.bypass).then_inc(p.sV.h, 1)
                p.sV2.n += 1
                p.sV.n += 1
                bpl[1] = (p.sV.h, p.sV.n)
                ev_all = p.sV.n

            if do_oproj:
                win = p.w[ow_key]
                rtoks = []
                for half in (0, 1):
                    for cc in range(CC):
                        ccb = cc % 2
                        ft = owfree.get(ccb, 0)

                        cw = p.cW0 if ccb == 0 else p.cW1

                        @blk.gpsimd
                        def _(g, cc=cc, half=half, ccb=ccb, ft=ft, cw=cw):
                            if ft:
                                g.wait_ge(p.sT.h, ft)
                            g.dma_start(out=aow[:, ccb, :],
                                        in_=win[:, cc, half * 1024:(half + 1) * 1024]).then_inc(cw.h, 16)
                        cw.n += 16
                        tokW = cw.n
                        rt0 = rtoks[0] if rtoks else 0

                        @blk.tensor
                        def _(t, cc=cc, half=half, ccb=ccb, tokW=tokW,
                              ev_all=ev_all, rt0=rt0, cw=cw):
                            t.wait_ge(cw.h, tokW)
                            if cc == 0:
                                if half == 0:
                                    t.wait_ge(p.sV.h, ev_all)
                                    t.wait_ge(p.sV.h, p.po_free)
                                else:
                                    t.wait_ge(p.sV.h, rt0)
                            for d2c in range(8):
                                _li3 = t.matmul(p.ps[:, d2c * 512:d2c * 512 + SBW],
                                         aow[:, ccb, d2c * 128:(d2c + 1) * 128],
                                         aosb[:, cc, :],
                                         start=(cc == 0), stop=(cc == CC - 1))
                            _li3.then_inc(p.sT.h, 1)
                        p.sT.n += 1
                        owfree[ccb] = p.sT.n
                    tok_half = p.sT.n

                    @blk.vector
                    def _(v, half=half, sbsl=sbsl, tok_half=tok_half):
                        v.wait_ge(p.sT.h, tok_half)
                        for d2c in range(8):
                            cc2 = half * 8 + d2c
                            _li4 = v.tensor_tensor(p.xsb[:, cc2, sbsl], p.xsb[:, cc2, sbsl],
                                            p.ps[:, d2c * 512:d2c * 512 + SBW], ALU.add)
                        _li4.then_inc(p.sV.h, 1)
                    p.sV.n += 1
                    rtoks.append(p.sV.n)
                p.po_free = rtoks[-1]
                _ln_only(p, sbsl, rtoks[-1])
                bpl[0] = (p.sV2.h, p.ln_b67)
                bpl[1] = (p.sV2.h, p.ln_b45)
            else:
                _res_ln(p, sb, from_po=False, y_sb=aosb, ytok=ev_all)



def _res_ln(p, sb, from_po, y_sb, ytok):
    assert not from_po
    nc, blk = p.nc, p.blk
    sbsl = slice(sb * SBW, (sb + 1) * SBW)

    @blk.vector
    def _(v, sb=sb, ytok=ytok):
        if from_po:
            v.wait_ge(p.sT.h, ytok)
            for cc in range(CC):
                _li5 = v.tensor_tensor(p.xsb[:, cc, sbsl], p.xsb[:, cc, sbsl],
                                p.ps[:, cc * 128:(cc + 1) * 128], ALU.add)
        else:
            v.wait_ge(p.sV.h, ytok)
            for cc in range(CC):
                _li5 = v.tensor_tensor(p.xsb[:, cc, sbsl], p.xsb[:, cc, sbsl],
                                y_sb[:, cc, :], ALU.add)
        _li5.then_inc(p.sV.h, 1)
    p.sV.n += 1
    tok_r = p.sV.n
    if from_po:
        p.po_free = tok_r
    _ln_only(p, sbsl, tok_r)


def _ln_only(p, sbsl, tok_r):
    nc, blk = p.nc, p.blk
    S0 = p.sS.n
    T0sq = p.sT.n + 1

    @blk.scalar
    def _(s, tok_r=tok_r, T0sq=T0sq):
        s.wait_ge(p.sV.h, tok_r)
        for cc in range(CC):
            if cc >= 1:
                s.wait_ge(p.sT.h, T0sq + cc)
            s.activation(out=p.sqbuf[:, 0, :], in_=p.xsb[:, cc, sbsl],
                         func=AF.Square).then_inc(p.sS.h, 1)
    p.sS.n += CC

    @blk.tensor
    def _(t, tok_r=tok_r, S0=S0):
        t.wait_ge(p.sV.h, tok_r)
        for cc in range(CC):
            _li6 = t.matmul(p.ps[0:1, 2048:2048 + SBW], p.onesc[:, :], p.xsb[:, cc, sbsl],
                     start=(cc == 0), stop=(cc == CC - 1))
        _li6.then_inc(p.sT.h, 1)
        for cc in range(CC):
            t.wait_ge(p.sS.h, S0 + cc + 1)
            t.matmul(p.ps[0:1, 2560:2560 + SBW], p.onesc[:, :], p.sqbuf[:, 0, :],
                     start=(cc == 0), stop=(cc == CC - 1)).then_inc(p.sT.h, 1)
    p.sT.n += 1 + CC
    tok_stats = p.sT.n

    p.ln_b45 = p.sV2.n + 2

    @blk.vector
    def _(v, tok_stats=tok_stats, sv2b=p.sV2.n):
        v.wait_ge(p.sT.h, tok_stats)
        v.tensor_scalar_mul(p.lnm[0:1, :], p.ps[0:1, 2048:2048 + SBW], 1.0 / D).then_inc(p.sV2.h, 1)
        v.tensor_scalar_mul(p.lnt1[0:1, :], p.ps[0:1, 2560:2560 + SBW], 1.0 / D).then_inc(p.sV2.h, 1)
        v.wait_ge(p.sV2.h, sv2b + 2)
        v.tensor_mul(p.lnt2[0:1, :], p.lnm[0:1, :], p.lnm[0:1, :]).then_inc(p.sV2.h, 1)
        v.wait_ge(p.sV2.h, sv2b + 3)
        v.tensor_sub(p.lnt1[0:1, :], p.lnt1[0:1, :], p.lnt2[0:1, :]).then_inc(p.sV2.h, 1)
        v.wait_ge(p.sV2.h, sv2b + 4)
        v.tensor_scalar_add(p.lnt1[0:1, :], p.lnt1[0:1, :], LN_EPS).then_inc(p.sV.h, 1)
    p.sV2.n += 4
    p.sV.n += 1
    tok_var = p.sV.n

    @blk.scalar
    def _(s, tok_var=tok_var, ss2b=p.sS2.n):
        s.wait_ge(p.sV.h, tok_var)
        s.activation(out=p.lnt2[0:1, :], in_=p.lnt1[0:1, :], func=AF.Ln).then_inc(p.sS2.h, 1)
        s.wait_ge(p.sS2.h, ss2b + 1)
        s.activation(out=p.lns[0:1, :], in_=p.lnt2[0:1, :], func=AF.Exp,
                     scale=-0.5).then_inc(p.sS.h, 1)
    p.sS2.n += 1
    p.sS.n += 1
    tok_s0 = p.sS.n

    @blk.vector
    def _(v, tok_s0=tok_s0, sv2b=p.sV2.n):
        v.wait_ge(p.sS.h, tok_s0)
        v.tensor_mul(p.lnt2[0:1, :], p.lns[0:1, :], p.lns[0:1, :]).then_inc(p.sV2.h, 1)
        v.wait_ge(p.sV2.h, sv2b + 1)
        v.tensor_mul(p.lnt2[0:1, :], p.lnt2[0:1, :], p.lnt1[0:1, :]).then_inc(p.sV2.h, 1)
        v.wait_ge(p.sV2.h, sv2b + 2)
        v.tensor_scalar(p.lnt2[0:1, :], p.lnt2[0:1, :], -0.5, 1.5, ALU.mult, ALU.add).then_inc(p.sV2.h, 1)
        v.wait_ge(p.sV2.h, sv2b + 3)
        v.tensor_mul(p.lns[0:1, :], p.lns[0:1, :], p.lnt2[0:1, :]).then_inc(p.sV2.h, 1)
        v.wait_ge(p.sV2.h, sv2b + 4)
        v.tensor_mul(p.lnm[0:1, :], p.lnm[0:1, :], p.lns[0:1, :]).then_inc(p.sV.h, 1)
    p.sV2.n += 4
    p.sV.n += 1
    tok_s = p.sV.n

    @blk.tensor
    def _(t, tok_s=tok_s):
        t.wait_ge(p.sV.h, tok_s)
        t.matmul(p.ps[:, 3072:3072 + SBW], p.onesr[:, :], p.lns[0:1, :], start=True, stop=True)
        t.matmul(p.ps[:, 3584:3584 + SBW], p.onesr[:, :], p.lnm[0:1, :], start=True, stop=True).then_inc(p.sT.h, 1)
    p.sT.n += 1
    tok_bc = p.sT.n

    p.ln_b67 = p.sV2.n + 1

    @blk.vector
    def _(v, tok_bc=tok_bc, sv2b=p.sV2.n):
        v.wait_ge(p.sT.h, tok_bc)
        v.tensor_copy(p.sbst[:, 0, :], p.ps[:, 3072:3072 + SBW])
        v.tensor_copy(p.sbst[:, 1, :], p.ps[:, 3584:3584 + SBW]).then_inc(p.sV2.h, 1)
        v.wait_ge(p.sV2.h, sv2b + 1)
        for cc in range(CC):
            _li10 = v.tensor_mul(p.xsb[:, cc, sbsl], p.xsb[:, cc, sbsl], p.sbst[:, 0, :])
        _li10.then_inc(p.sV2.h, 1)
        v.wait_ge(p.sV2.h, sv2b + 2)
        for cc in range(CC):
            _li11 = v.tensor_sub(p.xsb[:, cc, sbsl], p.xsb[:, cc, sbsl], p.sbst[:, 1, :])
        _li11.then_inc(p.sV.h, 1)
    p.sV2.n += 2
    p.sV.n += 1
    p.x_tok = p.sV.n



def _l0_attn(p):
    nc, blk = p.nc, p.blk
    _phase_barrier(p)

    with nc.sbuf_tensor(_u('kvw'), [128, CC, 2 * HD], BF16) as kvw, \
         nc.sbuf_tensor(_u('kvx'), [128, 2, CC, SBW], BF16) as kvx:

        @blk.gpsimd
        def _(g):
            g.dma_start(out=kvw[:, :, 0:HD], in_=p.w['l0_kwT'][:, :, :]).then_inc(p.cG.h, 16)
            g.dma_start(out=kvw[:, :, HD:2 * HD], in_=p.w['l0_vwT'][:, :, :]).then_inc(p.cG.h, 16)
        p.cG.n += 32
        tokW = p.cG.n

        kst = {}
        xslot = {}
        x_pre = p.sT.n
        for tsb in range(NSB):
            sl = slice(tsb * SBW, (tsb + 1) * SBW)
            tb = tsb % 2
            tokT_prev = xslot.get(tsb - 2, x_pre)

            @blk.vector
            def _(v, sl=sl, tb=tb, tokT_prev=tokT_prev):
                v.wait_ge(p.sT.h, tokT_prev)
                v.tensor_copy(kvx[:, tb, :, :], p.xsb[:, :, sl]).then_inc(p.sV2.h, 1)
            p.sV2.n += 1
            tokX = p.sV2.n
            b4, b5 = p.bankS.get(4, 0), p.bankS.get(5, 0)

            @blk.tensor
            def _(t, tb=tb, tokX=tokX, tokW=tokW, b4=b4, b5=b5):
                t.wait_ge(p.cG.h, tokW)
                t.wait_ge(p.sV2.h, tokX)
                t.wait_ge(p.sS.h, max(b4, b5))
                for cc in range(CC):
                    _li11 = t.matmul(_bank(p, 4), kvw[:, cc, 0:HD], kvx[:, tb, cc, :],
                             start=(cc == 0), stop=(cc == CC - 1))
                _li11.then_inc(p.sT.h, 1)
                for cc in range(CC):
                    _li12 = t.matmul(_bank(p, 5), kvw[:, cc, HD:2 * HD], kvx[:, tb, cc, :],
                             start=(cc == 0), stop=(cc == CC - 1))
                _li12.then_inc(p.sT.h, 1)
            p.sT.n += 2
            tokKV = p.sT.n
            xslot[tsb] = tokKV

            @blk.scalar
            def _(s, tokKV=tokKV, pk=kst.get('k', 0)):
                s.wait_ge(p.sT.h, tokKV)
                s.wait_ge(p.cE0.h, pk)
                s.copy(out=p.evac[:, 0, :], in_=_bank(p, 4)).then_inc(p.sS.h, 1)
                s.copy(out=p.evac[:, 1, :], in_=_bank(p, 5)).then_inc(p.sS.h, 1)
            p.sS.n += 2
            p.bankS[4] = p.sS.n - 1
            p.bankS[5] = p.sS.n
            tokEv = p.sS.n

            @blk.sync
            def _(s, tsb=tsb, tokEv=tokEv):
                s.wait_ge(p.sS.h, tokEv)
                s.dma_start(out=p.ag_kv_in[0, :, tsb * 4:(tsb + 1) * 4, :],
                            in_=p.evac[:, 0, :].rearrange('p (c d) -> p c d', d=HD)).then_inc(p.cE0.h, 16)
            p.cE0.n += 16
            kst['k'] = p.cE0.n

            for tt in (range(4) if _SUB not in ('l0a_prodK',) else []):
                b7 = p.bankS.get(7, 0)

                @blk.tensor
                def _(t, tt=tt, tokEv=tokEv, b7=b7):
                    t.wait_ge(p.sS.h, tokEv)
                    t.wait_ge(p.sS.h, b7)
                    t.transpose(_bank_bf(p, 7),
                                p.evac[:, 1, tt * 128:(tt + 1) * 128], p.ident[:, :]).then_inc(p.sT.h, 1)
                p.sT.n += 1
                tokTr = p.sT.n

                @blk.scalar
                def _(s, tt=tt, tokTr=tokTr, pv=kst.get('v', 0)):
                    s.wait_ge(p.sT.h, tokTr)
                    s.wait_ge(p.cE2.h, pv)
                    s.copy(out=p.evac[:, 2, tt * 128:(tt + 1) * 128],
                           in_=_bank_bf(p, 7)).then_inc(p.sS.h, 1)
                p.sS.n += 1
                p.bankS[7] = p.sS.n
            tokVt = p.sS.n

            if _SUB not in ('l0a_prodK',):
                @blk.sync
                def _(s, tsb=tsb, tokVt=tokVt):
                    s.wait_ge(p.sS.h, tokVt)
                    s.dma_start(out=p.ag_kv_in[1, :, tsb * 4:(tsb + 1) * 4, :],
                                in_=p.evac[:, 2, :].rearrange('p (c d) -> p c d', d=HD)).then_inc(p.cE2.h, 16)
                p.cE2.n += 16
                kst['v'] = p.cE2.n
        tok_k, tok_v = p.cE0.n, p.cE2.n

    if _SUB in ('l0a_prod', 'l0a_prodK'):
        @blk.gpsimd
        def _(g, tok_k=tok_k, tok_v=tok_v):
            g.wait_ge(p.cE0.h, tok_k)
            g.wait_ge(p.cE2.h, tok_v)
        return

    @blk.gpsimd
    def _(g, tok_k=tok_k, tok_v=tok_v):
        g.wait_ge(p.cE0.h, tok_k)
        g.wait_ge(p.cE2.h, tok_v)
        g.collective_compute('AllGather', ALU.bypass, replica_groups=p.RG,
                             ins=[p.ag_kv_in[:, :, :, :].opt()],
                             outs=[p.ag_kv_out[:, :, :, :, :].opt()]).then_inc(p.sCC.h, 1)
    p.sCC.n += 1
    kv_ready = p.sCC.n

    if _SUB == 'l0a_kv':
        return
    _proj_to_scratch(p, 'l0_qwT', p.q_scr, H)
    if _SUB == 'l0a_kvq':
        return

    def kload(s, dst, pi, j0, j1, h):
        ck = p.cKV if pi == 0 else p.cKV1
        s.dma_start(out=dst, in_=p.ag_kv_out[pi, 0, :, j0:j1, :]).then_inc(ck.h, 16)

    def vload(s, dst, pi, j0, j1, h):
        ck = p.cKV if pi == 0 else p.cKV1
        s.dma_start(out=dst, in_=p.ag_kv_out[pi, 1, :, j0:j1, :]).then_inc(ck.h, 16)

    _attention(p, _full_chunks, None, kload, vload, do_oproj=True,
               ow_key='l0_owT', kv_shared=True, kv_wait=[(p.sCC.h, kv_ready)])



def _l0_ffn(p):
    nc, blk = p.nc, p.blk
    _phase_barrier(p)
    mask_tok = _load_masks(p, 0, p.x_tok)

    with nc.sbuf_tensor(_u('gw'), [128, 2, CC, H], BF16) as gw, \
         nc.sbuf_tensor(_u('gwf'), [128, CC, H], F32) as gwf, \
         nc.sbuf_tensor(_u('fxh'), [128, 2, CC, 128], BF16) as fxh, \
         nc.sbuf_tensor(_u('fxl'), [128, 2, CC, 128], BF16) as fxl, \
         nc.sbuf_tensor(_u('gind'), [128, NKC, H], F32) as gind, \
         nc.sbuf_tensor(_u('gtop'), [128, NKC, H], F32) as gtop, \
         nc.sbuf_tensor(_u('glg'), [128, 2, H], F32) as glg, \
         nc.sbuf_tensor(_u('glge'), [128, 2, H], F32) as glge, \
         nc.sbuf_tensor(_u('gmx'), [128, 2, 2], F32) as gmx, \
         nc.sbuf_tensor(_u('gden'), [128, 2, 1], F32) as gden, \
         nc.sbuf_tensor(_u('fkw'), [128, CC, 512], BF16) as fkw, \
         nc.sbuf_tensor(_u('fxk'), [128, 2, CC, 128], BF16) as fxk, \
         nc.sbuf_tensor(_u('fsh'), [128, 2, NKC, HD], BF16) as fsh:

        @blk.gpsimd
        def _(g):
            g.dma_start(out=gwf[:, :, :], in_=p.w['l0_wgT'][:, :, :]).then_inc(p.cG.h, 16)
            g.dma_start(out=gw[:, 0, :, :], in_=p.w['l0_wgT'][:, :, :]).then_inc(p.cG.h, 16)
        p.cG.n += 32
        tokGW = p.cG.n

        @blk.vector
        def _(v, tokGW=tokGW):
            v.wait_ge(p.cG.h, tokGW)
            v.tensor_sub(gw[:, 1, :, :], gwf[:, :, :], gw[:, 0, :, :]).then_inc(p.sV.h, 1)
        p.sV.n += 1
        tokGWL = p.sV.n

        gmmtok = {}
        glgcopy = {}
        expdone = {}
        gtdone = {}
        gmaxtok = {}
        g_pre = p.sT.n
        for tc in range(NKC):
            tsl = slice(tc * 128, (tc + 1) * 128)
            tb = tc % 2
            tokT_prev = gmmtok.get(tc - 2, g_pre)
            xt = p.x_tok

            @blk.vector
            def _(v, tsl=tsl, tb=tb, tokT_prev=tokT_prev, xt=xt, sv2b=p.sV2.n):
                v.wait_ge(p.sV.h, xt)
                v.wait_ge(p.sT.h, tokT_prev)
                v.tensor_copy(fxh[:, tb, :, :], p.xsb[:, :, tsl]).then_inc(p.sV2.h, 1)
                v.wait_ge(p.sV2.h, sv2b + 1)
                v.scalar_tensor_tensor(fxl[:, tb, :, :], fxh[:, tb, :, :], -1.0,
                                       p.xsb[:, :, tsl], ALU.mult, ALU.add).then_inc(p.sV.h, 1)
            p.sV2.n += 1
            p.sV.n += 1
            tokXL = p.sV.n
            gbk = 6 + tb
            bkg = glgcopy.get(tc - 2, 0)

            @blk.tensor
            def _(t, tb=tb, gbk=gbk, tokXL=tokXL, tokGWL=tokGWL, bkg=bkg):
                t.wait_ge(p.sV.h, max(tokXL, tokGWL, bkg))
                for cc in range(CC):
                    t.matmul(_bank(p, gbk, H), fxh[:, tb, cc, :], gw[:, 0, cc, :],
                             start=(cc == 0), stop=False)
                for cc in range(CC):
                    t.matmul(_bank(p, gbk, H), fxh[:, tb, cc, :], gw[:, 1, cc, :],
                             start=False, stop=False)
                for cc in range(CC):
                    _li16 = t.matmul(_bank(p, gbk, H), fxl[:, tb, cc, :], gw[:, 0, cc, :],
                             start=False, stop=(cc == CC - 1))
                _li16.then_inc(p.sT.h, 1)
            p.sT.n += 1
            gmmtok[tc] = p.sT.n
            tokLG = p.sT.n

            @blk.vector
            def _(v, tc=tc, tb=tb, gbk=gbk, tokLG=tokLG, svb=p.sV.n,
                  sv2b=p.sV2.n, geprev=expdone.get(tc - 2, 0)):
                v.wait_ge(p.sT.h, tokLG)
                v.wait_ge(p.sS.h, geprev)
                v.tensor_copy(glg[:, tb, :], _bank(p, gbk, H)).then_inc(p.sV.h, 1)
                v.wait_ge(p.sV.h, svb + 1)
                v.tensor_reduce(gmx[:, tb, 0:1], glg[:, tb, :], mybir.AxisListType.X,
                                ALU.max).then_inc(p.sV2.h, 1)
                v.wait_ge(p.sV2.h, sv2b + 1)
                v.tensor_scalar_mul(gmx[:, tb, 1:2], gmx[:, tb, 0:1], -1.0).then_inc(p.sV2.h, 1)
                v.tensor_scalar(gind[:, tc, :], glg[:, tb, :], gmx[:, tb, 0:1], None,
                                ALU.is_ge).then_inc(p.sV.h, 1)
            p.sV2.n += 2
            p.sV.n += 2
            glgcopy[tc] = p.sV.n - 1
            gmaxtok[tc] = p.sV.n
            tokGMAX = p.sV.n

            @blk.scalar
            def _(s, tb=tb, tokGMAX=tokGMAX, gtprev=gtdone.get(tc - 2, 0)):
                s.wait_ge(p.sV.h, max(tokGMAX, gtprev))
                s.activation(out=glge[:, tb, :], in_=glg[:, tb, :], func=AF.Exp,
                             bias=gmx[:, tb, 1:2], accum_out=gden[:, tb, :]).then_inc(p.sS.h, 1)
            p.sS.n += 1
            tokGE = p.sS.n
            expdone[tc] = tokGE

            @blk.vector
            def _(v, tc=tc, tb=tb, tokGE=tokGE, sv2b=p.sV2.n):
                v.wait_ge(p.sS.h, tokGE)
                v.reciprocal(gden[:, tb, :], gden[:, tb, :]).then_inc(p.sV2.h, 1)
                v.wait_ge(p.sV2.h, sv2b + 1)
                v.tensor_scalar(gtop[:, tc, :], gind[:, tc, :], gden[:, tb, 0:1], None,
                                ALU.mult).then_inc(p.sV.h, 1)
            p.sV2.n += 1
            p.sV.n += 1
            gtdone[tc] = p.sV.n

        fkw_use = {}
        fxk_use = {}
        comb_done = {}
        kv_pre = p.sT.n
        for e in range(8):
            shi = 0 if e < 4 else 1
            hbase = (e % 4) * 4
            wprev = fkw_use.get(e - 1, kv_pre)

            @blk.gpsimd
            def _(g, e=e, wprev=wprev):
                g.wait_ge(p.sT.h, wprev)
                g.dma_start(out=fkw[:, :, :],
                            in_=p.w['l0_kvT'][:, :, e * 512:(e + 1) * 512]).then_inc(p.cW0.h, 16)
            p.cW0.n += 16
            tokKW = p.cW0.n

            for tc in range(NKC):
                tsl = slice(tc * 128, (tc + 1) * 128)
                tb = tc % 2
                le = e * NKC + tc
                xprev = fxk_use.get(le - 2, kv_pre)

                @blk.vector
                def _(v, tsl=tsl, tb=tb, xprev=xprev):
                    v.wait_ge(p.sT.h, xprev)
                    v.tensor_copy(fxk[:, tb, :, :], p.xsb[:, :, tsl]).then_inc(p.sV2.h, 1)
                p.sV2.n += 1
                tokXC = p.sV2.n
                bk = 4 + (e % 2)
                bkv = comb_done.get((e % 2, le - NKC), 0) if tc == 0 else \
                    comb_done.get((e % 2, le - 1), 0)

                @blk.tensor
                def _(t, tb=tb, bk=bk, tokXC=tokXC, tokKW=tokKW, bkv=bkv):
                    t.wait_ge(p.sV2.h, tokXC)
                    t.wait_ge(p.cW0.h, tokKW)
                    t.wait_ge(p.sV.h, bkv)
                    for cc in range(CC):
                        _li20 = t.matmul(_bank(p, bk), fxk[:, tb, cc, :], fkw[:, cc, :],
                                 start=(cc == 0), stop=(cc == CC - 1))
                    _li20.then_inc(p.sT.h, 1)
                p.sT.n += 1
                tokKVm = p.sT.n
                fxk_use[le] = tokKVm
                if tc == NKC - 1:
                    fkw_use[e] = tokKVm

                @blk.vector
                def _(v, e=e, tc=tc, bk=bk, shi=shi, hbase=hbase, tokKVm=tokKVm,
                      gsel_tok=(gmaxtok[tc] if e < 4 else gtdone[tc]), sv2b=p.sV2.n):
                    v.wait_ge(p.sT.h, tokKVm)
                    v.wait_ge(p.sV.h, gsel_tok)
                    gsel = gind if e < 4 else gtop
                    first = (e % 4 == 0)
                    src4 = _bank(p, bk)
                    sv2 = sv2b
                    for hh in range(4):
                        h = hbase + hh
                        so = hh * 128
                        if not (first and hh == 0):
                            v.wait_ge(p.sV2.h, sv2)
                        if first and hh == 0:
                            _li21 = v.tensor_scalar(
                                fsh[:, shi, tc, :], src4[:, so:so + 128],
                                gsel[:, tc, h:h + 1], None, ALU.mult)
                        else:
                            _li21 = v.scalar_tensor_tensor(
                                fsh[:, shi, tc, :], src4[:, so:so + 128],
                                gsel[:, tc, h:h + 1], fsh[:, shi, tc, :],
                                ALU.mult, ALU.add)
                        _li21.then_inc(p.sV2.h, 1)
                        sv2 += 1
                    v.wait_ge(p.sV2.h, sv2)
                    v.engine_nop().then_inc(p.sV.h, 1)
                p.sV2.n += 4
                p.sV.n += 1
                comb_done[(e % 2, le)] = p.sV.n
        tok_sel = p.sV.n

        with nc.sbuf_tensor(_u('fshT'), [128, NKC, HD], BF16) as fshT:
            for tc in range(NKC):
                b7 = p.bankS.get(7, 0)

                @blk.tensor
                def _(t, tc=tc, tok_sel=tok_sel, b7=b7):
                    t.wait_ge(p.sV.h, tok_sel)
                    t.wait_ge(p.sS.h, b7)
                    t.transpose(_bank_bf(p, 7), fsh[:, 0, tc, :], p.ident[:, :]).then_inc(p.sT.h, 1)
                p.sT.n += 1
                tokTr = p.sT.n

                @blk.scalar
                def _(s, tc=tc, tokTr=tokTr):
                    s.wait_ge(p.sT.h, tokTr)
                    s.copy(out=fshT[:, tc, :], in_=_bank_bf(p, 7)).then_inc(p.sS.h, 1)
                p.sS.n += 1
                p.bankS[7] = p.sS.n
            tokTall = p.sS.n

            @blk.sync
            def _(s, tokTall=tokTall, tok_sel=tok_sel):
                s.wait_ge(p.sS.h, tokTall)
                s.wait_ge(p.sV.h, tok_sel)
                s.dma_start(out=p.ag_kv_in[0, :, :, :], in_=fshT[:, :, :]).then_inc(p.cA.h, 16)
                s.dma_start(out=p.ag_kv_in[1, :, :, :], in_=fsh[:, 1, :, :]).then_inc(p.cA.h, 16)
            p.cA.n += 32
            tok_stores = p.cA.n

    @blk.gpsimd
    def _(g, tok_stores=tok_stores):
        g.wait_ge(p.cA.h, tok_stores)
        g.collective_compute('AllGather', ALU.bypass, replica_groups=p.RG,
                             ins=[p.ag_kv_in[:, :, :, :].opt()],
                             outs=[p.ag_kv_out[:, :, :, :, :].opt()]).then_inc(p.sCC.h, 1)
    p.sCC.n += 1
    kv_ready = p.sCC.n

    _proj_to_scratch(p, 'l0_qT', p.q_scr, H)

    def kload(s, dst, pi, j0, j1, h):
        ck = p.cKV if pi == 0 else p.cKV1
        s.dma_start(out=dst, in_=p.ag_kv_out[pi, 0, :, j0:j1, :]).then_inc(ck.h, 16)

    def vload(s, dst, pi, j0, j1, h):
        ck = p.cKV if pi == 0 else p.cKV1
        s.dma_start(out=dst, in_=p.ag_kv_out[pi, 1, :, j0:j1, :]).then_inc(ck.h, 16)

    _attention(p, _causal_chunks, 0, kload, vload, do_oproj=False,
               ow_key=None, kv_shared=True, kv_wait=[(p.sCC.h, kv_ready)])



def _l1_attn(p):
    nc, blk = p.nc, p.blk
    _phase_barrier(p)
    mask_tok = _load_masks(p, 8, p.x_tok)

    xt = p.x_tok
    with nc.sbuf_tensor(_u('lkw'), [128, 2, CC, 256], BF16) as lkw, \
         nc.sbuf_tensor(_u('lxt'), [128, 2, CC, SBW], BF16) as lxt:
        wuse = {}
        xuse = {}
        k_pre = p.sT.n
        for tsb in range(NSB):
            xs = tsb % 2
            xprev = xuse.get(tsb - 2, k_pre)
            tsl = slice(tsb * SBW, (tsb + 1) * SBW)

            @blk.vector
            def _(v, tsl=tsl, xs=xs, xprev=xprev, xt=xt):
                v.wait_ge(p.sV.h, xt)
                v.wait_ge(p.sT.h, xprev)
                v.tensor_copy(lxt[:, xs, :, :], p.xsb[:, :, tsl]).then_inc(p.sV2.h, 1)
            p.sV2.n += 1
            tokX = p.sV2.n

            for hp in range(8):
                q = tsb * 8 + hp
                ws = q % 2
                cw = p.cW0 if ws == 0 else p.cW1
                wprev = wuse.get(q - 2, k_pre)

                @blk.gpsimd
                def _(g, hp=hp, ws=ws, wprev=wprev, cw=cw):
                    g.wait_ge(p.sT.h, wprev)
                    g.dma_start(out=lkw[:, ws, :, :],
                                in_=p.w['l1_wk'][:, :, hp * 256:(hp + 1) * 256]).then_inc(cw.h, 16)
                cw.n += 16
                tokKW = cw.n

                for hh in range(2):
                    h = hp * 2 + hh
                    bk = 4 + hh
                    ev = hh
                    bs = p.bankS.get(bk, 0)

                    @blk.tensor
                    def _(t, hh=hh, ws=ws, xs=xs, bk=bk, tokX=tokX,
                          tokKW=tokKW, bs=bs, cw=cw):
                        t.wait_ge(cw.h, tokKW)
                        t.wait_ge(p.sV2.h, tokX)
                        t.wait_ge(p.sS.h, bs)
                        for cc in range(CC):
                            _li23 = t.matmul(_bank(p, bk), lkw[:, ws, cc, hh * 128:(hh + 1) * 128],
                                     lxt[:, xs, cc, :], start=(cc == 0), stop=(cc == CC - 1))
                        _li23.then_inc(p.sT.h, 1)
                    p.sT.n += 1
                    tokMM = p.sT.n
                    evfree = p.evS.get(ev, 0)
                    ce = p.cE0 if ev == 0 else p.cE1

                    @blk.scalar
                    def _(s, ev=ev, bk=bk, tokMM=tokMM, evfree=evfree, ce=ce):
                        s.wait_ge(p.sT.h, tokMM)
                        s.wait_ge(ce.h, evfree)
                        s.copy(out=p.evac[:, ev, :], in_=_bank(p, bk)).then_inc(p.sS.h, 1)
                    p.sS.n += 1
                    p.bankS[bk] = p.sS.n
                    tokEv = p.sS.n

                    @blk.sync
                    def _(s, h=h, tsl=tsl, ev=ev, tokEv=tokEv, ce=ce):
                        s.wait_ge(p.sS.h, tokEv)
                        s.dma_start(out=p.agki[h, :, tsl],
                                    in_=p.evac[:, ev, :]).then_inc(ce.h, 16)
                    ce.n += 16
                    p.evS[ev] = ce.n
                wuse[q] = p.sT.n
            xuse[tsb] = p.sT.n

    _phase_barrier(p)
    with nc.sbuf_tensor(_u('lvw'), [128, 2, CC, 512], BF16) as lvw, \
         nc.sbuf_tensor(_u('lxc'), [128, 2, CC, 128], BF16) as lxc:
        vwuse = {}
        vxuse = {}
        v_pre = p.sT.n
        for vh in range(2):
            for half in (0, 1):
                idx = 2 * vh + half
                cw = p.cW0 if half == 0 else p.cW1
                wprev = vwuse.get(idx - 2, v_pre)

                @blk.gpsimd
                def _(g, vh=vh, half=half, wprev=wprev, cw=cw):
                    g.wait_ge(p.sT.h, wprev)
                    c0 = vh * 1024 + half * 512
                    g.dma_start(out=lvw[:, half, :, :],
                                in_=p.w['l1_wv'][:, :, c0:c0 + 512]).then_inc(cw.h, 16)
                cw.n += 16
            tokVW0 = p.cW0.n
            tokVW1 = p.cW1.n

            for tc in range(NKC):
                tb = tc % 2
                it4 = vh * NKC + tc
                tsl = slice(tc * 128, (tc + 1) * 128)
                xprev = vxuse.get(it4 - 2, v_pre)

                @blk.vector
                def _(v, tsl=tsl, tb=tb, xprev=xprev, xt=xt):
                    v.wait_ge(p.sV.h, xt)
                    v.wait_ge(p.sT.h, xprev)
                    v.tensor_copy(lxc[:, tb, :, :], p.xsb[:, :, tsl]).then_inc(p.sV2.h, 1)
                p.sV2.n += 1
                tokXC = p.sV2.n
                b4, b5 = p.bankS.get(4, 0), p.bankS.get(5, 0)

                @blk.tensor
                def _(t, tb=tb, tokXC=tokXC, tokVW0=tokVW0, tokVW1=tokVW1,
                      b4=b4, b5=b5):
                    t.wait_ge(p.cW0.h, tokVW0)
                    t.wait_ge(p.cW1.h, tokVW1)
                    t.wait_ge(p.sV2.h, tokXC)
                    t.wait_ge(p.sS.h, max(b4, b5))
                    for cc in range(CC):
                        t.matmul(_bank(p, 4), lxc[:, tb, cc, :], lvw[:, 0, cc, :],
                                 start=(cc == 0), stop=(cc == CC - 1))
                    for cc in range(CC):
                        _li24 = t.matmul(_bank(p, 5), lxc[:, tb, cc, :], lvw[:, 1, cc, :],
                                 start=(cc == 0), stop=(cc == CC - 1))
                    _li24.then_inc(p.sT.h, 1)
                p.sT.n += 1
                tokVm = p.sT.n
                vxuse[it4] = tokVm
                if tc == NKC - 1:
                    vwuse[2 * vh] = tokVm
                    vwuse[2 * vh + 1] = tokVm

                @blk.scalar
                def _(s, tokVm=tokVm, pv=p.evS.get('l1v', 0)):
                    s.wait_ge(p.sT.h, tokVm)
                    s.wait_ge(p.cE2.h, pv)
                    s.copy(out=p.evac[:, 0, :], in_=_bank(p, 4))
                    s.copy(out=p.evac[:, 1, :], in_=_bank(p, 5)).then_inc(p.sS.h, 1)
                p.sS.n += 1
                p.bankS[4] = p.sS.n
                p.bankS[5] = p.sS.n
                tokEv = p.sS.n

                @blk.sync
                def _(s, tc=tc, vh=vh, tokEv=tokEv):
                    s.wait_ge(p.sS.h, tokEv)
                    s.dma_start(out=p.agvi[:, tc, vh * 1024:vh * 1024 + 512],
                                in_=p.evac[:, 0, :]).then_inc(p.cE2.h, 16)
                    s.dma_start(out=p.agvi[:, tc, vh * 1024 + 512:(vh + 1) * 1024],
                                in_=p.evac[:, 1, :]).then_inc(p.cE2.h, 16)
                p.cE2.n += 32
                p.evS['l1v'] = p.cE2.n

    @blk.gpsimd
    def _(g, e0=p.cE0.n, e1=p.cE1.n, e2=p.cE2.n):
        g.wait_ge(p.cE0.h, e0)
        g.wait_ge(p.cE1.h, e1)
        g.wait_ge(p.cE2.h, e2)
        g.collective_compute('AllGather', ALU.bypass, replica_groups=p.RG,
                             ins=[p.agki[:, :, :].opt()],
                             outs=[p.agko[:, :, :, :].opt()]).then_inc(p.sCC.h, 1)
        g.collective_compute('AllGather', ALU.bypass, replica_groups=p.RG,
                             ins=[p.agvi[:, :, :].opt()],
                             outs=[p.agvo[:, :, :, :].opt()]).then_inc(p.sCC.h, 1)
    p.sCC.n += 2
    kv_ag = p.sCC.n

    _proj_to_scratch(p, 'l1_wq', p.q_scr, H)
    kv_stores = [(p.sCC.h, kv_ag)]

    def kload(s, dst, pi, j0, j1, h):
        ck = p.cKV if pi == 0 else p.cKV1
        s.dma_start(out=dst,
                    in_=p.agko[pi, h, :, j0 * KCW:j1 * KCW].rearrange(
                        'p (c d) -> p c d', d=HD)).then_inc(ck.h, 16)

    def vload(s, dst, pi, j0, j1, h):
        ck = p.cKV if pi == 0 else p.cKV1
        s.dma_start(out=dst,
                    in_=p.agvo[pi, :, j0:j1, h * HD:(h + 1) * HD]).then_inc(ck.h, 16)

    _attention(p, _anti_chunks, 0, kload, vload, do_oproj=True,
               ow_key='l1_wo', l1_edge=True, kv_shared=False,
               kv_wait=kv_stores)



def _l1_ffn(p):
    nc, blk = p.nc, p.blk
    _phase_barrier(p)
    p.bankS[4] = 0
    p.bankS[5] = 0

    with nc.sbuf_tensor(_u('fgw'), [128, 2, CC, 256], BF16) as fgw, \
         nc.sbuf_tensor(_u('fuw'), [128, 2, CC, 256], BF16) as fuw, \
         nc.sbuf_tensor(_u('fxb'), [128, CC, SBW], BF16) as fxb, \
         nc.sbuf_tensor(_u('fsg'), [128, SBW], BF16) as fsg:

        xt = p.x_tok
        wslot = {}
        w_pre = p.sT.n
        for hq in range(NKC):
            for half in (0, 1):
                idx = 2 * hq + half
                cw = p.cW0 if half == 0 else p.cW1
                wprev = wslot.get(idx - 2, w_pre)

                @blk.gpsimd
                def _(g, hq=hq, half=half, wprev=wprev, cw=cw):
                    g.wait_ge(p.sT.h, wprev)
                    c0 = hq * 512 + half * 256
                    g.dma_start(out=fgw[:, half, :, :],
                                in_=p.w['l1_wgT'][:, :, c0:c0 + 256]).then_inc(cw.h, 16)
                    g.dma_start(out=fuw[:, half, :, :],
                                in_=p.w['l1_wuT'][:, :, c0:c0 + 256]).then_inc(cw.h, 16)
                cw.n += 32
            tokW0 = p.cW0.n
            tokW1 = p.cW1.n

            for tsb in range(NSB):
                tsl = slice(tsb * SBW, (tsb + 1) * SBW)
                tokT_prev2 = p.sT.n

                @blk.vector
                def _(v, tsl=tsl, tokT_prev2=tokT_prev2, xt=xt):
                    v.wait_ge(p.sV.h, xt)
                    v.wait_ge(p.sT.h, tokT_prev2)
                    v.tensor_copy(fxb[:, :, :], p.xsb[:, :, tsl]).then_inc(p.sV2.h, 1)
                p.sV2.n += 1
                tokXB = p.sV2.n
                for hh in range(4):
                    hc = hq * 4 + hh
                    hs = hh // 2
                    ho = (hh % 2) * 128
                    tw = tokW0 if hs == 0 else tokW1
                    cwx = p.cW0 if hs == 0 else p.cW1
                    b4 = p.bankS.get(4, 0)
                    b5 = p.bankS.get(5, 0)

                    @blk.tensor
                    def _(t, hs=hs, ho=ho, tw=tw, cwx=cwx, tokXB=tokXB, b4=b4, b5=b5):
                        t.wait_ge(cwx.h, tw)
                        t.wait_ge(p.sV2.h, tokXB)
                        t.wait_ge(p.sS.h, b4)
                        for cc in range(CC):
                            _li25 = t.matmul(_bank(p, 4), fgw[:, hs, cc, ho:ho + 128],
                                     fxb[:, cc, :], start=(cc == 0), stop=(cc == CC - 1))
                        _li25.then_inc(p.sT.h, 1)
                        t.wait_ge(p.sV.h, b5)
                        for cc in range(CC):
                            _li26 = t.matmul(_bank(p, 5), fuw[:, hs, cc, ho:ho + 128],
                                     fxb[:, cc, :], start=(cc == 0), stop=(cc == CC - 1))
                        _li26.then_inc(p.sT.h, 1)
                    p.sT.n += 2
                    if tsb == NSB - 1 and hh == 1:
                        wslot[2 * hq] = p.sT.n
                    if tsb == NSB - 1 and hh == 3:
                        wslot[2 * hq + 1] = p.sT.n
                    tokG = p.sT.n - 1
                    tokU = p.sT.n

                    @blk.scalar
                    def _(s, tokG=tokG, pgt=p.sV.n):
                        s.wait_ge(p.sT.h, tokG)
                        s.wait_ge(p.sV.h, pgt)
                        s.activation(out=fsg[:, :], in_=_bank(p, 4),
                                     func=AF.Silu).then_inc(p.sS.h, 1)
                    p.sS.n += 1
                    p.bankS[4] = p.sS.n
                    tokSG = p.sS.n
                    evfree = p.evS.get(0, 0)

                    @blk.vector
                    def _(v, tokSG=tokSG, tokU=tokU, evfree=evfree):
                        v.wait_ge(p.sS.h, tokSG)
                        v.wait_ge(p.sT.h, tokU)
                        v.wait_ge(p.cE0.h, evfree)
                        v.tensor_tensor(p.evac[:, 0, :], _bank(p, 5), fsg[:, :],
                                        ALU.mult).then_inc(p.sV.h, 1)
                    p.sV.n += 1
                    p.bankS[5] = p.sV.n
                    tokGT = p.sV.n

                    @blk.sync
                    def _(s, hc=hc, tsl=tsl, tokGT=tokGT):
                        s.wait_ge(p.sV.h, tokGT)
                        s.dma_start(out=p.gated_scr[:, hc, tsl],
                                    in_=p.evac[:, 0, :]).then_inc(p.cE0.h, 16)
                    p.cE0.n += 16
                    p.evS[0] = p.cE0.n
        g_stores = p.cE0.n

    _phase_barrier(p)
    with nc.sbuf_tensor(_u('dwd'), [128, 2, 8, 1024], BF16) as dwd, \
         nc.sbuf_tensor(_u('dgt'), [128, 2, 8, SBW], BF16) as dgt:

        dslot = {}
        it = 0
        for tsb in range(NSB):
            tsl = slice(tsb * SBW, (tsb + 1) * SBW)
            rtoks = []
            for d2h in range(2):
                for hb in range(8):
                    sl2 = it % 2
                    prev_use = dslot.get(it - 2, 0)

                    cwx = p.cW0 if sl2 == 0 else p.cW1
                    cdx = p.cD if sl2 == 0 else p.cD1

                    @blk.gpsimd
                    def _(g, d2h=d2h, hb=hb, sl2=sl2, prev_use=prev_use, cwx=cwx):
                        g.wait_ge(p.sT.h, prev_use)
                        g.dma_start(out=dwd[:, sl2, :, :],
                                    in_=p.w['l1_wdT'][:, hb * 8:(hb + 1) * 8,
                                                      d2h * 1024:(d2h + 1) * 1024]).then_inc(cwx.h, 16)
                    cwx.n += 16
                    tokDW = cwx.n

                    @blk.sync
                    def _(s, hb=hb, tsl=tsl, sl2=sl2, prev_use=prev_use, g_stores=g_stores, cdx=cdx):
                        s.wait_ge(p.cE0.h, g_stores)
                        s.wait_ge(p.sT.h, prev_use)
                        s.dma_start(out=dgt[:, sl2, :, :],
                                    in_=p.gated_scr[:, hb * 8:(hb + 1) * 8, tsl]).then_inc(cdx.h, 16)
                    cdx.n += 16
                    tokGL = cdx.n
                    pof = p.po_free if hb == 0 else 0

                    @blk.tensor
                    def _(t, d2h=d2h, hb=hb, sl2=sl2, tokDW=tokDW, tokGL=tokGL, pof=pof,
                          cwx=cwx, cdx=cdx):
                        t.wait_ge(cwx.h, tokDW)
                        t.wait_ge(cdx.h, tokGL)
                        if pof:
                            t.wait_ge(p.sV.h, pof)
                        for hc in range(8):
                            for d2c in range(8):
                                _li27 = t.matmul(p.ps[:, d2c * 512:d2c * 512 + SBW],
                                         dwd[:, sl2, hc, d2c * 128:(d2c + 1) * 128],
                                         dgt[:, sl2, hc, :],
                                         start=(hb == 0 and hc == 0),
                                         stop=(hb == 7 and hc == 7))
                        _li27.then_inc(p.sT.h, 1)
                    p.sT.n += 1
                    dslot[it] = p.sT.n
                    it += 1
                tok_half = p.sT.n

                @blk.vector
                def _(v, d2h=d2h, tsl=tsl, tok_half=tok_half):
                    v.wait_ge(p.sT.h, tok_half)
                    for d2c in range(8):
                        cc = d2h * 8 + d2c
                        _li28 = v.tensor_tensor(p.xsb[:, cc, tsl], p.xsb[:, cc, tsl],
                                        p.ps[:, d2c * 512:d2c * 512 + SBW], ALU.add)
                    _li28.then_inc(p.sV.h, 1)
                p.sV.n += 1
                p.po_free = p.sV.n
                rtoks.append(p.sV.n)
            _ln_only(p, tsl, rtoks[-1])
            p.po_free = p.x_tok


def kernel(**inputs):
    in_maps = _stage_inputs(inputs)
    nc = build_program(_PHASES)
    res = run_bass_kernel_spmd(nc, in_maps, core_ids=list(range(8)))
    return _assemble(res.results)


# version 15 (final)
# speedup vs baseline: 1.0785x; 1.0544x over previous
import math
from contextlib import ExitStack

import numpy as np
import ml_dtypes

import concourse.bass as bass
import concourse.mybir as mybir
from concourse.bass_utils import run_bass_kernel_spmd

F32 = mybir.dt.float32
BF16 = mybir.dt.bfloat16
FP8 = mybir.dt.float8e4
AF = mybir.ActivationFunctionType
ALU = mybir.AluOpType

NPBF16 = ml_dtypes.bfloat16
NPFP8 = ml_dtypes.float8_e4m3

B = 4
S = 4096
D = 2048
H = 16
HD = 128
T = 2048
CC = 16
NSB = 4
SBW = 512
KCW = 128
NKC = 16
DFF = 8192
HC = 64

ATT_SCALE = 1.0 / math.sqrt(HD)
SHIFT = 18.0
LN_EPS = 1e-5

_PHASES = 4
_SUB = None



def _chunkT(w):
    C, M = w.shape
    return np.ascontiguousarray(
        np.asarray(w).reshape(C // 128, 128, M).transpose(1, 0, 2).astype(np.float32))


def _mask_tiles(r):
    p = np.arange(128)[:, None]
    f = np.arange(512)[None, :]
    tiles = np.zeros((16, 128, 512), dtype=NPFP8)
    for pi in (0, 1):
        e = -1 if (r == 0 and pi == 1) else 0
        ep = 0 if (r == 0 and pi == 1) else 1
        for d in range(4):
            tiles[pi * 4 + d] = (p - f + 128 * d <= e).astype(NPFP8)
            tiles[8 + pi * 4 + d] = (p - f + 128 * d >= ep).astype(NPFP8)
    return tiles


def _stage_inputs(inputs):
    g = {}
    g['l0_qwT'] = _chunkT(np.asarray(inputs['L0_attn_q_w']).T)
    g['l0_kwT'] = _chunkT(np.asarray(inputs['L0_attn_k_w']).T)
    g['l0_vwT'] = _chunkT(np.asarray(inputs['L0_attn_v_w']).T)
    g['l0_owT'] = _chunkT(np.asarray(inputs['L0_attn_o_w']).T)
    g['l0_wgT'] = _chunkT(np.asarray(inputs['L0_ffn_W_gate']).T)
    wqkv = np.asarray(inputs['L0_ffn_W_qkv'])
    g['l0_qT'] = _chunkT(wqkv[0:D].T)
    g['l0_kvT'] = _chunkT(wqkv[D:3 * D].T)
    g['l1_wq'] = _chunkT(np.asarray(inputs['L1_attn_Wq']))
    g['l1_wk'] = _chunkT(np.asarray(inputs['L1_attn_Wk']))
    g['l1_wv'] = _chunkT(np.asarray(inputs['L1_attn_Wv']))
    g['l1_wo'] = _chunkT(np.asarray(inputs['L1_attn_Wo']))
    g['l1_wgT'] = _chunkT(np.asarray(inputs['L1_ffn_w_gate']).T)
    g['l1_wuT'] = _chunkT(np.asarray(inputs['L1_ffn_w_up']).T)
    g['l1_wdT'] = _chunkT(np.asarray(inputs['L1_ffn_w_down']).T)

    x0 = np.asarray(inputs['L0_attn_x'], dtype=np.float32)
    ident = np.eye(128, dtype=NPBF16)
    ones_col = np.ones((128, 1), dtype=np.float32)
    ones_sq = np.ones((128, 128), dtype=np.float32)
    ones_row = np.ones((1, 128), dtype=np.float32)
    ones_colb = np.ones((128, 1), dtype=NPBF16)
    ones_sqb = np.ones((128, 128), dtype=NPBF16)

    in_maps = []
    for c in range(8):
        b, r = c // 2, c % 2
        m = dict(g)
        xown = np.ascontiguousarray(x0[b, r::2].T)
        m['x0own'] = np.ascontiguousarray(
            xown.reshape(CC, 128, T).transpose(1, 0, 2))
        m['masks'] = np.ascontiguousarray(_mask_tiles(r).transpose(1, 0, 2))
        edge = np.zeros((128, 4), dtype=np.float32)
        edge[:, 0] = 1.0 if r == 0 else 0.0
        edge[:, 1] = 0.0 if r == 0 else 1.0 / S
        m['edge'] = edge
        patch = np.zeros((128, 1), dtype=np.float32)
        if r == 1:
            patch[:, 0] = 1.0
        m['patch'] = patch
        m['ident'] = ident
        m['ones_col'] = ones_col
        m['ones_sq'] = ones_sq
        m['ones_row'] = ones_row
        m['ones_colb'] = ones_colb
        m['ones_sqb'] = ones_sqb
        in_maps.append(m)
    return in_maps


def _assemble(results):
    out = np.empty((B, S, D), dtype=np.float32)
    for c in range(8):
        b, r = c // 2, c % 2
        o = np.asarray(results[c]['out_fm'])
        out[b, r::2, :] = o.transpose(1, 0, 2).reshape(D, T).T
    return out



_UN = [0]


def _u(name):
    _UN[0] += 1
    return "%s_%d" % (name, _UN[0])


class P:
    pass


def _diag_d(sb, j):
    d = j - 4 * sb
    return d if 0 <= d <= 3 else None


def _causal_chunks(sb):
    return [(pi, j) for pi in (0, 1) for j in range(4 * sb + 4)]


def _anti_chunks(sb):
    return [(pi, j) for pi in (0, 1) for j in range(4 * sb, NKC)]


def _full_chunks(sb):
    return [(pi, j) for pi in (0, 1) for j in range(NKC)]


def _bank(p, i, width=SBW):
    return p.ps[:, i * 512:i * 512 + width]


def build_program(phases=4):
    nc = bass.Bass('TRN2', target_bir_lowering=False)
    p = P()
    p.nc = nc

    dI = lambda name, shape, dt=F32: nc.dram_tensor(name, shape, dt, kind='ExternalInput').ap()
    dX = lambda name, shape, dt=BF16: nc.dram_tensor(name, shape, dt).ap()

    p.x0own = dI('x0own', [128, CC, T])
    p.w = {}
    for k, shape in {
        'l0_qwT': (128, CC, D), 'l0_kwT': (128, CC, HD), 'l0_vwT': (128, CC, HD),
        'l0_owT': (128, CC, D), 'l0_wgT': (128, CC, H), 'l0_qT': (128, CC, D),
        'l0_kvT': (128, CC, 2 * D), 'l1_wq': (128, CC, D), 'l1_wk': (128, CC, D),
        'l1_wv': (128, CC, D), 'l1_wo': (128, CC, D), 'l1_wgT': (128, CC, DFF),
        'l1_wuT': (128, CC, DFF), 'l1_wdT': (128, HC, D),
    }.items():
        p.w[k] = dI(k, list(shape))
    p.masks_in = dI('masks', [128, 16, SBW], FP8)
    p.edge_in = dI('edge', [128, 4])
    p.patch_in = dI('patch', [128, 1])
    p.ident_in = dI('ident', [128, 128], BF16)
    p.onesc_in = dI('ones_col', [128, 1])
    p.onessq_in = dI('ones_sq', [128, 128])
    p.onesr_in = dI('ones_row', [1, 128])
    p.onescb_in = dI('ones_colb', [128, 1], BF16)
    p.onesqb_in = dI('ones_sqb', [128, 128], BF16)
    p.out_fm = nc.dram_tensor('out_fm', [128, CC, T], F32, kind='ExternalOutput').ap()

    p.q_scr = dX('q_scr', [H, 128, T])
    p.k_loc = dX('k_loc', [2, H, 128, T])
    p.v_loc = dX('v_loc', [2, 128, NKC, D])
    p.gated_scr = dX('gated_scr', [128, HC, T])
    p.agki = dX('agki', [H, 128, T])
    p.agko = dX('agko', [2, H, 128, T])
    p.agvi = dX('agvi', [128, NKC, D])
    p.agvo = dX('agvo', [2, 128, NKC, D])
    p.ag_kv_in = dX('ag_kv_in', [2, 128, NKC, HD])
    p.ag_kv_out = dX('ag_kv_out', [2, 2, 128, NKC, HD])
    p.ag_x_in = dX('ag_x_in', [128, CC, T])
    p.ag_x_out = dX('ag_x_out', [2, 128, CC, T])
    p.RG = [[0, 1], [2, 3], [4, 5], [6, 7]]

    with ExitStack() as es:
        p.es = es
        sb_t = lambda name, shape, dt: es.enter_context(nc.sbuf_tensor(name, shape, dt))
        p.xsb = sb_t('xsb', [128, CC, T], F32)
        p.consts = sb_t('consts', [128, 8], F32)
        p.onesc = sb_t('onesc', [128, 1], F32)
        p.onessq = sb_t('onessq', [128, 128], F32)
        p.onesr = sb_t('onesr', [1, 128], F32)
        p.onescb = sb_t('onescb', [128, 1], BF16)
        p.onesqb = sb_t('onesqb', [128, 128], BF16)
        p.ident = sb_t('ident_sb', [128, 128], BF16)
        p.edge = sb_t('edge_sb', [128, 4], F32)
        p.patch = sb_t('patch_sb', [128, 1], F32)
        p.masks = sb_t('masks_sb', [128, 8, SBW], FP8)
        p.lnm = sb_t('lnm', [1, SBW], F32)
        p.lnt1 = sb_t('lnt1', [1, SBW], F32)
        p.lnt2 = sb_t('lnt2', [1, SBW], F32)
        p.lns = sb_t('lns', [1, SBW], F32)
        p.sbst = sb_t('sbst', [128, 2, SBW], F32)
        p.sqbuf = sb_t('sqbuf', [128, 1, SBW], F32)
        p.evac = sb_t('evac', [128, 3, SBW], BF16)
        p.sumv = sb_t('sumv', [128, H], F32)
        p.sumvb = sb_t('sumvb', [128, H], F32)

        class Ctr:
            def __init__(self, name):
                self.h = es.enter_context(nc.semaphore(name))
                self.n = 0
        p.cS = Ctr('cS')
        p.cG = Ctr('cG')
        p.cM = Ctr('cM')
        p.cW0 = Ctr('cW0')
        p.cW1 = Ctr('cW1')
        p.cE0 = Ctr('cE0')
        p.cE1 = Ctr('cE1')
        p.cE2 = Ctr('cE2')
        p.cKV = Ctr('cKV')
        p.cKV1 = Ctr('cKV1')
        p.cQ0 = Ctr('cQ0')
        p.cQ1 = Ctr('cQ1')
        p.cX = Ctr('cX')
        p.cD = Ctr('cD')
        p.cD1 = Ctr('cD1')
        p.cA = Ctr('cA')
        p.dma_ctrs = [p.cS, p.cG, p.cM, p.cW0, p.cW1, p.cE0, p.cE1, p.cE2,
                      p.cKV, p.cKV1, p.cQ0, p.cQ1, p.cX, p.cD, p.cD1, p.cA]
        p.sT = Ctr('sT')
        p.sS = Ctr('sS')
        p.sV = Ctr('sV')
        p.sV2 = Ctr('sV2')
        p.sS2 = Ctr('sS2')
        p.sCC = Ctr('sCC')

        p.blk = es.enter_context(nc.Block())
        p.ps = es.enter_context(nc.psum_tensor('arena', [128, 4096], F32))
        p.x_tok = 0
        p.po_free = 0
        p.bankS = {}
        p.evS = {}

        _prologue(p)
        if phases >= 1:
            _l0_attn(p)
        if phases >= 2:
            _l0_ffn(p)
        if phases >= 3:
            _l1_attn(p)
        if phases >= 4:
            _l1_ffn(p)
        _epilogue(p)
    return nc


def _bank_bf(p, i, cols=128):
    return p.ps[:, i * 512:i * 512 + cols // 2].bitcast(BF16)


def _prologue(p):
    nc, blk = p.nc, p.blk

    @blk.sync
    def _(s):
        for dst, src in ((p.edge, p.edge_in), (p.patch, p.patch_in),
                         (p.ident, p.ident_in), (p.onesc, p.onesc_in),
                         (p.onessq, p.onessq_in), (p.onesr, p.onesr_in),
                         (p.onescb, p.onescb_in), (p.onesqb, p.onesqb_in)):
            s.dma_start(out=dst[:, :], in_=src[:, :]).then_inc(p.cS.h, 16)
            p.cS.n += 16
        s.dma_start(out=p.xsb[:, :, :], in_=p.x0own[:, :, :]).then_inc(p.cS.h, 16)
        p.cS.n += 16

    vals = [-SHIFT, ATT_SCALE, LN_EPS, -0.5, 2.0]

    @blk.vector
    def _(v):
        for i, val in enumerate(vals):
            _li1 = v.memset(p.consts[:, i:i + 1], float(val))
        _li1.then_inc(p.sV.h, 1)
    p.sV.n += 1
    for i, val in enumerate(vals):
        nc.const_aps.register(F32, float(val), p.consts[:, i:i + 1])

    tok = p.cS.n
    tokv = p.sV.n
    for eng in ('tensor', 'scalar', 'vector', 'gpsimd', 'sync'):
        def _w(e, tok=tok, tokv=tokv):
            e.wait_ge(p.cS.h, tok)
            e.wait_ge(p.sV.h, tokv)
        getattr(blk, eng)(_w)


def _epilogue(p):
    blk = p.blk
    tokV, tokT, tokS = p.sV.n, p.sT.n, p.sS.n

    @blk.sync
    def _(s):
        s.wait_ge(p.sV.h, tokV)
        s.wait_ge(p.sT.h, tokT)
        s.wait_ge(p.sS.h, tokS)
        s.dma_start(out=p.out_fm[:, :, :], in_=p.xsb[:, :, :]).then_inc(p.cS.h, 16)
        p.cS.n += 16
        s.wait_ge(p.cS.h, p.cS.n)


def _load_masks(p, base, wait_v):
    @p.blk.sync
    def _(s):
        s.wait_ge(p.sV.h, wait_v)
        s.dma_start(out=p.masks[:, :, :], in_=p.masks_in[:, base:base + 8, :]).then_inc(p.cM.h, 16)
    p.cM.n += 16
    return p.cM.n



def _phase_barrier(p):
    blk = p.blk
    toks = [(c.h, c.n) for c in p.dma_ctrs if c.n]
    for eng in ('tensor', 'scalar', 'vector', 'gpsimd', 'sync'):
        def _w(e, toks=toks):
            for h, n in toks:
                e.wait_ge(h, n)
        getattr(blk, eng)(_w)



def _proj_to_scratch(p, wkey, dst, nheads, col0=0):
    nc, blk = p.nc, p.blk
    _phase_barrier(p)
    with nc.sbuf_tensor(_u('pw2'), [128, 2, CC, 256], BF16) as pw2, \
         nc.sbuf_tensor(_u('pxt'), [128, 2, CC, SBW], BF16) as pxt:
        win = p.w[wkey]
        wuse = {}
        xuse = {}
        ngr = (nheads + 1) // 2
        for hg in range(ngr):
            nh = min(2, nheads - hg * 2)
            ws = hg % 2
            cw = p.cW0 if ws == 0 else p.cW1
            prev = wuse.get(hg - 2, 0)

            @blk.gpsimd
            def _(g, hg=hg, nh=nh, ws=ws, prev=prev, cw=cw):
                g.wait_ge(p.sT.h, prev)
                g.dma_start(out=pw2[:, ws, :, 0:nh * 128],
                            in_=win[:, :, col0 + hg * 256:col0 + hg * 256 + nh * 128]).then_inc(cw.h, 16)
            cw.n += 16
            tokW = cw.n

            for tsb in range(NSB):
                it = hg * NSB + tsb
                xs = it % 2
                xprev = xuse.get(it - 2, 0)
                xt = p.x_tok

                @blk.vector
                def _(v, tsb=tsb, xs=xs, xprev=xprev, xt=xt):
                    v.wait_ge(p.sV.h, xt)
                    v.wait_ge(p.sT.h, xprev)
                    v.tensor_copy(pxt[:, xs, :, :],
                                  p.xsb[:, :, tsb * SBW:(tsb + 1) * SBW]).then_inc(p.sV2.h, 1)
                p.sV2.n += 1
                tokX = p.sV2.n

                for hh in range(nh):
                    h = hg * 2 + hh
                    bk = 4 + (hh % 2)
                    ev = hh % 2
                    bs = p.bankS.get(bk, 0)

                    @blk.tensor
                    def _(t, hh=hh, ws=ws, xs=xs, bk=bk, tokW=tokW, tokX=tokX,
                          bs=bs, cw=cw):
                        t.wait_ge(cw.h, tokW)
                        t.wait_ge(p.sV2.h, tokX)
                        t.wait_ge(p.sS.h, bs)
                        for cc in range(CC):
                            _lip = t.matmul(_bank(p, bk), pw2[:, ws, cc, hh * 128:(hh + 1) * 128],
                                     pxt[:, xs, cc, :], start=(cc == 0), stop=(cc == CC - 1))
                        _lip.then_inc(p.sT.h, 1)
                    p.sT.n += 1
                    tokMM = p.sT.n
                    evfree = p.evS.get(ev, 0)
                    ce = p.cE0 if ev == 0 else p.cE1

                    @blk.scalar
                    def _(s, ev=ev, bk=bk, tokMM=tokMM, evfree=evfree, ce=ce):
                        s.wait_ge(p.sT.h, tokMM)
                        s.wait_ge(ce.h, evfree)
                        s.copy(out=p.evac[:, ev, :], in_=_bank(p, bk)).then_inc(p.sS.h, 1)
                    p.sS.n += 1
                    p.bankS[bk] = p.sS.n
                    tokEv = p.sS.n

                    @blk.sync
                    def _(s, h=h, tsb=tsb, ev=ev, tokEv=tokEv, ce=ce):
                        s.wait_ge(p.sS.h, tokEv)
                        s.dma_start(out=dst[h, :, tsb * SBW:(tsb + 1) * SBW],
                                    in_=p.evac[:, ev, :]).then_inc(ce.h, 16)
                    ce.n += 16
                    p.evS[ev] = ce.n
                xuse[it] = p.sT.n
            wuse[hg] = p.sT.n
    return None



def _tok_score(T0, i, k_):
    if i == 0:
        return T0 + 1
    return T0 + (i + 1) + (i - 1) * (k_ - 1)


def _tok_den(T0, cv, n, k_):
    if cv >= n - 1:
        return T0 + n * k_
    return T0 + (cv + 2) + (cv + 1) * (k_ - 1)


def _attention(p, chunks_of, mask_base, kload, vload, do_oproj, ow_key,
               l1_edge=False, kv_shared=False, kv_wait=None):
    nc, blk = p.nc, p.blk
    _phase_barrier(p)
    with nc.sbuf_tensor(_u('akb'), [128, 2, NKC, HD], BF16) as akb, \
         nc.sbuf_tensor(_u('avb'), [128, 2, NKC, HD], BF16) as avb, \
         nc.sbuf_tensor(_u('aqb'), [128, 2, SBW], BF16) as aqb, \
         nc.sbuf_tensor(_u('apt'), [128, 8, SBW], BF16) as apt, \
         nc.sbuf_tensor(_u('aden'), [128, SBW], F32) as aden, \
         nc.sbuf_tensor(_u('aosb'), [128, H, SBW], BF16) as aosb, \
         nc.sbuf_tensor(_u('aow'), [128, 2, 1024], BF16) as aow:

        ufree = {}
        owfree = {}
        bpl = {}
        p0done = {}
        kvspan = {}
        for sb in range(NSB):
            chunks = chunks_of(sb)
            n = len(chunks)
            sv = l1_edge and sb == 0
            sbsl = slice(sb * SBW, (sb + 1) * SBW)

            for h in range(H):
                u = sb * H + h
                qub = h % 2
                ft = ufree.get(u - 2, 0)
                ftk = ufree.get(u - 1, 0)
                ftk0 = p0done.get(u - 1, ftk)
                loads = []
                if kv_shared:
                    if h == 0:
                        for pi in (0, 1):
                            js = sorted(j for (pj, j) in chunks if pj == pi)
                            j1 = js[-1] + 1
                            d0 = kvspan.get(pi, 0)
                            if j1 > d0:
                                loads.append((pi, d0, j1, 0))
                                kvspan[pi] = j1
                else:
                    for pi in (0, 1):
                        js = sorted(j for (pj, j) in chunks if pj == pi)
                        loads.append((pi, js[0], js[-1] + 1,
                                      ftk0 if pi == 0 else ftk))

                ltk0 = [0]

                @blk.sync
                def _(s, h=h, qub=qub, sb=sb, ft=ft, loads=loads, sbsl=sbsl):
                    if ft:
                        s.wait_ge(p.sT.h, ft)
                    cq = p.cQ0 if qub == 0 else p.cQ1
                    s.dma_start(out=aqb[:, qub, :], in_=p.q_scr[h, :, sbsl]).then_inc(cq.h, 16)
                    cq.n += 16
                    if loads and kv_wait is not None:
                        for _kwh, _kwv in kv_wait:
                            s.wait_ge(_kwh, _kwv)
                    hh = 0 if kv_shared else h
                    for pi, j0, j1, wtk in loads:
                        ck = p.cKV if pi == 0 else p.cKV1
                        if wtk:
                            s.wait_ge(p.sT.h, wtk)
                        kload(s, akb[:, pi, j0:j1, :], pi, j0, j1, hh)
                        ck.n += 16
                        vload(s, avb[:, pi, j0:j1, :], pi, j0, j1, hh)
                        ck.n += 16
                        if pi == 0:
                            ltk0[0] = ck.n
                if any(pi == 0 for pi, _a, _b, _c in loads):
                    p.last_ltk0 = p.cKV.n
                ltk0[0] = p.cKV.n if not kv_shared else p.last_ltk0
                ltk = p.cKV1.n
                ltq = (p.cQ0 if qub == 0 else p.cQ1).n

                T0, S0 = p.sT.n, p.sS.n
                sv2b = p.sV2.n
                xt_guard = 0
                if h == 0:
                    xt_guard = p.po_free if (do_oproj and sb > 0) else p.x_tok
                pof = p.po_free if (h == 0 and do_oproj) else 0
                k_ = 4 if sv else 3
                bp = 0 if (sv or u % 2 == 0) else 1
                bav, bdn = (6, 7) if bp == 0 else (4, 5)
                diag = [(_diag_d(sb, j) if mask_base is not None else None)
                        for (_pi, j) in chunks]
                ndth = []
                _c = 0
                for _dd in diag:
                    if _dd is not None:
                        _c += 1
                    ndth.append(_c)
                bpw = bpl.get(bp)

                cq = p.cQ0 if qub == 0 else p.cQ1

                @blk.tensor
                def _(t, sb=sb, h=h, qub=qub, ltk=ltk, ltk0v=ltk0[0], ltq=ltq,
                      cq=cq, chunks=chunks, S0=S0, sv=sv, n=n, diag=diag,
                      ndth=ndth, sv2b=sv2b, bav=bav, bdn=bdn, bpw=bpw,
                      xt_guard=xt_guard, pof=pof):
                    t.wait_ge(p.cKV.h, ltk0v)
                    t.wait_ge(cq.h, ltq)
                    if xt_guard:
                        t.wait_ge(p.sV.h, xt_guard)
                    if pof:
                        t.wait_ge(p.sV.h, pof)

                    def consume(cv, last):
                        pv, jv = chunks[cv]
                        if cv == 0 and bpw is not None:
                            t.wait_ge(bpw[0], bpw[1])
                        if diag[cv] is not None:
                            t.wait_ge(p.sV2.h, sv2b + ndth[cv])
                        else:
                            t.wait_ge(p.sS.h, S0 + cv + 1)
                        t.matmul(_bank(p, bav), avb[:, pv, jv, :],
                                 apt[:, cv % 8, :],
                                 start=(cv == 0), stop=last).then_inc(p.sT.h, 1)
                        if sv:
                            t.matmul(p.ps[:, 2048 + h:2049 + h],
                                     avb[:, pv, jv, :], p.onescb[:, :],
                                     start=(cv == 0), stop=last).then_inc(p.sT.h, 1)
                        t.matmul(p.ps[:, bdn * 512:bdn * 512 + SBW],
                                 p.onesqb[:, :], apt[:, cv % 8, :],
                                 start=(cv == 0), stop=last).then_inc(p.sT.h, 1)
                    first_p1 = next((ii for ii, (pii, _) in enumerate(chunks) if pii == 1), None)
                    for i, (pi, j) in enumerate(chunks):
                        if i == first_p1:
                            t.wait_ge(p.cKV1.h, ltk)
                        if i >= 4:
                            t.wait_ge(p.sS.h, S0 + i - 3)
                        t.matmul(_bank(p, i % 4),
                                 akb[:, pi, j, :], aqb[:, qub, :],
                                 start=True, stop=True).then_inc(p.sT.h, 1)
                        if i >= 1:
                            consume(i - 1, False)
                    consume(n - 1, True)
                p.sT.n += n * k_
                tok_den = p.sT.n
                ufree[u] = tok_den
                npi0 = sum(1 for (_pi, _j) in chunks if _pi == 0)
                p0done[u] = _tok_den(T0, npi0 - 1, n, k_) if npi0 < n else tok_den

                @blk.scalar
                def _(s, T0=T0, n=n, k_=k_):
                    for i in range(n):
                        s.wait_ge(p.sT.h, _tok_score(T0, i, k_))
                        if i >= 8:
                            s.wait_ge(p.sT.h, _tok_den(T0, i - 8, n, k_))
                        s.activation(out=apt[:, i % 8, :], in_=_bank(p, i % 4),
                                     func=AF.Exp, bias=-SHIFT, scale=ATT_SCALE).then_inc(p.sS.h, 1)
                p.sS.n += n

                @blk.vector
                def _(v, chunks=chunks, S0=S0, sb=sb, h=h, diag=diag,
                      tok_den=tok_den, n=n, sv2base=sv2b, bav=bav, bdn=bdn,
                      nvt=p.sV.n + 1, nvb=p.sV.n):
                    sv2 = sv2base
                    for i, (pi, j) in enumerate(chunks):
                        if diag[i] is not None:
                            v.wait_ge(p.sS.h, S0 + i + 1)
                            v.tensor_mul(apt[:, i % 8, :], apt[:, i % 8, :],
                                         p.masks[:, pi * 4 + diag[i], :]).then_inc(p.sV2.h, 1)
                            sv2 += 1
                    v.wait_ge(p.sT.h, tok_den)
                    if l1_edge and sb == 3:
                        v.tensor_tensor(p.ps[:, bdn * 512 + SBW - 1:bdn * 512 + SBW],
                                        p.ps[:, bdn * 512 + SBW - 1:bdn * 512 + SBW],
                                        p.patch[:, :], ALU.add).then_inc(p.sV2.h, 1)
                        sv2 += 1
                        v.wait_ge(p.sV2.h, sv2)
                    v.wait_ge(p.sV.h, nvb)
                    v.reciprocal(aden[:, :], p.ps[:, bdn * 512:bdn * 512 + SBW]).then_inc(p.sV2.h, 1)
                    sv2 += 1
                    v.wait_ge(p.sV2.h, sv2)
                    v.tensor_tensor(aosb[:, h, :], _bank(p, bav), aden[:, :],
                                    ALU.mult).then_inc(p.sV.h, 1)
                    if l1_edge and sb == 3:
                        v.wait_ge(p.sV.h, nvt)
                        v.scalar_tensor_tensor(
                            aosb[:, h, SBW - 1:SBW], aosb[:, h, SBW - 1:SBW],
                            p.edge[:, 0:1], p.sumvb[:, h:h + 1],
                            ALU.mult, ALU.add).then_inc(p.sV.h, 1)
                ndiag = sum(1 for _dd in diag if _dd is not None)
                p.sV2.n += ndiag + (1 if (l1_edge and sb == 3) else 0) + 1
                p.sV.n += 1 + (1 if (l1_edge and sb == 3) else 0)
                bpl[bp] = (p.sV.h, p.sV.n)
            ev_all = p.sV.n

            if sv:
                tokT = p.sT.n

                @blk.vector
                def _(v, tokT=tokT, sv2base=p.sV2.n):
                    v.wait_ge(p.sT.h, tokT)
                    v.tensor_copy(p.sumv[:, :], p.ps[:, 2048:2048 + H]).then_inc(p.sV2.h, 1)
                    v.wait_ge(p.sV2.h, sv2base + 1)
                    v.scalar_tensor_tensor(p.sumvb[:, :], p.sumv[:, :],
                                           p.edge[:, 1:2], p.sumv[:, :],
                                           ALU.mult, ALU.bypass).then_inc(p.sV.h, 1)
                p.sV2.n += 1
                p.sV.n += 1
                bpl[1] = (p.sV.h, p.sV.n)
                ev_all = p.sV.n

            if do_oproj:
                win = p.w[ow_key]
                rtoks = []
                for half in (0, 1):
                    for cc in range(CC):
                        ccb = cc % 2
                        ft = owfree.get(ccb, 0)

                        cw = p.cW0 if ccb == 0 else p.cW1

                        @blk.gpsimd
                        def _(g, cc=cc, half=half, ccb=ccb, ft=ft, cw=cw):
                            if ft:
                                g.wait_ge(p.sT.h, ft)
                            g.dma_start(out=aow[:, ccb, :],
                                        in_=win[:, cc, half * 1024:(half + 1) * 1024]).then_inc(cw.h, 16)
                        cw.n += 16
                        tokW = cw.n
                        rt0 = rtoks[0] if rtoks else 0

                        @blk.tensor
                        def _(t, cc=cc, half=half, ccb=ccb, tokW=tokW,
                              ev_all=ev_all, rt0=rt0, cw=cw):
                            t.wait_ge(cw.h, tokW)
                            if cc == 0:
                                if half == 0:
                                    t.wait_ge(p.sV.h, ev_all)
                                    t.wait_ge(p.sV.h, p.po_free)
                                else:
                                    t.wait_ge(p.sV.h, rt0)
                            for d2c in range(8):
                                _li3 = t.matmul(p.ps[:, d2c * 512:d2c * 512 + SBW],
                                         aow[:, ccb, d2c * 128:(d2c + 1) * 128],
                                         aosb[:, cc, :],
                                         start=(cc == 0), stop=(cc == CC - 1))
                            _li3.then_inc(p.sT.h, 1)
                        p.sT.n += 1
                        owfree[ccb] = p.sT.n
                    tok_half = p.sT.n

                    @blk.vector
                    def _(v, half=half, sbsl=sbsl, tok_half=tok_half):
                        v.wait_ge(p.sT.h, tok_half)
                        for d2c in range(8):
                            cc2 = half * 8 + d2c
                            _li4 = v.tensor_tensor(p.xsb[:, cc2, sbsl], p.xsb[:, cc2, sbsl],
                                            p.ps[:, d2c * 512:d2c * 512 + SBW], ALU.add)
                        _li4.then_inc(p.sV.h, 1)
                    p.sV.n += 1
                    rtoks.append(p.sV.n)
                p.po_free = rtoks[-1]
                _ln_only(p, sbsl, rtoks[-1])
                bpl[0] = (p.sV2.h, p.ln_b67)
                bpl[1] = (p.sV2.h, p.ln_b45)
            else:
                _res_ln(p, sb, from_po=False, y_sb=aosb, ytok=ev_all)



def _res_ln(p, sb, from_po, y_sb, ytok):
    assert not from_po
    nc, blk = p.nc, p.blk
    sbsl = slice(sb * SBW, (sb + 1) * SBW)

    @blk.vector
    def _(v, sb=sb, ytok=ytok):
        if from_po:
            v.wait_ge(p.sT.h, ytok)
            for cc in range(CC):
                _li5 = v.tensor_tensor(p.xsb[:, cc, sbsl], p.xsb[:, cc, sbsl],
                                p.ps[:, cc * 128:(cc + 1) * 128], ALU.add)
        else:
            v.wait_ge(p.sV.h, ytok)
            for cc in range(CC):
                _li5 = v.tensor_tensor(p.xsb[:, cc, sbsl], p.xsb[:, cc, sbsl],
                                y_sb[:, cc, :], ALU.add)
        _li5.then_inc(p.sV.h, 1)
    p.sV.n += 1
    tok_r = p.sV.n
    if from_po:
        p.po_free = tok_r
    _ln_only(p, sbsl, tok_r)


def _ln_only(p, sbsl, tok_r):
    nc, blk = p.nc, p.blk
    S0 = p.sS.n
    T0sq = p.sT.n + 1

    @blk.scalar
    def _(s, tok_r=tok_r, T0sq=T0sq):
        s.wait_ge(p.sV.h, tok_r)
        for cc in range(CC):
            if cc >= 1:
                s.wait_ge(p.sT.h, T0sq + cc)
            s.activation(out=p.sqbuf[:, 0, :], in_=p.xsb[:, cc, sbsl],
                         func=AF.Square).then_inc(p.sS.h, 1)
    p.sS.n += CC

    @blk.tensor
    def _(t, tok_r=tok_r, S0=S0):
        t.wait_ge(p.sV.h, tok_r)
        for cc in range(CC):
            _li6 = t.matmul(p.ps[0:1, 2048:2048 + SBW], p.onesc[:, :], p.xsb[:, cc, sbsl],
                     start=(cc == 0), stop=(cc == CC - 1))
        _li6.then_inc(p.sT.h, 1)
        for cc in range(CC):
            t.wait_ge(p.sS.h, S0 + cc + 1)
            t.matmul(p.ps[0:1, 2560:2560 + SBW], p.onesc[:, :], p.sqbuf[:, 0, :],
                     start=(cc == 0), stop=(cc == CC - 1)).then_inc(p.sT.h, 1)
    p.sT.n += 1 + CC
    tok_stats = p.sT.n

    p.ln_b45 = p.sV2.n + 2

    @blk.vector
    def _(v, tok_stats=tok_stats, sv2b=p.sV2.n):
        v.wait_ge(p.sT.h, tok_stats)
        v.tensor_scalar_mul(p.lnm[0:1, :], p.ps[0:1, 2048:2048 + SBW], 1.0 / D).then_inc(p.sV2.h, 1)
        v.tensor_scalar_mul(p.lnt1[0:1, :], p.ps[0:1, 2560:2560 + SBW], 1.0 / D).then_inc(p.sV2.h, 1)
        v.wait_ge(p.sV2.h, sv2b + 2)
        v.tensor_mul(p.lnt2[0:1, :], p.lnm[0:1, :], p.lnm[0:1, :]).then_inc(p.sV2.h, 1)
        v.wait_ge(p.sV2.h, sv2b + 3)
        v.tensor_sub(p.lnt1[0:1, :], p.lnt1[0:1, :], p.lnt2[0:1, :]).then_inc(p.sV2.h, 1)
        v.wait_ge(p.sV2.h, sv2b + 4)
        v.tensor_scalar_add(p.lnt1[0:1, :], p.lnt1[0:1, :], LN_EPS).then_inc(p.sV.h, 1)
    p.sV2.n += 4
    p.sV.n += 1
    tok_var = p.sV.n

    @blk.scalar
    def _(s, tok_var=tok_var, ss2b=p.sS2.n):
        s.wait_ge(p.sV.h, tok_var)
        s.activation(out=p.lnt2[0:1, :], in_=p.lnt1[0:1, :], func=AF.Ln).then_inc(p.sS2.h, 1)
        s.wait_ge(p.sS2.h, ss2b + 1)
        s.activation(out=p.lns[0:1, :], in_=p.lnt2[0:1, :], func=AF.Exp,
                     scale=-0.5).then_inc(p.sS.h, 1)
    p.sS2.n += 1
    p.sS.n += 1
    tok_s0 = p.sS.n

    @blk.vector
    def _(v, tok_s0=tok_s0, sv2b=p.sV2.n):
        v.wait_ge(p.sS.h, tok_s0)
        v.tensor_mul(p.lnt2[0:1, :], p.lns[0:1, :], p.lns[0:1, :]).then_inc(p.sV2.h, 1)
        v.wait_ge(p.sV2.h, sv2b + 1)
        v.tensor_mul(p.lnt2[0:1, :], p.lnt2[0:1, :], p.lnt1[0:1, :]).then_inc(p.sV2.h, 1)
        v.wait_ge(p.sV2.h, sv2b + 2)
        v.tensor_scalar(p.lnt2[0:1, :], p.lnt2[0:1, :], -0.5, 1.5, ALU.mult, ALU.add).then_inc(p.sV2.h, 1)
        v.wait_ge(p.sV2.h, sv2b + 3)
        v.tensor_mul(p.lns[0:1, :], p.lns[0:1, :], p.lnt2[0:1, :]).then_inc(p.sV2.h, 1)
        v.wait_ge(p.sV2.h, sv2b + 4)
        v.tensor_mul(p.lnm[0:1, :], p.lnm[0:1, :], p.lns[0:1, :]).then_inc(p.sV.h, 1)
    p.sV2.n += 4
    p.sV.n += 1
    tok_s = p.sV.n

    @blk.tensor
    def _(t, tok_s=tok_s):
        t.wait_ge(p.sV.h, tok_s)
        t.matmul(p.ps[:, 3072:3072 + SBW], p.onesr[:, :], p.lns[0:1, :], start=True, stop=True)
        t.matmul(p.ps[:, 3584:3584 + SBW], p.onesr[:, :], p.lnm[0:1, :], start=True, stop=True).then_inc(p.sT.h, 1)
    p.sT.n += 1
    tok_bc = p.sT.n

    p.ln_b67 = p.sV2.n + 1

    @blk.vector
    def _(v, tok_bc=tok_bc, sv2b=p.sV2.n):
        v.wait_ge(p.sT.h, tok_bc)
        v.tensor_copy(p.sbst[:, 0, :], p.ps[:, 3072:3072 + SBW])
        v.tensor_copy(p.sbst[:, 1, :], p.ps[:, 3584:3584 + SBW]).then_inc(p.sV2.h, 1)
        v.wait_ge(p.sV2.h, sv2b + 1)
        for cc in range(CC):
            _li10 = v.tensor_mul(p.xsb[:, cc, sbsl], p.xsb[:, cc, sbsl], p.sbst[:, 0, :])
        _li10.then_inc(p.sV2.h, 1)
        v.wait_ge(p.sV2.h, sv2b + 2)
        for cc in range(CC):
            _li11 = v.tensor_sub(p.xsb[:, cc, sbsl], p.xsb[:, cc, sbsl], p.sbst[:, 1, :])
        _li11.then_inc(p.sV.h, 1)
    p.sV2.n += 2
    p.sV.n += 1
    p.x_tok = p.sV.n



def _l0_attn(p):
    nc, blk = p.nc, p.blk
    _phase_barrier(p)

    with nc.sbuf_tensor(_u('kvw'), [128, CC, 2 * HD], BF16) as kvw, \
         nc.sbuf_tensor(_u('kvx'), [128, 2, CC, SBW], BF16) as kvx:

        @blk.gpsimd
        def _(g):
            g.dma_start(out=kvw[:, :, 0:HD], in_=p.w['l0_kwT'][:, :, :]).then_inc(p.cG.h, 16)
            g.dma_start(out=kvw[:, :, HD:2 * HD], in_=p.w['l0_vwT'][:, :, :]).then_inc(p.cG.h, 16)
        p.cG.n += 32
        tokW = p.cG.n

        kst = {}
        xslot = {}
        x_pre = p.sT.n
        for tsb in range(NSB):
            sl = slice(tsb * SBW, (tsb + 1) * SBW)
            tb = tsb % 2
            tokT_prev = xslot.get(tsb - 2, x_pre)

            @blk.vector
            def _(v, sl=sl, tb=tb, tokT_prev=tokT_prev):
                v.wait_ge(p.sT.h, tokT_prev)
                v.tensor_copy(kvx[:, tb, :, :], p.xsb[:, :, sl]).then_inc(p.sV2.h, 1)
            p.sV2.n += 1
            tokX = p.sV2.n
            b4, b5 = p.bankS.get(4, 0), p.bankS.get(5, 0)

            @blk.tensor
            def _(t, tb=tb, tokX=tokX, tokW=tokW, b4=b4, b5=b5):
                t.wait_ge(p.cG.h, tokW)
                t.wait_ge(p.sV2.h, tokX)
                t.wait_ge(p.sS.h, max(b4, b5))
                for cc in range(CC):
                    _li11 = t.matmul(_bank(p, 4), kvw[:, cc, 0:HD], kvx[:, tb, cc, :],
                             start=(cc == 0), stop=(cc == CC - 1))
                _li11.then_inc(p.sT.h, 1)
                for cc in range(CC):
                    _li12 = t.matmul(_bank(p, 5), kvw[:, cc, HD:2 * HD], kvx[:, tb, cc, :],
                             start=(cc == 0), stop=(cc == CC - 1))
                _li12.then_inc(p.sT.h, 1)
            p.sT.n += 2
            tokKV = p.sT.n
            xslot[tsb] = tokKV

            @blk.scalar
            def _(s, tokKV=tokKV, pk=kst.get('k', 0)):
                s.wait_ge(p.sT.h, tokKV)
                s.wait_ge(p.cE0.h, pk)
                s.copy(out=p.evac[:, 0, :], in_=_bank(p, 4)).then_inc(p.sS.h, 1)
                s.copy(out=p.evac[:, 1, :], in_=_bank(p, 5)).then_inc(p.sS.h, 1)
            p.sS.n += 2
            p.bankS[4] = p.sS.n - 1
            p.bankS[5] = p.sS.n
            tokEv = p.sS.n

            @blk.sync
            def _(s, tsb=tsb, tokEv=tokEv):
                s.wait_ge(p.sS.h, tokEv)
                s.dma_start(out=p.ag_kv_in[0, :, tsb * 4:(tsb + 1) * 4, :],
                            in_=p.evac[:, 0, :].rearrange('p (c d) -> p c d', d=HD)).then_inc(p.cE0.h, 16)
            p.cE0.n += 16
            kst['k'] = p.cE0.n

            for tt in (range(4) if _SUB not in ('l0a_prodK',) else []):
                b7 = p.bankS.get(7, 0)

                @blk.tensor
                def _(t, tt=tt, tokEv=tokEv, b7=b7):
                    t.wait_ge(p.sS.h, tokEv)
                    t.wait_ge(p.sS.h, b7)
                    t.transpose(_bank_bf(p, 7),
                                p.evac[:, 1, tt * 128:(tt + 1) * 128], p.ident[:, :]).then_inc(p.sT.h, 1)
                p.sT.n += 1
                tokTr = p.sT.n

                @blk.scalar
                def _(s, tt=tt, tokTr=tokTr, pv=kst.get('v', 0)):
                    s.wait_ge(p.sT.h, tokTr)
                    s.wait_ge(p.cE2.h, pv)
                    s.copy(out=p.evac[:, 2, tt * 128:(tt + 1) * 128],
                           in_=_bank_bf(p, 7)).then_inc(p.sS.h, 1)
                p.sS.n += 1
                p.bankS[7] = p.sS.n
            tokVt = p.sS.n

            if _SUB not in ('l0a_prodK',):
                @blk.sync
                def _(s, tsb=tsb, tokVt=tokVt):
                    s.wait_ge(p.sS.h, tokVt)
                    s.dma_start(out=p.ag_kv_in[1, :, tsb * 4:(tsb + 1) * 4, :],
                                in_=p.evac[:, 2, :].rearrange('p (c d) -> p c d', d=HD)).then_inc(p.cE2.h, 16)
                p.cE2.n += 16
                kst['v'] = p.cE2.n
        tok_k, tok_v = p.cE0.n, p.cE2.n

    if _SUB in ('l0a_prod', 'l0a_prodK'):
        @blk.gpsimd
        def _(g, tok_k=tok_k, tok_v=tok_v):
            g.wait_ge(p.cE0.h, tok_k)
            g.wait_ge(p.cE2.h, tok_v)
        return

    @blk.gpsimd
    def _(g, tok_k=tok_k, tok_v=tok_v):
        g.wait_ge(p.cE0.h, tok_k)
        g.wait_ge(p.cE2.h, tok_v)
        g.collective_compute('AllGather', ALU.bypass, replica_groups=p.RG,
                             ins=[p.ag_kv_in[:, :, :, :].opt()],
                             outs=[p.ag_kv_out[:, :, :, :, :].opt()]).then_inc(p.sCC.h, 1)
    p.sCC.n += 1
    kv_ready = p.sCC.n

    if _SUB == 'l0a_kv':
        return
    _proj_to_scratch(p, 'l0_qwT', p.q_scr, H)
    if _SUB == 'l0a_kvq':
        return

    def kload(s, dst, pi, j0, j1, h):
        ck = p.cKV if pi == 0 else p.cKV1
        s.dma_start(out=dst, in_=p.ag_kv_out[pi, 0, :, j0:j1, :]).then_inc(ck.h, 16)

    def vload(s, dst, pi, j0, j1, h):
        ck = p.cKV if pi == 0 else p.cKV1
        s.dma_start(out=dst, in_=p.ag_kv_out[pi, 1, :, j0:j1, :]).then_inc(ck.h, 16)

    _attention(p, _full_chunks, None, kload, vload, do_oproj=True,
               ow_key='l0_owT', kv_shared=True, kv_wait=[(p.sCC.h, kv_ready)])



def _l0_ffn(p):
    nc, blk = p.nc, p.blk
    _phase_barrier(p)
    mask_tok = _load_masks(p, 0, p.x_tok)

    with nc.sbuf_tensor(_u('gw'), [128, 2, CC, H], BF16) as gw, \
         nc.sbuf_tensor(_u('gwf'), [128, CC, H], F32) as gwf, \
         nc.sbuf_tensor(_u('fxh'), [128, 2, CC, 128], BF16) as fxh, \
         nc.sbuf_tensor(_u('fxl'), [128, 2, CC, 128], BF16) as fxl, \
         nc.sbuf_tensor(_u('gind'), [128, NKC, H], F32) as gind, \
         nc.sbuf_tensor(_u('gtop'), [128, NKC, H], F32) as gtop, \
         nc.sbuf_tensor(_u('glg'), [128, 2, H], F32) as glg, \
         nc.sbuf_tensor(_u('glge'), [128, 2, H], F32) as glge, \
         nc.sbuf_tensor(_u('gmx'), [128, 2, 2], F32) as gmx, \
         nc.sbuf_tensor(_u('gden'), [128, 2, 1], F32) as gden, \
         nc.sbuf_tensor(_u('fkw'), [128, CC, 512], BF16) as fkw, \
         nc.sbuf_tensor(_u('fxk'), [128, 2, CC, 128], BF16) as fxk, \
         nc.sbuf_tensor(_u('fsh'), [128, 2, NKC, HD], BF16) as fsh:

        @blk.gpsimd
        def _(g):
            g.dma_start(out=gwf[:, :, :], in_=p.w['l0_wgT'][:, :, :]).then_inc(p.cG.h, 16)
            g.dma_start(out=gw[:, 0, :, :], in_=p.w['l0_wgT'][:, :, :]).then_inc(p.cG.h, 16)
        p.cG.n += 32
        tokGW = p.cG.n

        @blk.vector
        def _(v, tokGW=tokGW):
            v.wait_ge(p.cG.h, tokGW)
            v.tensor_sub(gw[:, 1, :, :], gwf[:, :, :], gw[:, 0, :, :]).then_inc(p.sV.h, 1)
        p.sV.n += 1
        tokGWL = p.sV.n

        gmmtok = {}
        glgcopy = {}
        expdone = {}
        gtdone = {}
        gmaxtok = {}
        g_pre = p.sT.n
        for tc in range(NKC):
            tsl = slice(tc * 128, (tc + 1) * 128)
            tb = tc % 2
            tokT_prev = gmmtok.get(tc - 2, g_pre)
            xt = p.x_tok

            @blk.scalar
            def _(s, tsl=tsl, tb=tb, tokT_prev=tokT_prev, xt=xt):
                s.wait_ge(p.sV.h, xt)
                s.wait_ge(p.sT.h, tokT_prev)
                s.copy(out=fxh[:, tb, :, :], in_=p.xsb[:, :, tsl]).then_inc(p.sS.h, 1)
            p.sS.n += 1
            tokXH = p.sS.n

            @blk.vector
            def _(v, tsl=tsl, tb=tb, tokXH=tokXH):
                v.wait_ge(p.sS.h, tokXH)
                v.scalar_tensor_tensor(fxl[:, tb, :, :], fxh[:, tb, :, :], -1.0,
                                       p.xsb[:, :, tsl], ALU.mult, ALU.add).then_inc(p.sV.h, 1)
            p.sV.n += 1
            tokXL = p.sV.n
            gbk = 6 + tb
            bkg = glgcopy.get(tc - 2, 0)

            @blk.tensor
            def _(t, tb=tb, gbk=gbk, tokXL=tokXL, tokGWL=tokGWL, bkg=bkg):
                t.wait_ge(p.sV.h, max(tokXL, tokGWL, bkg))
                for cc in range(CC):
                    t.matmul(_bank(p, gbk, H), fxh[:, tb, cc, :], gw[:, 0, cc, :],
                             start=(cc == 0), stop=False)
                for cc in range(CC):
                    t.matmul(_bank(p, gbk, H), fxh[:, tb, cc, :], gw[:, 1, cc, :],
                             start=False, stop=False)
                for cc in range(CC):
                    _li16 = t.matmul(_bank(p, gbk, H), fxl[:, tb, cc, :], gw[:, 0, cc, :],
                             start=False, stop=(cc == CC - 1))
                _li16.then_inc(p.sT.h, 1)
            p.sT.n += 1
            gmmtok[tc] = p.sT.n
            tokLG = p.sT.n

            @blk.vector
            def _(v, tc=tc, tb=tb, gbk=gbk, tokLG=tokLG, svb=p.sV.n,
                  sv2b=p.sV2.n, geprev=expdone.get(tc - 2, 0)):
                v.wait_ge(p.sT.h, tokLG)
                v.wait_ge(p.sS.h, geprev)
                v.tensor_copy(glg[:, tb, :], _bank(p, gbk, H)).then_inc(p.sV.h, 1)
                v.wait_ge(p.sV.h, svb + 1)
                v.tensor_reduce(gmx[:, tb, 0:1], glg[:, tb, :], mybir.AxisListType.X,
                                ALU.max).then_inc(p.sV2.h, 1)
                v.wait_ge(p.sV2.h, sv2b + 1)
                v.tensor_scalar_mul(gmx[:, tb, 1:2], gmx[:, tb, 0:1], -1.0).then_inc(p.sV2.h, 1)
                v.tensor_scalar(gind[:, tc, :], glg[:, tb, :], gmx[:, tb, 0:1], None,
                                ALU.is_ge).then_inc(p.sV.h, 1)
            p.sV2.n += 2
            p.sV.n += 2
            glgcopy[tc] = p.sV.n - 1
            gmaxtok[tc] = p.sV.n
            tokGMAX = p.sV.n

            @blk.scalar
            def _(s, tb=tb, tokGMAX=tokGMAX, gtprev=gtdone.get(tc - 2, 0)):
                s.wait_ge(p.sV.h, max(tokGMAX, gtprev))
                s.activation(out=glge[:, tb, :], in_=glg[:, tb, :], func=AF.Exp,
                             bias=gmx[:, tb, 1:2], accum_out=gden[:, tb, :]).then_inc(p.sS.h, 1)
            p.sS.n += 1
            tokGE = p.sS.n
            expdone[tc] = tokGE

            @blk.vector
            def _(v, tc=tc, tb=tb, tokGE=tokGE, sv2b=p.sV2.n):
                v.wait_ge(p.sS.h, tokGE)
                v.reciprocal(gden[:, tb, :], gden[:, tb, :]).then_inc(p.sV2.h, 1)
                v.wait_ge(p.sV2.h, sv2b + 1)
                v.tensor_scalar(gtop[:, tc, :], gind[:, tc, :], gden[:, tb, 0:1], None,
                                ALU.mult).then_inc(p.sV.h, 1)
            p.sV2.n += 1
            p.sV.n += 1
            gtdone[tc] = p.sV.n

        fkw_use = {}
        fxk_use = {}
        comb_done = {}
        kv_pre = p.sT.n
        for e in range(8):
            shi = 0 if e < 4 else 1
            hbase = (e % 4) * 4
            wprev = fkw_use.get(e - 1, kv_pre)

            @blk.gpsimd
            def _(g, e=e, wprev=wprev):
                g.wait_ge(p.sT.h, wprev)
                g.dma_start(out=fkw[:, :, :],
                            in_=p.w['l0_kvT'][:, :, e * 512:(e + 1) * 512]).then_inc(p.cW0.h, 16)
            p.cW0.n += 16
            tokKW = p.cW0.n

            for tc in range(NKC):
                tsl = slice(tc * 128, (tc + 1) * 128)
                tb = tc % 2
                le = e * NKC + tc
                xprev = fxk_use.get(le - 2, kv_pre)

                @blk.scalar
                def _(s, tsl=tsl, tb=tb, xprev=xprev):
                    s.wait_ge(p.sT.h, xprev)
                    s.copy(out=fxk[:, tb, :, :], in_=p.xsb[:, :, tsl]).then_inc(p.sS.h, 1)
                p.sS.n += 1
                tokXC = p.sS.n
                bk = 4 + (tc % 2)
                bkv = comb_done.get((tc % 2, le - 2), 0)

                @blk.tensor
                def _(t, tb=tb, bk=bk, tokXC=tokXC, tokKW=tokKW, bkv=bkv):
                    t.wait_ge(p.sS.h, tokXC)
                    t.wait_ge(p.cW0.h, tokKW)
                    t.wait_ge(p.sV.h, bkv)
                    for cc in range(CC):
                        _li20 = t.matmul(_bank(p, bk), fxk[:, tb, cc, :], fkw[:, cc, :],
                                 start=(cc == 0), stop=(cc == CC - 1))
                    _li20.then_inc(p.sT.h, 1)
                p.sT.n += 1
                tokKVm = p.sT.n
                fxk_use[le] = tokKVm
                if tc == NKC - 1:
                    fkw_use[e] = tokKVm

                @blk.vector
                def _(v, e=e, tc=tc, bk=bk, shi=shi, hbase=hbase, tokKVm=tokKVm,
                      gsel_tok=(gmaxtok[tc] if e < 4 else gtdone[tc]), sv2b=p.sV2.n):
                    v.wait_ge(p.sT.h, tokKVm)
                    v.wait_ge(p.sV.h, gsel_tok)
                    gsel = gind if e < 4 else gtop
                    first = (e % 4 == 0)
                    src4 = _bank(p, bk)
                    sv2 = sv2b
                    for hh in range(4):
                        h = hbase + hh
                        so = hh * 128
                        if not (first and hh == 0):
                            v.wait_ge(p.sV2.h, sv2)
                        if first and hh == 0:
                            _li21 = v.tensor_scalar(
                                fsh[:, shi, tc, :], src4[:, so:so + 128],
                                gsel[:, tc, h:h + 1], None, ALU.mult)
                        else:
                            _li21 = v.scalar_tensor_tensor(
                                fsh[:, shi, tc, :], src4[:, so:so + 128],
                                gsel[:, tc, h:h + 1], fsh[:, shi, tc, :],
                                ALU.mult, ALU.add)
                        _li21.then_inc(p.sV2.h, 1)
                        sv2 += 1
                    v.wait_ge(p.sV2.h, sv2)
                    v.engine_nop().then_inc(p.sV.h, 1)
                p.sV2.n += 4
                p.sV.n += 1
                comb_done[(tc % 2, le)] = p.sV.n
        tok_sel = p.sV.n

        with nc.sbuf_tensor(_u('fshT'), [128, NKC, HD], BF16) as fshT:
            for tc in range(NKC):
                b7 = p.bankS.get(7, 0)

                @blk.tensor
                def _(t, tc=tc, tok_sel=tok_sel, b7=b7):
                    t.wait_ge(p.sV.h, tok_sel)
                    t.wait_ge(p.sS.h, b7)
                    t.transpose(_bank_bf(p, 7), fsh[:, 0, tc, :], p.ident[:, :]).then_inc(p.sT.h, 1)
                p.sT.n += 1
                tokTr = p.sT.n

                @blk.scalar
                def _(s, tc=tc, tokTr=tokTr):
                    s.wait_ge(p.sT.h, tokTr)
                    s.copy(out=fshT[:, tc, :], in_=_bank_bf(p, 7)).then_inc(p.sS.h, 1)
                p.sS.n += 1
                p.bankS[7] = p.sS.n
            tokTall = p.sS.n

            @blk.sync
            def _(s, tokTall=tokTall, tok_sel=tok_sel):
                s.wait_ge(p.sS.h, tokTall)
                s.wait_ge(p.sV.h, tok_sel)
                s.dma_start(out=p.ag_kv_in[0, :, :, :], in_=fshT[:, :, :]).then_inc(p.cA.h, 16)
                s.dma_start(out=p.ag_kv_in[1, :, :, :], in_=fsh[:, 1, :, :]).then_inc(p.cA.h, 16)
            p.cA.n += 32
            tok_stores = p.cA.n

    @blk.gpsimd
    def _(g, tok_stores=tok_stores):
        g.wait_ge(p.cA.h, tok_stores)
        g.collective_compute('AllGather', ALU.bypass, replica_groups=p.RG,
                             ins=[p.ag_kv_in[:, :, :, :].opt()],
                             outs=[p.ag_kv_out[:, :, :, :, :].opt()]).then_inc(p.sCC.h, 1)
    p.sCC.n += 1
    kv_ready = p.sCC.n

    _proj_to_scratch(p, 'l0_qT', p.q_scr, H)

    def kload(s, dst, pi, j0, j1, h):
        ck = p.cKV if pi == 0 else p.cKV1
        s.dma_start(out=dst, in_=p.ag_kv_out[pi, 0, :, j0:j1, :]).then_inc(ck.h, 16)

    def vload(s, dst, pi, j0, j1, h):
        ck = p.cKV if pi == 0 else p.cKV1
        s.dma_start(out=dst, in_=p.ag_kv_out[pi, 1, :, j0:j1, :]).then_inc(ck.h, 16)

    _attention(p, _causal_chunks, 0, kload, vload, do_oproj=False,
               ow_key=None, kv_shared=True, kv_wait=[(p.sCC.h, kv_ready)])



def _l1_attn(p):
    nc, blk = p.nc, p.blk
    _phase_barrier(p)
    mask_tok = _load_masks(p, 8, p.x_tok)

    xt = p.x_tok
    with nc.sbuf_tensor(_u('lkw'), [128, 2, CC, 256], BF16) as lkw, \
         nc.sbuf_tensor(_u('lxt'), [128, 2, CC, SBW], BF16) as lxt:
        wuse = {}
        xuse = {}
        k_pre = p.sT.n
        for tsb in range(NSB):
            xs = tsb % 2
            xprev = xuse.get(tsb - 2, k_pre)
            tsl = slice(tsb * SBW, (tsb + 1) * SBW)

            @blk.vector
            def _(v, tsl=tsl, xs=xs, xprev=xprev, xt=xt):
                v.wait_ge(p.sV.h, xt)
                v.wait_ge(p.sT.h, xprev)
                v.tensor_copy(lxt[:, xs, :, :], p.xsb[:, :, tsl]).then_inc(p.sV2.h, 1)
            p.sV2.n += 1
            tokX = p.sV2.n

            for hp in range(8):
                q = tsb * 8 + hp
                ws = q % 2
                cw = p.cW0 if ws == 0 else p.cW1
                wprev = wuse.get(q - 2, k_pre)

                @blk.gpsimd
                def _(g, hp=hp, ws=ws, wprev=wprev, cw=cw):
                    g.wait_ge(p.sT.h, wprev)
                    g.dma_start(out=lkw[:, ws, :, :],
                                in_=p.w['l1_wk'][:, :, hp * 256:(hp + 1) * 256]).then_inc(cw.h, 16)
                cw.n += 16
                tokKW = cw.n

                for hh in range(2):
                    h = hp * 2 + hh
                    bk = 4 + hh
                    ev = hh
                    bs = p.bankS.get(bk, 0)

                    @blk.tensor
                    def _(t, hh=hh, ws=ws, xs=xs, bk=bk, tokX=tokX,
                          tokKW=tokKW, bs=bs, cw=cw):
                        t.wait_ge(cw.h, tokKW)
                        t.wait_ge(p.sV2.h, tokX)
                        t.wait_ge(p.sS.h, bs)
                        for cc in range(CC):
                            _li23 = t.matmul(_bank(p, bk), lkw[:, ws, cc, hh * 128:(hh + 1) * 128],
                                     lxt[:, xs, cc, :], start=(cc == 0), stop=(cc == CC - 1))
                        _li23.then_inc(p.sT.h, 1)
                    p.sT.n += 1
                    tokMM = p.sT.n
                    evfree = p.evS.get(ev, 0)
                    ce = p.cE0 if ev == 0 else p.cE1

                    @blk.scalar
                    def _(s, ev=ev, bk=bk, tokMM=tokMM, evfree=evfree, ce=ce):
                        s.wait_ge(p.sT.h, tokMM)
                        s.wait_ge(ce.h, evfree)
                        s.copy(out=p.evac[:, ev, :], in_=_bank(p, bk)).then_inc(p.sS.h, 1)
                    p.sS.n += 1
                    p.bankS[bk] = p.sS.n
                    tokEv = p.sS.n

                    @blk.sync
                    def _(s, h=h, tsl=tsl, ev=ev, tokEv=tokEv, ce=ce):
                        s.wait_ge(p.sS.h, tokEv)
                        s.dma_start(out=p.agki[h, :, tsl],
                                    in_=p.evac[:, ev, :]).then_inc(ce.h, 16)
                    ce.n += 16
                    p.evS[ev] = ce.n
                wuse[q] = p.sT.n
            xuse[tsb] = p.sT.n

    _phase_barrier(p)
    with nc.sbuf_tensor(_u('lvw'), [128, 2, CC, 512], BF16) as lvw, \
         nc.sbuf_tensor(_u('lxc'), [128, 2, CC, 128], BF16) as lxc:
        vwuse = {}
        vxuse = {}
        v_pre = p.sT.n
        for vh in range(2):
            for half in (0, 1):
                idx = 2 * vh + half
                cw = p.cW0 if half == 0 else p.cW1
                wprev = vwuse.get(idx - 2, v_pre)

                @blk.gpsimd
                def _(g, vh=vh, half=half, wprev=wprev, cw=cw):
                    g.wait_ge(p.sT.h, wprev)
                    c0 = vh * 1024 + half * 512
                    g.dma_start(out=lvw[:, half, :, :],
                                in_=p.w['l1_wv'][:, :, c0:c0 + 512]).then_inc(cw.h, 16)
                cw.n += 16
            tokVW0 = p.cW0.n
            tokVW1 = p.cW1.n

            for tc in range(NKC):
                tb = tc % 2
                it4 = vh * NKC + tc
                tsl = slice(tc * 128, (tc + 1) * 128)
                xprev = vxuse.get(it4 - 2, v_pre)

                @blk.vector
                def _(v, tsl=tsl, tb=tb, xprev=xprev, xt=xt):
                    v.wait_ge(p.sV.h, xt)
                    v.wait_ge(p.sT.h, xprev)
                    v.tensor_copy(lxc[:, tb, :, :], p.xsb[:, :, tsl]).then_inc(p.sV2.h, 1)
                p.sV2.n += 1
                tokXC = p.sV2.n
                b4, b5 = p.bankS.get(4, 0), p.bankS.get(5, 0)

                @blk.tensor
                def _(t, tb=tb, tokXC=tokXC, tokVW0=tokVW0, tokVW1=tokVW1,
                      b4=b4, b5=b5):
                    t.wait_ge(p.cW0.h, tokVW0)
                    t.wait_ge(p.cW1.h, tokVW1)
                    t.wait_ge(p.sV2.h, tokXC)
                    t.wait_ge(p.sS.h, max(b4, b5))
                    for cc in range(CC):
                        t.matmul(_bank(p, 4), lxc[:, tb, cc, :], lvw[:, 0, cc, :],
                                 start=(cc == 0), stop=(cc == CC - 1))
                    for cc in range(CC):
                        _li24 = t.matmul(_bank(p, 5), lxc[:, tb, cc, :], lvw[:, 1, cc, :],
                                 start=(cc == 0), stop=(cc == CC - 1))
                    _li24.then_inc(p.sT.h, 1)
                p.sT.n += 1
                tokVm = p.sT.n
                vxuse[it4] = tokVm
                if tc == NKC - 1:
                    vwuse[2 * vh] = tokVm
                    vwuse[2 * vh + 1] = tokVm

                @blk.scalar
                def _(s, tokVm=tokVm, pv=p.evS.get('l1v', 0)):
                    s.wait_ge(p.sT.h, tokVm)
                    s.wait_ge(p.cE2.h, pv)
                    s.copy(out=p.evac[:, 0, :], in_=_bank(p, 4))
                    s.copy(out=p.evac[:, 1, :], in_=_bank(p, 5)).then_inc(p.sS.h, 1)
                p.sS.n += 1
                p.bankS[4] = p.sS.n
                p.bankS[5] = p.sS.n
                tokEv = p.sS.n

                @blk.sync
                def _(s, tc=tc, vh=vh, tokEv=tokEv):
                    s.wait_ge(p.sS.h, tokEv)
                    s.dma_start(out=p.agvi[:, tc, vh * 1024:vh * 1024 + 512],
                                in_=p.evac[:, 0, :]).then_inc(p.cE2.h, 16)
                    s.dma_start(out=p.agvi[:, tc, vh * 1024 + 512:(vh + 1) * 1024],
                                in_=p.evac[:, 1, :]).then_inc(p.cE2.h, 16)
                p.cE2.n += 32
                p.evS['l1v'] = p.cE2.n

    @blk.gpsimd
    def _(g, e0=p.cE0.n, e1=p.cE1.n, e2=p.cE2.n):
        g.wait_ge(p.cE0.h, e0)
        g.wait_ge(p.cE1.h, e1)
        g.wait_ge(p.cE2.h, e2)
        g.collective_compute('AllGather', ALU.bypass, replica_groups=p.RG,
                             ins=[p.agki[:, :, :].opt()],
                             outs=[p.agko[:, :, :, :].opt()]).then_inc(p.sCC.h, 1)
        g.collective_compute('AllGather', ALU.bypass, replica_groups=p.RG,
                             ins=[p.agvi[:, :, :].opt()],
                             outs=[p.agvo[:, :, :, :].opt()]).then_inc(p.sCC.h, 1)
    p.sCC.n += 2
    kv_ag = p.sCC.n

    _proj_to_scratch(p, 'l1_wq', p.q_scr, H)
    kv_stores = [(p.sCC.h, kv_ag)]

    def kload(s, dst, pi, j0, j1, h):
        ck = p.cKV if pi == 0 else p.cKV1
        s.dma_start(out=dst,
                    in_=p.agko[pi, h, :, j0 * KCW:j1 * KCW].rearrange(
                        'p (c d) -> p c d', d=HD)).then_inc(ck.h, 16)

    def vload(s, dst, pi, j0, j1, h):
        ck = p.cKV if pi == 0 else p.cKV1
        s.dma_start(out=dst,
                    in_=p.agvo[pi, :, j0:j1, h * HD:(h + 1) * HD]).then_inc(ck.h, 16)

    _attention(p, _anti_chunks, 0, kload, vload, do_oproj=True,
               ow_key='l1_wo', l1_edge=True, kv_shared=False,
               kv_wait=kv_stores)



def _l1_ffn(p):
    nc, blk = p.nc, p.blk
    _phase_barrier(p)
    p.bankS[4] = 0
    p.bankS[5] = 0

    with nc.sbuf_tensor(_u('fgw'), [128, 2, CC, 256], BF16) as fgw, \
         nc.sbuf_tensor(_u('fuw'), [128, 2, CC, 256], BF16) as fuw, \
         nc.sbuf_tensor(_u('fxb'), [128, CC, SBW], BF16) as fxb, \
         nc.sbuf_tensor(_u('fsg'), [128, SBW], BF16) as fsg:

        xt = p.x_tok
        wslot = {}
        w_pre = p.sT.n
        for hq in range(NKC):
            for half in (0, 1):
                idx = 2 * hq + half
                cw = p.cW0 if half == 0 else p.cW1
                wprev = wslot.get(idx - 2, w_pre)

                @blk.gpsimd
                def _(g, hq=hq, half=half, wprev=wprev, cw=cw):
                    g.wait_ge(p.sT.h, wprev)
                    c0 = hq * 512 + half * 256
                    g.dma_start(out=fgw[:, half, :, :],
                                in_=p.w['l1_wgT'][:, :, c0:c0 + 256]).then_inc(cw.h, 16)
                    g.dma_start(out=fuw[:, half, :, :],
                                in_=p.w['l1_wuT'][:, :, c0:c0 + 256]).then_inc(cw.h, 16)
                cw.n += 32
            tokW0 = p.cW0.n
            tokW1 = p.cW1.n

            for tsb in range(NSB):
                tsl = slice(tsb * SBW, (tsb + 1) * SBW)
                tokT_prev2 = p.sT.n

                @blk.vector
                def _(v, tsl=tsl, tokT_prev2=tokT_prev2, xt=xt):
                    v.wait_ge(p.sV.h, xt)
                    v.wait_ge(p.sT.h, tokT_prev2)
                    v.tensor_copy(fxb[:, :, :], p.xsb[:, :, tsl]).then_inc(p.sV2.h, 1)
                p.sV2.n += 1
                tokXB = p.sV2.n
                for hh in range(4):
                    hc = hq * 4 + hh
                    hs = hh // 2
                    ho = (hh % 2) * 128
                    tw = tokW0 if hs == 0 else tokW1
                    cwx = p.cW0 if hs == 0 else p.cW1
                    b4 = p.bankS.get(4, 0)
                    b5 = p.bankS.get(5, 0)

                    @blk.tensor
                    def _(t, hs=hs, ho=ho, tw=tw, cwx=cwx, tokXB=tokXB, b4=b4, b5=b5):
                        t.wait_ge(cwx.h, tw)
                        t.wait_ge(p.sV2.h, tokXB)
                        t.wait_ge(p.sS.h, b4)
                        for cc in range(CC):
                            _li25 = t.matmul(_bank(p, 4), fgw[:, hs, cc, ho:ho + 128],
                                     fxb[:, cc, :], start=(cc == 0), stop=(cc == CC - 1))
                        _li25.then_inc(p.sT.h, 1)
                        t.wait_ge(p.sV.h, b5)
                        for cc in range(CC):
                            _li26 = t.matmul(_bank(p, 5), fuw[:, hs, cc, ho:ho + 128],
                                     fxb[:, cc, :], start=(cc == 0), stop=(cc == CC - 1))
                        _li26.then_inc(p.sT.h, 1)
                    p.sT.n += 2
                    if tsb == NSB - 1 and hh == 1:
                        wslot[2 * hq] = p.sT.n
                    if tsb == NSB - 1 and hh == 3:
                        wslot[2 * hq + 1] = p.sT.n
                    tokG = p.sT.n - 1
                    tokU = p.sT.n

                    @blk.scalar
                    def _(s, tokG=tokG, pgt=p.sV.n):
                        s.wait_ge(p.sT.h, tokG)
                        s.wait_ge(p.sV.h, pgt)
                        s.activation(out=fsg[:, :], in_=_bank(p, 4),
                                     func=AF.Silu).then_inc(p.sS.h, 1)
                    p.sS.n += 1
                    p.bankS[4] = p.sS.n
                    tokSG = p.sS.n
                    evfree = p.evS.get(0, 0)

                    @blk.vector
                    def _(v, tokSG=tokSG, tokU=tokU, evfree=evfree):
                        v.wait_ge(p.sS.h, tokSG)
                        v.wait_ge(p.sT.h, tokU)
                        v.wait_ge(p.cE0.h, evfree)
                        v.tensor_tensor(p.evac[:, 0, :], _bank(p, 5), fsg[:, :],
                                        ALU.mult).then_inc(p.sV.h, 1)
                    p.sV.n += 1
                    p.bankS[5] = p.sV.n
                    tokGT = p.sV.n

                    @blk.sync
                    def _(s, hc=hc, tsl=tsl, tokGT=tokGT):
                        s.wait_ge(p.sV.h, tokGT)
                        s.dma_start(out=p.gated_scr[:, hc, tsl],
                                    in_=p.evac[:, 0, :]).then_inc(p.cE0.h, 16)
                    p.cE0.n += 16
                    p.evS[0] = p.cE0.n
        g_stores = p.cE0.n

    _phase_barrier(p)
    with nc.sbuf_tensor(_u('dwd'), [128, 2, 8, 1024], BF16) as dwd, \
         nc.sbuf_tensor(_u('dgt'), [128, 2, 8, SBW], BF16) as dgt:

        dslot = {}
        it = 0
        for tsb in range(NSB):
            tsl = slice(tsb * SBW, (tsb + 1) * SBW)
            rtoks = []
            for d2h in range(2):
                for hb in range(8):
                    sl2 = it % 2
                    prev_use = dslot.get(it - 2, 0)

                    cwx = p.cW0 if sl2 == 0 else p.cW1
                    cdx = p.cD if sl2 == 0 else p.cD1

                    @blk.gpsimd
                    def _(g, d2h=d2h, hb=hb, sl2=sl2, prev_use=prev_use, cwx=cwx):
                        g.wait_ge(p.sT.h, prev_use)
                        g.dma_start(out=dwd[:, sl2, :, :],
                                    in_=p.w['l1_wdT'][:, hb * 8:(hb + 1) * 8,
                                                      d2h * 1024:(d2h + 1) * 1024]).then_inc(cwx.h, 16)
                    cwx.n += 16
                    tokDW = cwx.n

                    @blk.sync
                    def _(s, hb=hb, tsl=tsl, sl2=sl2, prev_use=prev_use, g_stores=g_stores, cdx=cdx):
                        s.wait_ge(p.cE0.h, g_stores)
                        s.wait_ge(p.sT.h, prev_use)
                        s.dma_start(out=dgt[:, sl2, :, :],
                                    in_=p.gated_scr[:, hb * 8:(hb + 1) * 8, tsl]).then_inc(cdx.h, 16)
                    cdx.n += 16
                    tokGL = cdx.n
                    pof = p.po_free if hb == 0 else 0

                    @blk.tensor
                    def _(t, d2h=d2h, hb=hb, sl2=sl2, tokDW=tokDW, tokGL=tokGL, pof=pof,
                          cwx=cwx, cdx=cdx):
                        t.wait_ge(cwx.h, tokDW)
                        t.wait_ge(cdx.h, tokGL)
                        if pof:
                            t.wait_ge(p.sV.h, pof)
                        for hc in range(8):
                            for d2c in range(8):
                                _li27 = t.matmul(p.ps[:, d2c * 512:d2c * 512 + SBW],
                                         dwd[:, sl2, hc, d2c * 128:(d2c + 1) * 128],
                                         dgt[:, sl2, hc, :],
                                         start=(hb == 0 and hc == 0),
                                         stop=(hb == 7 and hc == 7))
                        _li27.then_inc(p.sT.h, 1)
                    p.sT.n += 1
                    dslot[it] = p.sT.n
                    it += 1
                tok_half = p.sT.n

                @blk.vector
                def _(v, d2h=d2h, tsl=tsl, tok_half=tok_half):
                    v.wait_ge(p.sT.h, tok_half)
                    for d2c in range(8):
                        cc = d2h * 8 + d2c
                        _li28 = v.tensor_tensor(p.xsb[:, cc, tsl], p.xsb[:, cc, tsl],
                                        p.ps[:, d2c * 512:d2c * 512 + SBW], ALU.add)
                    _li28.then_inc(p.sV.h, 1)
                p.sV.n += 1
                p.po_free = p.sV.n
                rtoks.append(p.sV.n)
            _ln_only(p, tsl, rtoks[-1])
            p.po_free = p.x_tok


def kernel(**inputs):
    in_maps = _stage_inputs(inputs)
    nc = build_program(_PHASES)
    res = run_bass_kernel_spmd(nc, in_maps, core_ids=list(range(8)))
    return _assemble(res.results)


# version 16
# speedup vs baseline: 1.0949x; 1.0151x over previous
import math
from contextlib import ExitStack

import numpy as np
import ml_dtypes

import concourse.bass as bass
import concourse.mybir as mybir
from concourse.bass_utils import run_bass_kernel_spmd

F32 = mybir.dt.float32
BF16 = mybir.dt.bfloat16
FP8 = mybir.dt.float8e4
AF = mybir.ActivationFunctionType
ALU = mybir.AluOpType

NPBF16 = ml_dtypes.bfloat16
NPFP8 = ml_dtypes.float8_e4m3

B = 4
S = 4096
D = 2048
H = 16
HD = 128
T = 2048
CC = 16
NSB = 4
SBW = 512
KCW = 128
NKC = 16
DFF = 8192
HC = 64

ATT_SCALE = 1.0 / math.sqrt(HD)
SHIFT = 18.0
LN_EPS = 1e-5

_PHASES = 4
_SUB = None



def _chunkT(w):
    C, M = w.shape
    return np.ascontiguousarray(
        np.asarray(w).reshape(C // 128, 128, M).transpose(1, 0, 2).astype(np.float32))


def _mask_tiles(r):
    p = np.arange(128)[:, None]
    f = np.arange(512)[None, :]
    tiles = np.zeros((16, 128, 512), dtype=NPFP8)
    for pi in (0, 1):
        e = -1 if (r == 0 and pi == 1) else 0
        ep = 0 if (r == 0 and pi == 1) else 1
        for d in range(4):
            tiles[pi * 4 + d] = (p - f + 128 * d <= e).astype(NPFP8)
            tiles[8 + pi * 4 + d] = (p - f + 128 * d >= ep).astype(NPFP8)
    return tiles


def _stage_inputs(inputs):
    g = {}
    g['l0_qwT'] = _chunkT(np.asarray(inputs['L0_attn_q_w']).T)
    g['l0_kwT'] = _chunkT(np.asarray(inputs['L0_attn_k_w']).T)
    g['l0_vwT'] = _chunkT(np.asarray(inputs['L0_attn_v_w']).T)
    g['l0_owT'] = _chunkT(np.asarray(inputs['L0_attn_o_w']).T)
    g['l0_wgT'] = _chunkT(np.asarray(inputs['L0_ffn_W_gate']).T)
    wqkv = np.asarray(inputs['L0_ffn_W_qkv'])
    g['l0_qT'] = _chunkT(wqkv[0:D].T)
    g['l0_kvT'] = _chunkT(wqkv[D:3 * D].T)
    g['l1_wq'] = _chunkT(np.asarray(inputs['L1_attn_Wq']))
    g['l1_wk'] = _chunkT(np.asarray(inputs['L1_attn_Wk']))
    g['l1_wv'] = _chunkT(np.asarray(inputs['L1_attn_Wv']))
    g['l1_wo'] = _chunkT(np.asarray(inputs['L1_attn_Wo']))
    g['l1_wgT'] = _chunkT(np.asarray(inputs['L1_ffn_w_gate']).T)
    g['l1_wuT'] = _chunkT(np.asarray(inputs['L1_ffn_w_up']).T)
    g['l1_wdT'] = _chunkT(np.asarray(inputs['L1_ffn_w_down']).T)

    x0 = np.asarray(inputs['L0_attn_x'], dtype=np.float32)
    ident = np.eye(128, dtype=NPBF16)
    ones_col = np.ones((128, 1), dtype=np.float32)
    ones_sq = np.ones((128, 128), dtype=np.float32)
    ones_row = np.ones((1, 128), dtype=np.float32)
    ones_colb = np.ones((128, 1), dtype=NPBF16)
    ones_sqb = np.ones((128, 128), dtype=NPBF16)

    in_maps = []
    for c in range(8):
        b, r = c // 2, c % 2
        m = dict(g)
        xown = np.ascontiguousarray(x0[b, r::2].T)
        m['x0own'] = np.ascontiguousarray(
            xown.reshape(CC, 128, T).transpose(1, 0, 2))
        m['masks'] = np.ascontiguousarray(_mask_tiles(r).transpose(1, 0, 2))
        edge = np.zeros((128, 4), dtype=np.float32)
        edge[:, 0] = 1.0 if r == 0 else 0.0
        edge[:, 1] = 0.0 if r == 0 else 1.0 / S
        m['edge'] = edge
        patch = np.zeros((128, 1), dtype=np.float32)
        if r == 1:
            patch[:, 0] = 1.0
        m['patch'] = patch
        m['ident'] = ident
        m['ones_col'] = ones_col
        m['ones_sq'] = ones_sq
        m['ones_row'] = ones_row
        m['ones_colb'] = ones_colb
        m['ones_sqb'] = ones_sqb
        in_maps.append(m)
    return in_maps


def _assemble(results):
    out = np.empty((B, S, D), dtype=np.float32)
    for c in range(8):
        b, r = c // 2, c % 2
        o = np.asarray(results[c]['out_fm'])
        out[b, r::2, :] = o.transpose(1, 0, 2).reshape(D, T).T
    return out



_UN = [0]


def _u(name):
    _UN[0] += 1
    return "%s_%d" % (name, _UN[0])


class P:
    pass


def _diag_d(sb, j):
    d = j - 4 * sb
    return d if 0 <= d <= 3 else None


def _causal_chunks(sb):
    return [(pi, j) for pi in (0, 1) for j in range(4 * sb + 4)]


def _anti_chunks(sb):
    return [(pi, j) for pi in (0, 1) for j in range(4 * sb, NKC)]


def _full_chunks(sb):
    return [(pi, j) for pi in (0, 1) for j in range(NKC)]


def _bank(p, i, width=SBW):
    return p.ps[:, i * 512:i * 512 + width]


def build_program(phases=4):
    nc = bass.Bass('TRN2', target_bir_lowering=False)
    p = P()
    p.nc = nc

    dI = lambda name, shape, dt=F32: nc.dram_tensor(name, shape, dt, kind='ExternalInput').ap()
    dX = lambda name, shape, dt=BF16: nc.dram_tensor(name, shape, dt).ap()

    p.x0own = dI('x0own', [128, CC, T])
    p.w = {}
    for k, shape in {
        'l0_qwT': (128, CC, D), 'l0_kwT': (128, CC, HD), 'l0_vwT': (128, CC, HD),
        'l0_owT': (128, CC, D), 'l0_wgT': (128, CC, H), 'l0_qT': (128, CC, D),
        'l0_kvT': (128, CC, 2 * D), 'l1_wq': (128, CC, D), 'l1_wk': (128, CC, D),
        'l1_wv': (128, CC, D), 'l1_wo': (128, CC, D), 'l1_wgT': (128, CC, DFF),
        'l1_wuT': (128, CC, DFF), 'l1_wdT': (128, HC, D),
    }.items():
        p.w[k] = dI(k, list(shape))
    p.masks_in = dI('masks', [128, 16, SBW], FP8)
    p.edge_in = dI('edge', [128, 4])
    p.patch_in = dI('patch', [128, 1])
    p.ident_in = dI('ident', [128, 128], BF16)
    p.onesc_in = dI('ones_col', [128, 1])
    p.onessq_in = dI('ones_sq', [128, 128])
    p.onesr_in = dI('ones_row', [1, 128])
    p.onescb_in = dI('ones_colb', [128, 1], BF16)
    p.onesqb_in = dI('ones_sqb', [128, 128], BF16)
    p.out_fm = nc.dram_tensor('out_fm', [128, CC, T], F32, kind='ExternalOutput').ap()

    p.q_scr = dX('q_scr', [H, 128, T])
    p.k_loc = dX('k_loc', [2, H, 128, T])
    p.v_loc = dX('v_loc', [2, 128, NKC, D])
    p.gated_scr = dX('gated_scr', [128, HC, T])
    p.agki = dX('agki', [H, 128, T])
    p.agko = dX('agko', [2, H, 128, T])
    p.agvi = dX('agvi', [128, NKC, D])
    p.agvo = dX('agvo', [2, 128, NKC, D])
    p.ag_kv_in = dX('ag_kv_in', [2, 128, NKC, HD])
    p.ag_kv_out = dX('ag_kv_out', [2, 2, 128, NKC, HD])
    p.ag_x_in = dX('ag_x_in', [128, CC, T])
    p.ag_x_out = dX('ag_x_out', [2, 128, CC, T])
    p.RG = [[0, 1], [2, 3], [4, 5], [6, 7]]

    with ExitStack() as es:
        p.es = es
        sb_t = lambda name, shape, dt: es.enter_context(nc.sbuf_tensor(name, shape, dt))
        p.xsb = sb_t('xsb', [128, CC, T], F32)
        p.consts = sb_t('consts', [128, 8], F32)
        p.onesc = sb_t('onesc', [128, 1], F32)
        p.onessq = sb_t('onessq', [128, 128], F32)
        p.onesr = sb_t('onesr', [1, 128], F32)
        p.onescb = sb_t('onescb', [128, 1], BF16)
        p.onesqb = sb_t('onesqb', [128, 128], BF16)
        p.ident = sb_t('ident_sb', [128, 128], BF16)
        p.edge = sb_t('edge_sb', [128, 4], F32)
        p.patch = sb_t('patch_sb', [128, 1], F32)
        p.masks = sb_t('masks_sb', [128, 8, SBW], FP8)
        p.lnm = sb_t('lnm', [1, SBW], F32)
        p.lnt1 = sb_t('lnt1', [1, SBW], F32)
        p.lnt2 = sb_t('lnt2', [1, SBW], F32)
        p.lns = sb_t('lns', [1, SBW], F32)
        p.sbst = sb_t('sbst', [128, 2, SBW], F32)
        p.sqbuf = sb_t('sqbuf', [128, 1, SBW], F32)
        p.evac = sb_t('evac', [128, 3, SBW], BF16)
        p.sumv = sb_t('sumv', [128, H], F32)
        p.sumvb = sb_t('sumvb', [128, H], F32)

        class Ctr:
            def __init__(self, name):
                self.h = es.enter_context(nc.semaphore(name))
                self.n = 0
        p.cS = Ctr('cS')
        p.cG = Ctr('cG')
        p.cM = Ctr('cM')
        p.cW0 = Ctr('cW0')
        p.cW1 = Ctr('cW1')
        p.cE0 = Ctr('cE0')
        p.cE1 = Ctr('cE1')
        p.cE2 = Ctr('cE2')
        p.cKV = Ctr('cKV')
        p.cKV1 = Ctr('cKV1')
        p.cQ0 = Ctr('cQ0')
        p.cQ1 = Ctr('cQ1')
        p.cX = Ctr('cX')
        p.cD = Ctr('cD')
        p.cD1 = Ctr('cD1')
        p.cA = Ctr('cA')
        p.dma_ctrs = [p.cS, p.cG, p.cM, p.cW0, p.cW1, p.cE0, p.cE1, p.cE2,
                      p.cKV, p.cKV1, p.cQ0, p.cQ1, p.cX, p.cD, p.cD1, p.cA]
        p.sT = Ctr('sT')
        p.sS = Ctr('sS')
        p.sV = Ctr('sV')
        p.sV2 = Ctr('sV2')
        p.sS2 = Ctr('sS2')
        p.sCC = Ctr('sCC')

        p.blk = es.enter_context(nc.Block())
        p.ps = es.enter_context(nc.psum_tensor('arena', [128, 4096], F32))
        p.x_tok = 0
        p.po_free = 0
        p.resid_tok = 0
        p.bankS = {}
        p.evS = {}

        _prologue(p)
        if phases >= 1:
            _l0_attn(p)
        if phases >= 2:
            _l0_ffn(p)
        if phases >= 3:
            _l1_attn(p)
        if phases >= 4:
            _l1_ffn(p)
        _epilogue(p)
    return nc


def _bank_bf(p, i, cols=128):
    return p.ps[:, i * 512:i * 512 + cols // 2].bitcast(BF16)


def _prologue(p):
    nc, blk = p.nc, p.blk

    @blk.sync
    def _(s):
        for dst, src in ((p.edge, p.edge_in), (p.patch, p.patch_in),
                         (p.ident, p.ident_in), (p.onesc, p.onesc_in),
                         (p.onessq, p.onessq_in), (p.onesr, p.onesr_in),
                         (p.onescb, p.onescb_in), (p.onesqb, p.onesqb_in)):
            s.dma_start(out=dst[:, :], in_=src[:, :]).then_inc(p.cS.h, 16)
            p.cS.n += 16
        s.dma_start(out=p.xsb[:, :, :], in_=p.x0own[:, :, :]).then_inc(p.cS.h, 16)
        p.cS.n += 16

    vals = [-SHIFT, ATT_SCALE, LN_EPS, -0.5, 2.0]

    @blk.vector
    def _(v):
        for i, val in enumerate(vals):
            _li1 = v.memset(p.consts[:, i:i + 1], float(val))
        _li1.then_inc(p.sV.h, 1)
    p.sV.n += 1
    for i, val in enumerate(vals):
        nc.const_aps.register(F32, float(val), p.consts[:, i:i + 1])

    tok = p.cS.n
    tokv = p.sV.n
    for eng in ('tensor', 'scalar', 'vector', 'gpsimd', 'sync'):
        def _w(e, tok=tok, tokv=tokv):
            e.wait_ge(p.cS.h, tok)
            e.wait_ge(p.sV.h, tokv)
        getattr(blk, eng)(_w)


def _epilogue(p):
    blk = p.blk
    tokV, tokT, tokS = p.sV.n, p.sT.n, p.sS.n

    @blk.sync
    def _(s):
        s.wait_ge(p.sV.h, tokV)
        s.wait_ge(p.sT.h, tokT)
        s.wait_ge(p.sS.h, tokS)
        s.dma_start(out=p.out_fm[:, :, :], in_=p.xsb[:, :, :]).then_inc(p.cS.h, 16)
        p.cS.n += 16
        s.wait_ge(p.cS.h, p.cS.n)


def _load_masks(p, base, wait_v):
    @p.blk.sync
    def _(s):
        s.wait_ge(p.sV.h, wait_v)
        s.dma_start(out=p.masks[:, :, :], in_=p.masks_in[:, base:base + 8, :]).then_inc(p.cM.h, 16)
    p.cM.n += 16
    return p.cM.n



def _phase_barrier(p):
    blk = p.blk
    toks = [(c.h, c.n) for c in p.dma_ctrs if c.n]
    for eng in ('tensor', 'scalar', 'vector', 'gpsimd', 'sync'):
        def _w(e, toks=toks):
            for h, n in toks:
                e.wait_ge(h, n)
        getattr(blk, eng)(_w)



def _proj_to_scratch(p, wkey, dst, nheads, col0=0):
    nc, blk = p.nc, p.blk
    _phase_barrier(p)
    with nc.sbuf_tensor(_u('pw2'), [128, 2, CC, 256], BF16) as pw2, \
         nc.sbuf_tensor(_u('pxt'), [128, 2, CC, SBW], BF16) as pxt:
        win = p.w[wkey]
        wuse = {}
        xuse = {}
        ngr = (nheads + 1) // 2
        for hg in range(ngr):
            nh = min(2, nheads - hg * 2)
            ws = hg % 2
            cw = p.cW0 if ws == 0 else p.cW1
            prev = wuse.get(hg - 2, 0)

            @blk.gpsimd
            def _(g, hg=hg, nh=nh, ws=ws, prev=prev, cw=cw):
                g.wait_ge(p.sT.h, prev)
                g.dma_start(out=pw2[:, ws, :, 0:nh * 128],
                            in_=win[:, :, col0 + hg * 256:col0 + hg * 256 + nh * 128]).then_inc(cw.h, 16)
            cw.n += 16
            tokW = cw.n

            for tsb in range(NSB):
                it = hg * NSB + tsb
                xs = it % 2
                xprev = xuse.get(it - 2, 0)
                xt = p.x_tok

                @blk.vector
                def _(v, tsb=tsb, xs=xs, xprev=xprev, xt=xt):
                    v.wait_ge(p.sV.h, xt)
                    v.wait_ge(p.sT.h, xprev)
                    v.tensor_copy(pxt[:, xs, :, :],
                                  p.xsb[:, :, tsb * SBW:(tsb + 1) * SBW]).then_inc(p.sV2.h, 1)
                p.sV2.n += 1
                tokX = p.sV2.n

                for hh in range(nh):
                    h = hg * 2 + hh
                    bk = 4 + (hh % 2)
                    ev = hh % 2
                    bs = p.bankS.get(bk, 0)

                    @blk.tensor
                    def _(t, hh=hh, ws=ws, xs=xs, bk=bk, tokW=tokW, tokX=tokX,
                          bs=bs, cw=cw):
                        t.wait_ge(cw.h, tokW)
                        t.wait_ge(p.sV2.h, tokX)
                        t.wait_ge(p.sS.h, bs)
                        for cc in range(CC):
                            _lip = t.matmul(_bank(p, bk), pw2[:, ws, cc, hh * 128:(hh + 1) * 128],
                                     pxt[:, xs, cc, :], start=(cc == 0), stop=(cc == CC - 1))
                        _lip.then_inc(p.sT.h, 1)
                    p.sT.n += 1
                    tokMM = p.sT.n
                    evfree = p.evS.get(ev, 0)
                    ce = p.cE0 if ev == 0 else p.cE1

                    @blk.scalar
                    def _(s, ev=ev, bk=bk, tokMM=tokMM, evfree=evfree, ce=ce):
                        s.wait_ge(p.sT.h, tokMM)
                        s.wait_ge(ce.h, evfree)
                        s.copy(out=p.evac[:, ev, :], in_=_bank(p, bk)).then_inc(p.sS.h, 1)
                    p.sS.n += 1
                    p.bankS[bk] = p.sS.n
                    tokEv = p.sS.n

                    @blk.sync
                    def _(s, h=h, tsb=tsb, ev=ev, tokEv=tokEv, ce=ce):
                        s.wait_ge(p.sS.h, tokEv)
                        s.dma_start(out=dst[h, :, tsb * SBW:(tsb + 1) * SBW],
                                    in_=p.evac[:, ev, :]).then_inc(ce.h, 16)
                    ce.n += 16
                    p.evS[ev] = ce.n
                xuse[it] = p.sT.n
            wuse[hg] = p.sT.n
    return None



def _tok_score(T0, i, k_):
    if i == 0:
        return T0 + 1
    return T0 + (i + 1) + (i - 1) * (k_ - 1)


def _tok_den(T0, cv, n, k_):
    if cv >= n - 1:
        return T0 + n * k_
    return T0 + (cv + 2) + (cv + 1) * (k_ - 1)


def _attention(p, chunks_of, mask_base, kload, vload, do_oproj, ow_key,
               l1_edge=False, kv_shared=False, kv_wait=None):
    nc, blk = p.nc, p.blk
    _phase_barrier(p)
    with nc.sbuf_tensor(_u('akb'), [128, 2, NKC, HD], BF16) as akb, \
         nc.sbuf_tensor(_u('avb'), [128, 2, NKC, HD], BF16) as avb, \
         nc.sbuf_tensor(_u('aqb'), [128, 2, SBW], BF16) as aqb, \
         nc.sbuf_tensor(_u('apt'), [128, 8, SBW], BF16) as apt, \
         nc.sbuf_tensor(_u('aden'), [128, SBW], F32) as aden, \
         nc.sbuf_tensor(_u('aosb'), [128, H, SBW], BF16) as aosb, \
         nc.sbuf_tensor(_u('aow'), [128, 2, 1024], BF16) as aow:

        ufree = {}
        owfree = {}
        bpl = {}
        p0done = {}
        kvspan = {}
        for sb in range(NSB):
            chunks = chunks_of(sb)
            n = len(chunks)
            sv = l1_edge and sb == 0
            sbsl = slice(sb * SBW, (sb + 1) * SBW)

            for h in range(H):
                u = sb * H + h
                qub = h % 2
                ft = ufree.get(u - 2, 0)
                ftk = ufree.get(u - 1, 0)
                ftk0 = p0done.get(u - 1, ftk)
                loads = []
                if kv_shared:
                    if h == 0:
                        for pi in (0, 1):
                            js = sorted(j for (pj, j) in chunks if pj == pi)
                            j1 = js[-1] + 1
                            d0 = kvspan.get(pi, 0)
                            if j1 > d0:
                                loads.append((pi, d0, j1, 0))
                                kvspan[pi] = j1
                else:
                    for pi in (0, 1):
                        js = sorted(j for (pj, j) in chunks if pj == pi)
                        loads.append((pi, js[0], js[-1] + 1,
                                      ftk0 if pi == 0 else ftk))

                ltk0 = [0]

                @blk.sync
                def _(s, h=h, qub=qub, sb=sb, ft=ft, loads=loads, sbsl=sbsl):
                    if ft:
                        s.wait_ge(p.sT.h, ft)
                    cq = p.cQ0 if qub == 0 else p.cQ1
                    s.dma_start(out=aqb[:, qub, :], in_=p.q_scr[h, :, sbsl]).then_inc(cq.h, 16)
                    cq.n += 16
                    if loads and kv_wait is not None:
                        for _kwh, _kwv in kv_wait:
                            s.wait_ge(_kwh, _kwv)
                    hh = 0 if kv_shared else h
                    for pi, j0, j1, wtk in loads:
                        ck = p.cKV if pi == 0 else p.cKV1
                        if wtk:
                            s.wait_ge(p.sT.h, wtk)
                        kload(s, akb[:, pi, j0:j1, :], pi, j0, j1, hh)
                        ck.n += 16
                        vload(s, avb[:, pi, j0:j1, :], pi, j0, j1, hh)
                        ck.n += 16
                        if pi == 0:
                            ltk0[0] = ck.n
                if any(pi == 0 for pi, _a, _b, _c in loads):
                    p.last_ltk0 = p.cKV.n
                ltk0[0] = p.cKV.n if not kv_shared else p.last_ltk0
                ltk = p.cKV1.n
                ltq = (p.cQ0 if qub == 0 else p.cQ1).n

                T0, S0 = p.sT.n, p.sS.n
                sv2b = p.sV2.n
                xt_guard = 0
                if h == 0:
                    xt_guard = p.resid_tok if sb > 0 else p.x_tok
                pof = p.po_free if (h == 0 and do_oproj) else 0
                k_ = 4 if sv else 3
                bp = 0 if (sv or u % 2 == 0) else 1
                bav, bdn = (6, 7) if bp == 0 else (4, 5)
                diag = [(_diag_d(sb, j) if mask_base is not None else None)
                        for (_pi, j) in chunks]
                ndth = []
                _c = 0
                for _dd in diag:
                    if _dd is not None:
                        _c += 1
                    ndth.append(_c)
                bpw = bpl.get(bp)

                cq = p.cQ0 if qub == 0 else p.cQ1

                @blk.tensor
                def _(t, sb=sb, h=h, qub=qub, ltk=ltk, ltk0v=ltk0[0], ltq=ltq,
                      cq=cq, chunks=chunks, S0=S0, sv=sv, n=n, diag=diag,
                      ndth=ndth, sv2b=sv2b, bav=bav, bdn=bdn, bpw=bpw,
                      xt_guard=xt_guard, pof=pof):
                    t.wait_ge(p.cKV.h, ltk0v)
                    t.wait_ge(cq.h, ltq)
                    if xt_guard:
                        t.wait_ge(p.sV.h, xt_guard)
                    if pof:
                        t.wait_ge(p.sV.h, pof)

                    def consume(cv, last):
                        pv, jv = chunks[cv]
                        if cv == 0 and bpw is not None:
                            t.wait_ge(bpw[0], bpw[1])
                        if diag[cv] is not None:
                            t.wait_ge(p.sV2.h, sv2b + ndth[cv])
                        else:
                            t.wait_ge(p.sS.h, S0 + cv + 1)
                        t.matmul(_bank(p, bav), avb[:, pv, jv, :],
                                 apt[:, cv % 8, :],
                                 start=(cv == 0), stop=last).then_inc(p.sT.h, 1)
                        if sv:
                            t.matmul(p.ps[:, 2048 + h:2049 + h],
                                     avb[:, pv, jv, :], p.onescb[:, :],
                                     start=(cv == 0), stop=last).then_inc(p.sT.h, 1)
                        t.matmul(p.ps[:, bdn * 512:bdn * 512 + SBW],
                                 p.onesqb[:, :], apt[:, cv % 8, :],
                                 start=(cv == 0), stop=last).then_inc(p.sT.h, 1)
                    first_p1 = next((ii for ii, (pii, _) in enumerate(chunks) if pii == 1), None)
                    for i, (pi, j) in enumerate(chunks):
                        if i == first_p1:
                            t.wait_ge(p.cKV1.h, ltk)
                        if i >= 4:
                            t.wait_ge(p.sS.h, S0 + i - 3)
                        t.matmul(_bank(p, i % 4),
                                 akb[:, pi, j, :], aqb[:, qub, :],
                                 start=True, stop=True).then_inc(p.sT.h, 1)
                        if i >= 1:
                            consume(i - 1, False)
                    consume(n - 1, True)
                p.sT.n += n * k_
                tok_den = p.sT.n
                ufree[u] = tok_den
                npi0 = sum(1 for (_pi, _j) in chunks if _pi == 0)
                p0done[u] = _tok_den(T0, npi0 - 1, n, k_) if npi0 < n else tok_den

                @blk.scalar
                def _(s, T0=T0, n=n, k_=k_):
                    for i in range(n):
                        s.wait_ge(p.sT.h, _tok_score(T0, i, k_))
                        if i >= 8:
                            s.wait_ge(p.sT.h, _tok_den(T0, i - 8, n, k_))
                        s.activation(out=apt[:, i % 8, :], in_=_bank(p, i % 4),
                                     func=AF.Exp, bias=-SHIFT, scale=ATT_SCALE).then_inc(p.sS.h, 1)
                p.sS.n += n

                @blk.vector
                def _(v, chunks=chunks, S0=S0, sb=sb, h=h, diag=diag,
                      tok_den=tok_den, n=n, sv2base=sv2b, bav=bav, bdn=bdn,
                      nvt=p.sV.n + 1, nvb=p.sV.n):
                    sv2 = sv2base
                    for i, (pi, j) in enumerate(chunks):
                        if diag[i] is not None:
                            v.wait_ge(p.sS.h, S0 + i + 1)
                            v.tensor_mul(apt[:, i % 8, :], apt[:, i % 8, :],
                                         p.masks[:, pi * 4 + diag[i], :]).then_inc(p.sV2.h, 1)
                            sv2 += 1
                    v.wait_ge(p.sT.h, tok_den)
                    if l1_edge and sb == 3:
                        v.tensor_tensor(p.ps[:, bdn * 512 + SBW - 1:bdn * 512 + SBW],
                                        p.ps[:, bdn * 512 + SBW - 1:bdn * 512 + SBW],
                                        p.patch[:, :], ALU.add).then_inc(p.sV2.h, 1)
                        sv2 += 1
                        v.wait_ge(p.sV2.h, sv2)
                    v.wait_ge(p.sV.h, nvb)
                    v.reciprocal(aden[:, :], p.ps[:, bdn * 512:bdn * 512 + SBW]).then_inc(p.sV2.h, 1)
                    sv2 += 1
                    v.wait_ge(p.sV2.h, sv2)
                    v.tensor_tensor(aosb[:, h, :], _bank(p, bav), aden[:, :],
                                    ALU.mult).then_inc(p.sV.h, 1)
                    if l1_edge and sb == 3:
                        v.wait_ge(p.sV.h, nvt)
                        v.scalar_tensor_tensor(
                            aosb[:, h, SBW - 1:SBW], aosb[:, h, SBW - 1:SBW],
                            p.edge[:, 0:1], p.sumvb[:, h:h + 1],
                            ALU.mult, ALU.add).then_inc(p.sV.h, 1)
                ndiag = sum(1 for _dd in diag if _dd is not None)
                p.sV2.n += ndiag + (1 if (l1_edge and sb == 3) else 0) + 1
                p.sV.n += 1 + (1 if (l1_edge and sb == 3) else 0)
                bpl[bp] = (p.sV.h, p.sV.n)
            ev_all = p.sV.n

            if sv:
                tokT = p.sT.n

                @blk.vector
                def _(v, tokT=tokT, sv2base=p.sV2.n):
                    v.wait_ge(p.sT.h, tokT)
                    v.tensor_copy(p.sumv[:, :], p.ps[:, 2048:2048 + H]).then_inc(p.sV2.h, 1)
                    v.wait_ge(p.sV2.h, sv2base + 1)
                    v.scalar_tensor_tensor(p.sumvb[:, :], p.sumv[:, :],
                                           p.edge[:, 1:2], p.sumv[:, :],
                                           ALU.mult, ALU.bypass).then_inc(p.sV.h, 1)
                p.sV2.n += 1
                p.sV.n += 1
                bpl[1] = (p.sV.h, p.sV.n)
                ev_all = p.sV.n

            if do_oproj:
                win = p.w[ow_key]
                rtoks = []
                for half in (0, 1):
                    for cc in range(CC):
                        ccb = cc % 2
                        ft = owfree.get(ccb, 0)

                        cw = p.cW0 if ccb == 0 else p.cW1

                        @blk.gpsimd
                        def _(g, cc=cc, half=half, ccb=ccb, ft=ft, cw=cw):
                            if ft:
                                g.wait_ge(p.sT.h, ft)
                            g.dma_start(out=aow[:, ccb, :],
                                        in_=win[:, cc, half * 1024:(half + 1) * 1024]).then_inc(cw.h, 16)
                        cw.n += 16
                        tokW = cw.n
                        rt0 = rtoks[0] if rtoks else 0

                        @blk.tensor
                        def _(t, cc=cc, half=half, ccb=ccb, tokW=tokW,
                              ev_all=ev_all, rt0=rt0, cw=cw):
                            t.wait_ge(cw.h, tokW)
                            if cc == 0:
                                if half == 0:
                                    t.wait_ge(p.sV.h, ev_all)
                                    t.wait_ge(p.sV.h, p.po_free)
                                else:
                                    t.wait_ge(p.sV.h, rt0)
                            for d2c in range(8):
                                _li3 = t.matmul(p.ps[:, d2c * 512:d2c * 512 + SBW],
                                         aow[:, ccb, d2c * 128:(d2c + 1) * 128],
                                         aosb[:, cc, :],
                                         start=(cc == 0), stop=(cc == CC - 1))
                            _li3.then_inc(p.sT.h, 1)
                        p.sT.n += 1
                        owfree[ccb] = p.sT.n
                    tok_half = p.sT.n

                    @blk.vector
                    def _(v, half=half, sbsl=sbsl, tok_half=tok_half):
                        v.wait_ge(p.sT.h, tok_half)
                        for d2c in range(8):
                            cc2 = half * 8 + d2c
                            _li4 = v.tensor_tensor(p.xsb[:, cc2, sbsl], p.xsb[:, cc2, sbsl],
                                            p.ps[:, d2c * 512:d2c * 512 + SBW], ALU.add)
                        _li4.then_inc(p.sV.h, 1)
                    p.sV.n += 1
                    rtoks.append(p.sV.n)
                p.po_free = rtoks[-1]
                p.resid_tok = rtoks[-1]
                _ln_only(p, sbsl, rtoks[-1])
                bpl[0] = (p.sV2.h, p.ln_b67)
                bpl[1] = (p.sV2.h, p.ln_b45)
            else:
                _res_ln(p, sb, from_po=False, y_sb=aosb, ytok=ev_all)
                bpl[0] = (p.sV2.h, p.ln_b67)
                bpl[1] = (p.sV2.h, p.ln_b45)



def _res_ln(p, sb, from_po, y_sb, ytok):
    assert not from_po
    nc, blk = p.nc, p.blk
    sbsl = slice(sb * SBW, (sb + 1) * SBW)

    @blk.vector
    def _(v, sb=sb, ytok=ytok):
        if from_po:
            v.wait_ge(p.sT.h, ytok)
            for cc in range(CC):
                _li5 = v.tensor_tensor(p.xsb[:, cc, sbsl], p.xsb[:, cc, sbsl],
                                p.ps[:, cc * 128:(cc + 1) * 128], ALU.add)
        else:
            v.wait_ge(p.sV.h, ytok)
            for cc in range(CC):
                _li5 = v.tensor_tensor(p.xsb[:, cc, sbsl], p.xsb[:, cc, sbsl],
                                y_sb[:, cc, :], ALU.add)
        _li5.then_inc(p.sV.h, 1)
    p.sV.n += 1
    tok_r = p.sV.n
    p.resid_tok = tok_r
    if from_po:
        p.po_free = tok_r
    _ln_only(p, sbsl, tok_r)


def _ln_only(p, sbsl, tok_r):
    nc, blk = p.nc, p.blk
    V20 = p.sV2.n
    T0sq = p.sT.n + 1

    @blk.vector
    def _(v, tok_r=tok_r, T0sq=T0sq):
        v.wait_ge(p.sV.h, tok_r)
        for cc in range(CC):
            if cc >= 1:
                v.wait_ge(p.sT.h, T0sq + cc)
            v.tensor_mul(p.sqbuf[:, 0, :], p.xsb[:, cc, sbsl],
                         p.xsb[:, cc, sbsl]).then_inc(p.sV2.h, 1)
    p.sV2.n += CC

    @blk.tensor
    def _(t, tok_r=tok_r, V20=V20):
        t.wait_ge(p.sV.h, tok_r)
        for cc in range(CC):
            _li6 = t.matmul(p.ps[0:1, 2048:2048 + SBW], p.onesc[:, :], p.xsb[:, cc, sbsl],
                     start=(cc == 0), stop=(cc == CC - 1))
        _li6.then_inc(p.sT.h, 1)
        for cc in range(CC):
            t.wait_ge(p.sV2.h, V20 + cc + 1)
            t.matmul(p.ps[0:1, 2560:2560 + SBW], p.onesc[:, :], p.sqbuf[:, 0, :],
                     start=(cc == 0), stop=(cc == CC - 1)).then_inc(p.sT.h, 1)
    p.sT.n += 1 + CC
    tok_stats = p.sT.n

    p.ln_b45 = p.sV2.n + 2

    @blk.vector
    def _(v, tok_stats=tok_stats, sv2b=p.sV2.n):
        v.wait_ge(p.sT.h, tok_stats)
        v.tensor_scalar_mul(p.lnm[0:1, :], p.ps[0:1, 2048:2048 + SBW], 1.0 / D).then_inc(p.sV2.h, 1)
        v.tensor_scalar_mul(p.lnt1[0:1, :], p.ps[0:1, 2560:2560 + SBW], 1.0 / D).then_inc(p.sV2.h, 1)
        v.wait_ge(p.sV2.h, sv2b + 2)
        v.tensor_mul(p.lnt2[0:1, :], p.lnm[0:1, :], p.lnm[0:1, :]).then_inc(p.sV2.h, 1)
        v.wait_ge(p.sV2.h, sv2b + 3)
        v.tensor_sub(p.lnt1[0:1, :], p.lnt1[0:1, :], p.lnt2[0:1, :]).then_inc(p.sV2.h, 1)
        v.wait_ge(p.sV2.h, sv2b + 4)
        v.tensor_scalar_add(p.lnt1[0:1, :], p.lnt1[0:1, :], LN_EPS).then_inc(p.sV.h, 1)
    p.sV2.n += 4
    p.sV.n += 1
    tok_var = p.sV.n

    @blk.scalar
    def _(s, tok_var=tok_var, ss2b=p.sS2.n):
        s.wait_ge(p.sV.h, tok_var)
        s.activation(out=p.lnt2[0:1, :], in_=p.lnt1[0:1, :], func=AF.Ln).then_inc(p.sS2.h, 1)
        s.wait_ge(p.sS2.h, ss2b + 1)
        s.activation(out=p.lns[0:1, :], in_=p.lnt2[0:1, :], func=AF.Exp,
                     scale=-0.5).then_inc(p.sS.h, 1)
    p.sS2.n += 1
    p.sS.n += 1
    tok_s0 = p.sS.n

    @blk.vector
    def _(v, tok_s0=tok_s0, sv2b=p.sV2.n):
        v.wait_ge(p.sS.h, tok_s0)
        v.tensor_mul(p.lnt2[0:1, :], p.lns[0:1, :], p.lns[0:1, :]).then_inc(p.sV2.h, 1)
        v.wait_ge(p.sV2.h, sv2b + 1)
        v.tensor_mul(p.lnt2[0:1, :], p.lnt2[0:1, :], p.lnt1[0:1, :]).then_inc(p.sV2.h, 1)
        v.wait_ge(p.sV2.h, sv2b + 2)
        v.tensor_scalar(p.lnt2[0:1, :], p.lnt2[0:1, :], -0.5, 1.5, ALU.mult, ALU.add).then_inc(p.sV2.h, 1)
        v.wait_ge(p.sV2.h, sv2b + 3)
        v.tensor_mul(p.lns[0:1, :], p.lns[0:1, :], p.lnt2[0:1, :]).then_inc(p.sV2.h, 1)
        v.wait_ge(p.sV2.h, sv2b + 4)
        v.tensor_mul(p.lnm[0:1, :], p.lnm[0:1, :], p.lns[0:1, :]).then_inc(p.sV.h, 1)
    p.sV2.n += 4
    p.sV.n += 1
    tok_s = p.sV.n

    @blk.tensor
    def _(t, tok_s=tok_s):
        t.wait_ge(p.sV.h, tok_s)
        t.matmul(p.ps[:, 3072:3072 + SBW], p.onesr[:, :], p.lns[0:1, :], start=True, stop=True)
        t.matmul(p.ps[:, 3584:3584 + SBW], p.onesr[:, :], p.lnm[0:1, :], start=True, stop=True).then_inc(p.sT.h, 1)
    p.sT.n += 1
    tok_bc = p.sT.n

    p.ln_b67 = p.sV2.n + 1

    @blk.vector
    def _(v, tok_bc=tok_bc, sv2b=p.sV2.n):
        v.wait_ge(p.sT.h, tok_bc)
        v.tensor_copy(p.sbst[:, 0, :], p.ps[:, 3072:3072 + SBW])
        v.tensor_copy(p.sbst[:, 1, :], p.ps[:, 3584:3584 + SBW]).then_inc(p.sV2.h, 1)
        v.wait_ge(p.sV2.h, sv2b + 1)
        for cc in range(CC):
            _li10 = v.tensor_mul(p.xsb[:, cc, sbsl], p.xsb[:, cc, sbsl], p.sbst[:, 0, :])
        _li10.then_inc(p.sV2.h, 1)
        v.wait_ge(p.sV2.h, sv2b + 2)
        for cc in range(CC):
            _li11 = v.tensor_sub(p.xsb[:, cc, sbsl], p.xsb[:, cc, sbsl], p.sbst[:, 1, :])
        _li11.then_inc(p.sV.h, 1)
    p.sV2.n += 2
    p.sV.n += 1
    p.x_tok = p.sV.n



def _l0_attn(p):
    nc, blk = p.nc, p.blk
    _phase_barrier(p)

    with nc.sbuf_tensor(_u('kvw'), [128, CC, 2 * HD], BF16) as kvw, \
         nc.sbuf_tensor(_u('kvx'), [128, 2, CC, SBW], BF16) as kvx:

        @blk.gpsimd
        def _(g):
            g.dma_start(out=kvw[:, :, 0:HD], in_=p.w['l0_kwT'][:, :, :]).then_inc(p.cG.h, 16)
            g.dma_start(out=kvw[:, :, HD:2 * HD], in_=p.w['l0_vwT'][:, :, :]).then_inc(p.cG.h, 16)
        p.cG.n += 32
        tokW = p.cG.n

        kst = {}
        xslot = {}
        x_pre = p.sT.n
        for tsb in range(NSB):
            sl = slice(tsb * SBW, (tsb + 1) * SBW)
            tb = tsb % 2
            tokT_prev = xslot.get(tsb - 2, x_pre)

            @blk.vector
            def _(v, sl=sl, tb=tb, tokT_prev=tokT_prev):
                v.wait_ge(p.sT.h, tokT_prev)
                v.tensor_copy(kvx[:, tb, :, :], p.xsb[:, :, sl]).then_inc(p.sV2.h, 1)
            p.sV2.n += 1
            tokX = p.sV2.n
            b4, b5 = p.bankS.get(4, 0), p.bankS.get(5, 0)

            @blk.tensor
            def _(t, tb=tb, tokX=tokX, tokW=tokW, b4=b4, b5=b5):
                t.wait_ge(p.cG.h, tokW)
                t.wait_ge(p.sV2.h, tokX)
                t.wait_ge(p.sS.h, max(b4, b5))
                for cc in range(CC):
                    _li11 = t.matmul(_bank(p, 4), kvw[:, cc, 0:HD], kvx[:, tb, cc, :],
                             start=(cc == 0), stop=(cc == CC - 1))
                _li11.then_inc(p.sT.h, 1)
                for cc in range(CC):
                    _li12 = t.matmul(_bank(p, 5), kvw[:, cc, HD:2 * HD], kvx[:, tb, cc, :],
                             start=(cc == 0), stop=(cc == CC - 1))
                _li12.then_inc(p.sT.h, 1)
            p.sT.n += 2
            tokKV = p.sT.n
            xslot[tsb] = tokKV

            @blk.scalar
            def _(s, tokKV=tokKV, pk=kst.get('k', 0)):
                s.wait_ge(p.sT.h, tokKV)
                s.wait_ge(p.cE0.h, pk)
                s.copy(out=p.evac[:, 0, :], in_=_bank(p, 4)).then_inc(p.sS.h, 1)
                s.copy(out=p.evac[:, 1, :], in_=_bank(p, 5)).then_inc(p.sS.h, 1)
            p.sS.n += 2
            p.bankS[4] = p.sS.n - 1
            p.bankS[5] = p.sS.n
            tokEv = p.sS.n

            @blk.sync
            def _(s, tsb=tsb, tokEv=tokEv):
                s.wait_ge(p.sS.h, tokEv)
                s.dma_start(out=p.ag_kv_in[0, :, tsb * 4:(tsb + 1) * 4, :],
                            in_=p.evac[:, 0, :].rearrange('p (c d) -> p c d', d=HD)).then_inc(p.cE0.h, 16)
            p.cE0.n += 16
            kst['k'] = p.cE0.n

            for tt in (range(4) if _SUB not in ('l0a_prodK',) else []):
                b7 = p.bankS.get(7, 0)

                @blk.tensor
                def _(t, tt=tt, tokEv=tokEv, b7=b7):
                    t.wait_ge(p.sS.h, tokEv)
                    t.wait_ge(p.sS.h, b7)
                    t.transpose(_bank_bf(p, 7),
                                p.evac[:, 1, tt * 128:(tt + 1) * 128], p.ident[:, :]).then_inc(p.sT.h, 1)
                p.sT.n += 1
                tokTr = p.sT.n

                @blk.scalar
                def _(s, tt=tt, tokTr=tokTr, pv=kst.get('v', 0)):
                    s.wait_ge(p.sT.h, tokTr)
                    s.wait_ge(p.cE2.h, pv)
                    s.copy(out=p.evac[:, 2, tt * 128:(tt + 1) * 128],
                           in_=_bank_bf(p, 7)).then_inc(p.sS.h, 1)
                p.sS.n += 1
                p.bankS[7] = p.sS.n
            tokVt = p.sS.n

            if _SUB not in ('l0a_prodK',):
                @blk.sync
                def _(s, tsb=tsb, tokVt=tokVt):
                    s.wait_ge(p.sS.h, tokVt)
                    s.dma_start(out=p.ag_kv_in[1, :, tsb * 4:(tsb + 1) * 4, :],
                                in_=p.evac[:, 2, :].rearrange('p (c d) -> p c d', d=HD)).then_inc(p.cE2.h, 16)
                p.cE2.n += 16
                kst['v'] = p.cE2.n
        tok_k, tok_v = p.cE0.n, p.cE2.n

    if _SUB in ('l0a_prod', 'l0a_prodK'):
        @blk.gpsimd
        def _(g, tok_k=tok_k, tok_v=tok_v):
            g.wait_ge(p.cE0.h, tok_k)
            g.wait_ge(p.cE2.h, tok_v)
        return

    @blk.gpsimd
    def _(g, tok_k=tok_k, tok_v=tok_v):
        g.wait_ge(p.cE0.h, tok_k)
        g.wait_ge(p.cE2.h, tok_v)
        g.collective_compute('AllGather', ALU.bypass, replica_groups=p.RG,
                             ins=[p.ag_kv_in[:, :, :, :].opt()],
                             outs=[p.ag_kv_out[:, :, :, :, :].opt()]).then_inc(p.sCC.h, 1)
    p.sCC.n += 1
    kv_ready = p.sCC.n

    if _SUB == 'l0a_kv':
        return
    _proj_to_scratch(p, 'l0_qwT', p.q_scr, H)
    if _SUB == 'l0a_kvq':
        return

    def kload(s, dst, pi, j0, j1, h):
        ck = p.cKV if pi == 0 else p.cKV1
        s.dma_start(out=dst, in_=p.ag_kv_out[pi, 0, :, j0:j1, :]).then_inc(ck.h, 16)

    def vload(s, dst, pi, j0, j1, h):
        ck = p.cKV if pi == 0 else p.cKV1
        s.dma_start(out=dst, in_=p.ag_kv_out[pi, 1, :, j0:j1, :]).then_inc(ck.h, 16)

    _attention(p, _full_chunks, None, kload, vload, do_oproj=True,
               ow_key='l0_owT', kv_shared=True, kv_wait=[(p.sCC.h, kv_ready)])



def _l0_ffn(p):
    nc, blk = p.nc, p.blk
    _phase_barrier(p)
    mask_tok = _load_masks(p, 0, p.x_tok)

    with nc.sbuf_tensor(_u('gw'), [128, 2, CC, H], BF16) as gw, \
         nc.sbuf_tensor(_u('gwf'), [128, CC, H], F32) as gwf, \
         nc.sbuf_tensor(_u('fxh'), [128, 2, CC, 128], BF16) as fxh, \
         nc.sbuf_tensor(_u('fxl'), [128, 2, CC, 128], BF16) as fxl, \
         nc.sbuf_tensor(_u('gind'), [128, NKC, H], F32) as gind, \
         nc.sbuf_tensor(_u('gtop'), [128, NKC, H], F32) as gtop, \
         nc.sbuf_tensor(_u('glg'), [128, 2, H], F32) as glg, \
         nc.sbuf_tensor(_u('glge'), [128, 2, H], F32) as glge, \
         nc.sbuf_tensor(_u('gmx'), [128, 2, 2], F32) as gmx, \
         nc.sbuf_tensor(_u('gden'), [128, 2, 1], F32) as gden, \
         nc.sbuf_tensor(_u('fkw'), [128, CC, 512], BF16) as fkw, \
         nc.sbuf_tensor(_u('fxk'), [128, 2, CC, 128], BF16) as fxk, \
         nc.sbuf_tensor(_u('fsh'), [128, 2, NKC, HD], BF16) as fsh:

        @blk.gpsimd
        def _(g):
            g.dma_start(out=gwf[:, :, :], in_=p.w['l0_wgT'][:, :, :]).then_inc(p.cG.h, 16)
            g.dma_start(out=gw[:, 0, :, :], in_=p.w['l0_wgT'][:, :, :]).then_inc(p.cG.h, 16)
        p.cG.n += 32
        tokGW = p.cG.n

        @blk.vector
        def _(v, tokGW=tokGW):
            v.wait_ge(p.cG.h, tokGW)
            v.tensor_sub(gw[:, 1, :, :], gwf[:, :, :], gw[:, 0, :, :]).then_inc(p.sV.h, 1)
        p.sV.n += 1
        tokGWL = p.sV.n

        gmmtok = {}
        glgcopy = {}
        expdone = {}
        gtdone = {}
        gmaxtok = {}
        g_pre = p.sT.n
        for tc in range(NKC):
            tsl = slice(tc * 128, (tc + 1) * 128)
            tb = tc % 2
            tokT_prev = gmmtok.get(tc - 2, g_pre)
            xt = p.x_tok

            @blk.scalar
            def _(s, tsl=tsl, tb=tb, tokT_prev=tokT_prev, xt=xt):
                s.wait_ge(p.sV.h, xt)
                s.wait_ge(p.sT.h, tokT_prev)
                s.copy(out=fxh[:, tb, :, :], in_=p.xsb[:, :, tsl]).then_inc(p.sS.h, 1)
            p.sS.n += 1
            tokXH = p.sS.n

            @blk.vector
            def _(v, tsl=tsl, tb=tb, tokXH=tokXH):
                v.wait_ge(p.sS.h, tokXH)
                v.scalar_tensor_tensor(fxl[:, tb, :, :], fxh[:, tb, :, :], -1.0,
                                       p.xsb[:, :, tsl], ALU.mult, ALU.add).then_inc(p.sV.h, 1)
            p.sV.n += 1
            tokXL = p.sV.n
            gbk = 6 + tb
            bkg = glgcopy.get(tc - 2, 0)

            @blk.tensor
            def _(t, tb=tb, gbk=gbk, tokXL=tokXL, tokGWL=tokGWL, bkg=bkg):
                t.wait_ge(p.sV.h, max(tokXL, tokGWL, bkg))
                for cc in range(CC):
                    t.matmul(_bank(p, gbk, H), fxh[:, tb, cc, :], gw[:, 0, cc, :],
                             start=(cc == 0), stop=False)
                for cc in range(CC):
                    t.matmul(_bank(p, gbk, H), fxh[:, tb, cc, :], gw[:, 1, cc, :],
                             start=False, stop=False)
                for cc in range(CC):
                    _li16 = t.matmul(_bank(p, gbk, H), fxl[:, tb, cc, :], gw[:, 0, cc, :],
                             start=False, stop=(cc == CC - 1))
                _li16.then_inc(p.sT.h, 1)
            p.sT.n += 1
            gmmtok[tc] = p.sT.n
            tokLG = p.sT.n

            @blk.vector
            def _(v, tc=tc, tb=tb, gbk=gbk, tokLG=tokLG, svb=p.sV.n,
                  sv2b=p.sV2.n, geprev=expdone.get(tc - 2, 0)):
                v.wait_ge(p.sT.h, tokLG)
                v.wait_ge(p.sS.h, geprev)
                v.tensor_copy(glg[:, tb, :], _bank(p, gbk, H)).then_inc(p.sV.h, 1)
                v.wait_ge(p.sV.h, svb + 1)
                v.tensor_reduce(gmx[:, tb, 0:1], glg[:, tb, :], mybir.AxisListType.X,
                                ALU.max).then_inc(p.sV2.h, 1)
                v.wait_ge(p.sV2.h, sv2b + 1)
                v.tensor_scalar_mul(gmx[:, tb, 1:2], gmx[:, tb, 0:1], -1.0).then_inc(p.sV2.h, 1)
                v.tensor_scalar(gind[:, tc, :], glg[:, tb, :], gmx[:, tb, 0:1], None,
                                ALU.is_ge).then_inc(p.sV.h, 1)
            p.sV2.n += 2
            p.sV.n += 2
            glgcopy[tc] = p.sV.n - 1
            gmaxtok[tc] = p.sV.n
            tokGMAX = p.sV.n

            @blk.scalar
            def _(s, tb=tb, tokGMAX=tokGMAX, gtprev=gtdone.get(tc - 2, 0)):
                s.wait_ge(p.sV.h, max(tokGMAX, gtprev))
                s.activation(out=glge[:, tb, :], in_=glg[:, tb, :], func=AF.Exp,
                             bias=gmx[:, tb, 1:2], accum_out=gden[:, tb, :]).then_inc(p.sS.h, 1)
            p.sS.n += 1
            tokGE = p.sS.n
            expdone[tc] = tokGE

            @blk.vector
            def _(v, tc=tc, tb=tb, tokGE=tokGE, sv2b=p.sV2.n):
                v.wait_ge(p.sS.h, tokGE)
                v.reciprocal(gden[:, tb, :], gden[:, tb, :]).then_inc(p.sV2.h, 1)
                v.wait_ge(p.sV2.h, sv2b + 1)
                v.tensor_scalar(gtop[:, tc, :], gind[:, tc, :], gden[:, tb, 0:1], None,
                                ALU.mult).then_inc(p.sV.h, 1)
            p.sV2.n += 1
            p.sV.n += 1
            gtdone[tc] = p.sV.n

        fkw_use = {}
        fxk_use = {}
        comb_done = {}
        kv_pre = p.sT.n
        for e in range(8):
            shi = 0 if e < 4 else 1
            hbase = (e % 4) * 4
            wprev = fkw_use.get(e - 1, kv_pre)

            @blk.gpsimd
            def _(g, e=e, wprev=wprev):
                g.wait_ge(p.sT.h, wprev)
                g.dma_start(out=fkw[:, :, :],
                            in_=p.w['l0_kvT'][:, :, e * 512:(e + 1) * 512]).then_inc(p.cW0.h, 16)
            p.cW0.n += 16
            tokKW = p.cW0.n

            for tc in range(NKC):
                tsl = slice(tc * 128, (tc + 1) * 128)
                tb = tc % 2
                le = e * NKC + tc
                xprev = fxk_use.get(le - 2, kv_pre)

                @blk.scalar
                def _(s, tsl=tsl, tb=tb, xprev=xprev):
                    s.wait_ge(p.sT.h, xprev)
                    s.copy(out=fxk[:, tb, :, :], in_=p.xsb[:, :, tsl]).then_inc(p.sS.h, 1)
                p.sS.n += 1
                tokXC = p.sS.n
                bk = 4 + (tc % 2)
                bkv = comb_done.get((tc % 2, le - 2), 0)

                @blk.tensor
                def _(t, tb=tb, bk=bk, tokXC=tokXC, tokKW=tokKW, bkv=bkv):
                    t.wait_ge(p.sS.h, tokXC)
                    t.wait_ge(p.cW0.h, tokKW)
                    t.wait_ge(p.sV.h, bkv)
                    for cc in range(CC):
                        _li20 = t.matmul(_bank(p, bk), fxk[:, tb, cc, :], fkw[:, cc, :],
                                 start=(cc == 0), stop=(cc == CC - 1))
                    _li20.then_inc(p.sT.h, 1)
                p.sT.n += 1
                tokKVm = p.sT.n
                fxk_use[le] = tokKVm
                if tc == NKC - 1:
                    fkw_use[e] = tokKVm

                @blk.vector
                def _(v, e=e, tc=tc, bk=bk, shi=shi, hbase=hbase, tokKVm=tokKVm,
                      gsel_tok=(gmaxtok[tc] if e < 4 else gtdone[tc]), sv2b=p.sV2.n):
                    v.wait_ge(p.sT.h, tokKVm)
                    v.wait_ge(p.sV.h, gsel_tok)
                    gsel = gind if e < 4 else gtop
                    first = (e % 4 == 0)
                    src4 = _bank(p, bk)
                    sv2 = sv2b
                    for hh in range(4):
                        h = hbase + hh
                        so = hh * 128
                        if not (first and hh == 0):
                            v.wait_ge(p.sV2.h, sv2)
                        if first and hh == 0:
                            _li21 = v.tensor_scalar(
                                fsh[:, shi, tc, :], src4[:, so:so + 128],
                                gsel[:, tc, h:h + 1], None, ALU.mult)
                        else:
                            _li21 = v.scalar_tensor_tensor(
                                fsh[:, shi, tc, :], src4[:, so:so + 128],
                                gsel[:, tc, h:h + 1], fsh[:, shi, tc, :],
                                ALU.mult, ALU.add)
                        _li21.then_inc(p.sV2.h, 1)
                        sv2 += 1
                    v.wait_ge(p.sV2.h, sv2)
                    v.engine_nop().then_inc(p.sV.h, 1)
                p.sV2.n += 4
                p.sV.n += 1
                comb_done[(tc % 2, le)] = p.sV.n
        tok_sel = p.sV.n

        with nc.sbuf_tensor(_u('fshT'), [128, NKC, HD], BF16) as fshT:
            for tc in range(NKC):
                b7 = p.bankS.get(7, 0)

                @blk.tensor
                def _(t, tc=tc, tok_sel=tok_sel, b7=b7):
                    t.wait_ge(p.sV.h, tok_sel)
                    t.wait_ge(p.sS.h, b7)
                    t.transpose(_bank_bf(p, 7), fsh[:, 0, tc, :], p.ident[:, :]).then_inc(p.sT.h, 1)
                p.sT.n += 1
                tokTr = p.sT.n

                @blk.scalar
                def _(s, tc=tc, tokTr=tokTr):
                    s.wait_ge(p.sT.h, tokTr)
                    s.copy(out=fshT[:, tc, :], in_=_bank_bf(p, 7)).then_inc(p.sS.h, 1)
                p.sS.n += 1
                p.bankS[7] = p.sS.n
            tokTall = p.sS.n

            @blk.sync
            def _(s, tokTall=tokTall, tok_sel=tok_sel):
                s.wait_ge(p.sS.h, tokTall)
                s.wait_ge(p.sV.h, tok_sel)
                s.dma_start(out=p.ag_kv_in[0, :, :, :], in_=fshT[:, :, :]).then_inc(p.cA.h, 16)
                s.dma_start(out=p.ag_kv_in[1, :, :, :], in_=fsh[:, 1, :, :]).then_inc(p.cA.h, 16)
            p.cA.n += 32
            tok_stores = p.cA.n

    @blk.gpsimd
    def _(g, tok_stores=tok_stores):
        g.wait_ge(p.cA.h, tok_stores)
        g.collective_compute('AllGather', ALU.bypass, replica_groups=p.RG,
                             ins=[p.ag_kv_in[:, :, :, :].opt()],
                             outs=[p.ag_kv_out[:, :, :, :, :].opt()]).then_inc(p.sCC.h, 1)
    p.sCC.n += 1
    kv_ready = p.sCC.n

    _proj_to_scratch(p, 'l0_qT', p.q_scr, H)

    def kload(s, dst, pi, j0, j1, h):
        ck = p.cKV if pi == 0 else p.cKV1
        s.dma_start(out=dst, in_=p.ag_kv_out[pi, 0, :, j0:j1, :]).then_inc(ck.h, 16)

    def vload(s, dst, pi, j0, j1, h):
        ck = p.cKV if pi == 0 else p.cKV1
        s.dma_start(out=dst, in_=p.ag_kv_out[pi, 1, :, j0:j1, :]).then_inc(ck.h, 16)

    _attention(p, _causal_chunks, 0, kload, vload, do_oproj=False,
               ow_key=None, kv_shared=True, kv_wait=[(p.sCC.h, kv_ready)])



def _l1_attn(p):
    nc, blk = p.nc, p.blk
    _phase_barrier(p)
    mask_tok = _load_masks(p, 8, p.x_tok)

    xt = p.x_tok
    with nc.sbuf_tensor(_u('lkw'), [128, 2, CC, 256], BF16) as lkw, \
         nc.sbuf_tensor(_u('lxt'), [128, 2, CC, SBW], BF16) as lxt:
        wuse = {}
        xuse = {}
        k_pre = p.sT.n
        for tsb in range(NSB):
            xs = tsb % 2
            xprev = xuse.get(tsb - 2, k_pre)
            tsl = slice(tsb * SBW, (tsb + 1) * SBW)

            @blk.vector
            def _(v, tsl=tsl, xs=xs, xprev=xprev, xt=xt):
                v.wait_ge(p.sV.h, xt)
                v.wait_ge(p.sT.h, xprev)
                v.tensor_copy(lxt[:, xs, :, :], p.xsb[:, :, tsl]).then_inc(p.sV2.h, 1)
            p.sV2.n += 1
            tokX = p.sV2.n

            for hp in range(8):
                q = tsb * 8 + hp
                ws = q % 2
                cw = p.cW0 if ws == 0 else p.cW1
                wprev = wuse.get(q - 2, k_pre)

                @blk.gpsimd
                def _(g, hp=hp, ws=ws, wprev=wprev, cw=cw):
                    g.wait_ge(p.sT.h, wprev)
                    g.dma_start(out=lkw[:, ws, :, :],
                                in_=p.w['l1_wk'][:, :, hp * 256:(hp + 1) * 256]).then_inc(cw.h, 16)
                cw.n += 16
                tokKW = cw.n

                for hh in range(2):
                    h = hp * 2 + hh
                    bk = 4 + hh
                    ev = hh
                    bs = p.bankS.get(bk, 0)

                    @blk.tensor
                    def _(t, hh=hh, ws=ws, xs=xs, bk=bk, tokX=tokX,
                          tokKW=tokKW, bs=bs, cw=cw):
                        t.wait_ge(cw.h, tokKW)
                        t.wait_ge(p.sV2.h, tokX)
                        t.wait_ge(p.sS.h, bs)
                        for cc in range(CC):
                            _li23 = t.matmul(_bank(p, bk), lkw[:, ws, cc, hh * 128:(hh + 1) * 128],
                                     lxt[:, xs, cc, :], start=(cc == 0), stop=(cc == CC - 1))
                        _li23.then_inc(p.sT.h, 1)
                    p.sT.n += 1
                    tokMM = p.sT.n
                    evfree = p.evS.get(ev, 0)
                    ce = p.cE0 if ev == 0 else p.cE1

                    @blk.scalar
                    def _(s, ev=ev, bk=bk, tokMM=tokMM, evfree=evfree, ce=ce):
                        s.wait_ge(p.sT.h, tokMM)
                        s.wait_ge(ce.h, evfree)
                        s.copy(out=p.evac[:, ev, :], in_=_bank(p, bk)).then_inc(p.sS.h, 1)
                    p.sS.n += 1
                    p.bankS[bk] = p.sS.n
                    tokEv = p.sS.n

                    @blk.sync
                    def _(s, h=h, tsl=tsl, ev=ev, tokEv=tokEv, ce=ce):
                        s.wait_ge(p.sS.h, tokEv)
                        s.dma_start(out=p.agki[h, :, tsl],
                                    in_=p.evac[:, ev, :]).then_inc(ce.h, 16)
                    ce.n += 16
                    p.evS[ev] = ce.n
                wuse[q] = p.sT.n
            xuse[tsb] = p.sT.n

    _phase_barrier(p)
    with nc.sbuf_tensor(_u('lvw'), [128, 2, CC, 512], BF16) as lvw, \
         nc.sbuf_tensor(_u('lxc'), [128, 2, CC, 128], BF16) as lxc:
        vwuse = {}
        vxuse = {}
        v_pre = p.sT.n
        for vh in range(2):
            for half in (0, 1):
                idx = 2 * vh + half
                cw = p.cW0 if half == 0 else p.cW1
                wprev = vwuse.get(idx - 2, v_pre)

                @blk.gpsimd
                def _(g, vh=vh, half=half, wprev=wprev, cw=cw):
                    g.wait_ge(p.sT.h, wprev)
                    c0 = vh * 1024 + half * 512
                    g.dma_start(out=lvw[:, half, :, :],
                                in_=p.w['l1_wv'][:, :, c0:c0 + 512]).then_inc(cw.h, 16)
                cw.n += 16
            tokVW0 = p.cW0.n
            tokVW1 = p.cW1.n

            for tc in range(NKC):
                tb = tc % 2
                it4 = vh * NKC + tc
                tsl = slice(tc * 128, (tc + 1) * 128)
                xprev = vxuse.get(it4 - 2, v_pre)

                @blk.vector
                def _(v, tsl=tsl, tb=tb, xprev=xprev, xt=xt):
                    v.wait_ge(p.sV.h, xt)
                    v.wait_ge(p.sT.h, xprev)
                    v.tensor_copy(lxc[:, tb, :, :], p.xsb[:, :, tsl]).then_inc(p.sV2.h, 1)
                p.sV2.n += 1
                tokXC = p.sV2.n
                b4, b5 = p.bankS.get(4, 0), p.bankS.get(5, 0)

                @blk.tensor
                def _(t, tb=tb, tokXC=tokXC, tokVW0=tokVW0, tokVW1=tokVW1,
                      b4=b4, b5=b5):
                    t.wait_ge(p.cW0.h, tokVW0)
                    t.wait_ge(p.cW1.h, tokVW1)
                    t.wait_ge(p.sV2.h, tokXC)
                    t.wait_ge(p.sS.h, max(b4, b5))
                    for cc in range(CC):
                        t.matmul(_bank(p, 4), lxc[:, tb, cc, :], lvw[:, 0, cc, :],
                                 start=(cc == 0), stop=(cc == CC - 1))
                    for cc in range(CC):
                        _li24 = t.matmul(_bank(p, 5), lxc[:, tb, cc, :], lvw[:, 1, cc, :],
                                 start=(cc == 0), stop=(cc == CC - 1))
                    _li24.then_inc(p.sT.h, 1)
                p.sT.n += 1
                tokVm = p.sT.n
                vxuse[it4] = tokVm
                if tc == NKC - 1:
                    vwuse[2 * vh] = tokVm
                    vwuse[2 * vh + 1] = tokVm

                @blk.scalar
                def _(s, tokVm=tokVm, pv=p.evS.get('l1v', 0)):
                    s.wait_ge(p.sT.h, tokVm)
                    s.wait_ge(p.cE2.h, pv)
                    s.copy(out=p.evac[:, 0, :], in_=_bank(p, 4))
                    s.copy(out=p.evac[:, 1, :], in_=_bank(p, 5)).then_inc(p.sS.h, 1)
                p.sS.n += 1
                p.bankS[4] = p.sS.n
                p.bankS[5] = p.sS.n
                tokEv = p.sS.n

                @blk.sync
                def _(s, tc=tc, vh=vh, tokEv=tokEv):
                    s.wait_ge(p.sS.h, tokEv)
                    s.dma_start(out=p.agvi[:, tc, vh * 1024:vh * 1024 + 512],
                                in_=p.evac[:, 0, :]).then_inc(p.cE2.h, 16)
                    s.dma_start(out=p.agvi[:, tc, vh * 1024 + 512:(vh + 1) * 1024],
                                in_=p.evac[:, 1, :]).then_inc(p.cE2.h, 16)
                p.cE2.n += 32
                p.evS['l1v'] = p.cE2.n

    @blk.gpsimd
    def _(g, e0=p.cE0.n, e1=p.cE1.n, e2=p.cE2.n):
        g.wait_ge(p.cE0.h, e0)
        g.wait_ge(p.cE1.h, e1)
        g.wait_ge(p.cE2.h, e2)
        g.collective_compute('AllGather', ALU.bypass, replica_groups=p.RG,
                             ins=[p.agki[:, :, :].opt()],
                             outs=[p.agko[:, :, :, :].opt()]).then_inc(p.sCC.h, 1)
        g.collective_compute('AllGather', ALU.bypass, replica_groups=p.RG,
                             ins=[p.agvi[:, :, :].opt()],
                             outs=[p.agvo[:, :, :, :].opt()]).then_inc(p.sCC.h, 1)
    p.sCC.n += 2
    kv_ag = p.sCC.n

    _proj_to_scratch(p, 'l1_wq', p.q_scr, H)
    kv_stores = [(p.sCC.h, kv_ag)]

    def kload(s, dst, pi, j0, j1, h):
        ck = p.cKV if pi == 0 else p.cKV1
        s.dma_start(out=dst,
                    in_=p.agko[pi, h, :, j0 * KCW:j1 * KCW].rearrange(
                        'p (c d) -> p c d', d=HD)).then_inc(ck.h, 16)

    def vload(s, dst, pi, j0, j1, h):
        ck = p.cKV if pi == 0 else p.cKV1
        s.dma_start(out=dst,
                    in_=p.agvo[pi, :, j0:j1, h * HD:(h + 1) * HD]).then_inc(ck.h, 16)

    _attention(p, _anti_chunks, 0, kload, vload, do_oproj=True,
               ow_key='l1_wo', l1_edge=True, kv_shared=False,
               kv_wait=kv_stores)



def _l1_ffn(p):
    nc, blk = p.nc, p.blk
    _phase_barrier(p)
    p.bankS[4] = 0
    p.bankS[5] = 0

    with nc.sbuf_tensor(_u('fgw'), [128, 2, CC, 256], BF16) as fgw, \
         nc.sbuf_tensor(_u('fuw'), [128, 2, CC, 256], BF16) as fuw, \
         nc.sbuf_tensor(_u('fxb'), [128, CC, SBW], BF16) as fxb, \
         nc.sbuf_tensor(_u('fsg'), [128, SBW], BF16) as fsg:

        xt = p.x_tok
        wslot = {}
        w_pre = p.sT.n
        for hq in range(NKC):
            for half in (0, 1):
                idx = 2 * hq + half
                cw = p.cW0 if half == 0 else p.cW1
                wprev = wslot.get(idx - 2, w_pre)

                @blk.gpsimd
                def _(g, hq=hq, half=half, wprev=wprev, cw=cw):
                    g.wait_ge(p.sT.h, wprev)
                    c0 = hq * 512 + half * 256
                    g.dma_start(out=fgw[:, half, :, :],
                                in_=p.w['l1_wgT'][:, :, c0:c0 + 256]).then_inc(cw.h, 16)
                    g.dma_start(out=fuw[:, half, :, :],
                                in_=p.w['l1_wuT'][:, :, c0:c0 + 256]).then_inc(cw.h, 16)
                cw.n += 32
            tokW0 = p.cW0.n
            tokW1 = p.cW1.n

            for tsb in range(NSB):
                tsl = slice(tsb * SBW, (tsb + 1) * SBW)
                tokT_prev2 = p.sT.n

                @blk.vector
                def _(v, tsl=tsl, tokT_prev2=tokT_prev2, xt=xt):
                    v.wait_ge(p.sV.h, xt)
                    v.wait_ge(p.sT.h, tokT_prev2)
                    v.tensor_copy(fxb[:, :, :], p.xsb[:, :, tsl]).then_inc(p.sV2.h, 1)
                p.sV2.n += 1
                tokXB = p.sV2.n
                for hh in range(4):
                    hc = hq * 4 + hh
                    hs = hh // 2
                    ho = (hh % 2) * 128
                    tw = tokW0 if hs == 0 else tokW1
                    cwx = p.cW0 if hs == 0 else p.cW1
                    b4 = p.bankS.get(4, 0)
                    b5 = p.bankS.get(5, 0)

                    @blk.tensor
                    def _(t, hs=hs, ho=ho, tw=tw, cwx=cwx, tokXB=tokXB, b4=b4, b5=b5):
                        t.wait_ge(cwx.h, tw)
                        t.wait_ge(p.sV2.h, tokXB)
                        t.wait_ge(p.sS.h, b4)
                        for cc in range(CC):
                            _li25 = t.matmul(_bank(p, 4), fgw[:, hs, cc, ho:ho + 128],
                                     fxb[:, cc, :], start=(cc == 0), stop=(cc == CC - 1))
                        _li25.then_inc(p.sT.h, 1)
                        t.wait_ge(p.sV.h, b5)
                        for cc in range(CC):
                            _li26 = t.matmul(_bank(p, 5), fuw[:, hs, cc, ho:ho + 128],
                                     fxb[:, cc, :], start=(cc == 0), stop=(cc == CC - 1))
                        _li26.then_inc(p.sT.h, 1)
                    p.sT.n += 2
                    if tsb == NSB - 1 and hh == 1:
                        wslot[2 * hq] = p.sT.n
                    if tsb == NSB - 1 and hh == 3:
                        wslot[2 * hq + 1] = p.sT.n
                    tokG = p.sT.n - 1
                    tokU = p.sT.n

                    @blk.scalar
                    def _(s, tokG=tokG, pgt=p.sV.n):
                        s.wait_ge(p.sT.h, tokG)
                        s.wait_ge(p.sV.h, pgt)
                        s.activation(out=fsg[:, :], in_=_bank(p, 4),
                                     func=AF.Silu).then_inc(p.sS.h, 1)
                    p.sS.n += 1
                    p.bankS[4] = p.sS.n
                    tokSG = p.sS.n
                    evfree = p.evS.get(0, 0)

                    @blk.vector
                    def _(v, tokSG=tokSG, tokU=tokU, evfree=evfree):
                        v.wait_ge(p.sS.h, tokSG)
                        v.wait_ge(p.sT.h, tokU)
                        v.wait_ge(p.cE0.h, evfree)
                        v.tensor_tensor(p.evac[:, 0, :], _bank(p, 5), fsg[:, :],
                                        ALU.mult).then_inc(p.sV.h, 1)
                    p.sV.n += 1
                    p.bankS[5] = p.sV.n
                    tokGT = p.sV.n

                    @blk.sync
                    def _(s, hc=hc, tsl=tsl, tokGT=tokGT):
                        s.wait_ge(p.sV.h, tokGT)
                        s.dma_start(out=p.gated_scr[:, hc, tsl],
                                    in_=p.evac[:, 0, :]).then_inc(p.cE0.h, 16)
                    p.cE0.n += 16
                    p.evS[0] = p.cE0.n
        g_stores = p.cE0.n

    _phase_barrier(p)
    with nc.sbuf_tensor(_u('dwd'), [128, 2, 8, 1024], BF16) as dwd, \
         nc.sbuf_tensor(_u('dgt'), [128, 2, 8, SBW], BF16) as dgt:

        dslot = {}
        it = 0
        for tsb in range(NSB):
            tsl = slice(tsb * SBW, (tsb + 1) * SBW)
            rtoks = []
            for d2h in range(2):
                for hb in range(8):
                    sl2 = it % 2
                    prev_use = dslot.get(it - 2, 0)

                    cwx = p.cW0 if sl2 == 0 else p.cW1
                    cdx = p.cD if sl2 == 0 else p.cD1

                    @blk.gpsimd
                    def _(g, d2h=d2h, hb=hb, sl2=sl2, prev_use=prev_use, cwx=cwx):
                        g.wait_ge(p.sT.h, prev_use)
                        g.dma_start(out=dwd[:, sl2, :, :],
                                    in_=p.w['l1_wdT'][:, hb * 8:(hb + 1) * 8,
                                                      d2h * 1024:(d2h + 1) * 1024]).then_inc(cwx.h, 16)
                    cwx.n += 16
                    tokDW = cwx.n

                    @blk.sync
                    def _(s, hb=hb, tsl=tsl, sl2=sl2, prev_use=prev_use, g_stores=g_stores, cdx=cdx):
                        s.wait_ge(p.cE0.h, g_stores)
                        s.wait_ge(p.sT.h, prev_use)
                        s.dma_start(out=dgt[:, sl2, :, :],
                                    in_=p.gated_scr[:, hb * 8:(hb + 1) * 8, tsl]).then_inc(cdx.h, 16)
                    cdx.n += 16
                    tokGL = cdx.n
                    pof = p.po_free if hb == 0 else 0

                    @blk.tensor
                    def _(t, d2h=d2h, hb=hb, sl2=sl2, tokDW=tokDW, tokGL=tokGL, pof=pof,
                          cwx=cwx, cdx=cdx):
                        t.wait_ge(cwx.h, tokDW)
                        t.wait_ge(cdx.h, tokGL)
                        if pof:
                            t.wait_ge(p.sV.h, pof)
                        for hc in range(8):
                            for d2c in range(8):
                                _li27 = t.matmul(p.ps[:, d2c * 512:d2c * 512 + SBW],
                                         dwd[:, sl2, hc, d2c * 128:(d2c + 1) * 128],
                                         dgt[:, sl2, hc, :],
                                         start=(hb == 0 and hc == 0),
                                         stop=(hb == 7 and hc == 7))
                        _li27.then_inc(p.sT.h, 1)
                    p.sT.n += 1
                    dslot[it] = p.sT.n
                    it += 1
                tok_half = p.sT.n

                @blk.vector
                def _(v, d2h=d2h, tsl=tsl, tok_half=tok_half):
                    v.wait_ge(p.sT.h, tok_half)
                    for d2c in range(8):
                        cc = d2h * 8 + d2c
                        _li28 = v.tensor_tensor(p.xsb[:, cc, tsl], p.xsb[:, cc, tsl],
                                        p.ps[:, d2c * 512:d2c * 512 + SBW], ALU.add)
                    _li28.then_inc(p.sV.h, 1)
                p.sV.n += 1
                p.po_free = p.sV.n
                rtoks.append(p.sV.n)
            _ln_only(p, tsl, rtoks[-1])
            p.po_free = p.x_tok


def kernel(**inputs):
    in_maps = _stage_inputs(inputs)
    nc = build_program(_PHASES)
    res = run_bass_kernel_spmd(nc, in_maps, core_ids=list(range(8)))
    return _assemble(res.results)
